# version 04
# baseline (speedup 1.0000x reference)
import numpy as np
import ml_dtypes
from contextlib import ExitStack

import concourse.bass as bass
import concourse.mybir as mybir
from concourse.bass import IndirectOffsetOnAxis
from concourse.bass_utils import run_bass_kernel_spmd

F32 = mybir.dt.float32
BF16 = mybir.dt.bfloat16
U32 = mybir.dt.uint32
I16 = mybir.dt.int16
U8 = mybir.dt.uint8
AF = mybir.ActivationFunctionType
ALU = mybir.AluOpType
AX = mybir.AxisListType

D_MODEL = 1024
NCH = D_MODEL // 128
T_CORE = 2048
N_CORES = 8
LN_EPS = 1e-5

BF = ml_dtypes.bfloat16

ENGINES = ("sync", "scalar", "vector", "tensor", "gpsimd")



def fmT(a):
    return np.ascontiguousarray(np.asarray(a, dtype=np.float32).T)


def pack_vec(v):
    return np.ascontiguousarray(np.asarray(v, dtype=np.float32).reshape(-1, 128).T)


def core_slice_tokens(x, core):
    b, h = core // 2, core % 2
    return np.asarray(x[b, h * 2048:(h + 1) * 2048])



class Seq:

    def __init__(self, nc):
        self.nc = nc
        self.steps = {e: [] for e in ENGINES}
        self.counts = {}
        self.waited = {e: {} for e in ENGINES}

    def _tok_list(self, wait):
        if wait is None:
            return [(s, c) for s, c in self.counts.items()]
        if isinstance(wait, tuple):
            return [wait]
        out = {}
        for s, c in wait:
            if c > out.get(s, 0):
                out[s] = c
        return list(out.items())

    def step(self, engine, fn, inc=1, wait=None, dma=False, sclass=None):
        stream = (engine, sclass if sclass else ('q' if dma else 'c'))
        toks = self._tok_list(wait)
        waits = []
        wt = self.waited[engine]
        for s, c in toks:
            assert c <= self.counts.get(s, 0), f"future wait {s}:{c}"
            if wt.get(s, 0) < c:
                waits.append((s, c))
                wt[s] = c
        self.counts[stream] = self.counts.get(stream, 0) + inc
        self.steps[engine].append((waits, fn, inc, stream))
        return (stream, self.counts[stream])

    def dma(self, out, in_, engine="sync", wait=None, **kw):
        return self.step(engine,
                         lambda e: e.dma_start(out=out, in_=in_, **kw),
                         inc=16, wait=wait, dma=True)

    def barrier(self):
        return [(s, c) for s, c in self.counts.items()]

    def build_into(self, block, sems):
        for ename in ENGINES:
            steps = self.steps[ename]
            if not steps:
                continue

            def make(steps):
                def prog(eng):
                    for waits, fn, inc, stream in steps:
                        for s, c in waits:
                            eng.wait_ge(sems[s], c)
                        ins = fn(eng)
                        if isinstance(ins, list):
                            ins = ins[-1]
                        ins.then_inc(sems[stream], inc)
                return prog
            getattr(block, ename)(make(steps))

    def finish(self, block, sems):
        toks = [(s, c) for s, c in self.counts.items()]
        wt = self.waited["sync"]

        def prog(eng):
            for s, c in toks:
                if wt.get(s, 0) < c or s == ("sync", "q"):
                    eng.wait_ge(sems[s], c)
        block.sync(prog)


def tmax(*toks):
    out = {}
    for t in toks:
        if t is None:
            continue
        items = [t] if isinstance(t, tuple) and len(t) == 2 and \
            isinstance(t[1], int) else t
        for s, c in items:
            if c > out.get(s, 0):
                out[s] = c
    return list(out.items())


class FusedIO(dict):

    def __init__(self, prefix, mapping, nprefix=None):
        super().__init__(mapping)
        self.prefix = prefix
        self.nprefix = nprefix or prefix
        self.auto = []


def _dram(nc, io, name, shape, dtype, kind):
    if io is None:
        return nc.dram_tensor(name, shape, dtype, kind=kind).ap()
    if name in io:
        return io[name]
    assert kind == "ExternalInput", f"unmapped non-input tensor {name} ({kind})"
    io.auto.append(name)
    ap = nc.dram_tensor(io.prefix + name, shape, dtype, kind=kind).ap()
    io[name] = ap
    return ap


def seq_program(build_fn):
    nc = bass.Bass("TRN2", target_bir_lowering=False, debug=False)
    sq = Seq(nc)
    with ExitStack() as es:
        build_fn(nc, sq, es)
        sems = {}
        for s in sq.counts:
            sems[s] = es.enter_context(nc.semaphore(f"s_{s[0]}_{s[1]}"))
        block = es.enter_context(nc.Block())
        sq.build_into(block, sems)
        sq.finish(block, sems)
    return nc



def emit_ln_fm(nc, sq, X_view, TT, xn, xn_off, scr, rows, psA, psB, ones_col,
               ones_row, load_chunk):
    raise NotImplementedError



def build_l1_ffn(io=None, emit_only=False):
    TE = T_CORE + 1
    NT, T = 4, 512

    def prog(nc, sq, es):
        x_ext = _dram(nc, io, "x_ext", [D_MODEL, TE], F32, "ExternalInput")
        wrT = _dram(nc, io, "wrT", [D_MODEL, D_MODEL], F32, "ExternalInput")
        wkT = _dram(nc, io, "wkT", [D_MODEL, D_MODEL], F32, "ExternalInput")
        wvT = _dram(nc, io, "wvT", [D_MODEL, D_MODEL], F32, "ExternalInput")
        mu_r = _dram(nc, io, "mu_r", [128, NCH], F32, "ExternalInput")
        mu_k = _dram(nc, io, "mu_k", [128, NCH], F32, "ExternalInput")
        x_out = _dram(nc, io, "x_out", [D_MODEL, T_CORE], F32, "ExternalOutput")

        xe = x_ext.rearrange("(c p) t -> p c t", p=128)
        xo = x_out.rearrange("(c p) t -> p c t", p=128)

        sb = lambda n, s, d=F32: es.enter_context(nc.sbuf_tensor(n, s, d))
        pt = lambda n, s, d=F32: es.enter_context(nc.psum_tensor(n, s, d))

        Xt = sb("Xt", [128, NCH * 512])
        xn = sb("xn", [128, NCH * TE], BF16)
        sq32 = sb("sq32", [128, 512])
        a_row = sb("a_row", [1, TE])
        b_row = sb("b_row", [1, TE])
        t_row = sb("t_row", [1, 512])
        wk_s = sb("wk_s", [128, NCH * D_MODEL], BF16)
        wr_s = sb("wr_s", [128, NCH * D_MODEL], BF16)
        wv_s = sb("wv_s", [128, NCH * D_MODEL], BF16)
        mur_s = sb("mur_s", [128, NCH])
        muk_s = sb("muk_s", [128, NCH])
        dscr = sb("dscr", [128, 512])
        oscr = sb("oscr", [128, 512])
        xk_s = sb("xk_s", [128, NCH * T], BF16)
        xr_s = sb("xr_s", [128, NCH * T], BF16)
        kact = sb("kact", [128, NCH * T], BF16)
        r_s = sb("r_s", [128, NCH * T])
        ones_col = sb("ones_col", [128, 1])
        ones_row = sb("ones_row", [1, 128])
        eps_row = sb("eps_row", [1, 1])

        ps_a = pt("ps_a", [1, 512])
        ps_b = pt("ps_b", [1, 512])
        ps_bc1 = pt("ps_bc1", [128, 512])
        ps_bc2 = pt("ps_bc2", [128, 512])
        ps_mm = pt("ps_mm", [128, 512])

        sq.step("gpsimd", lambda e: e.memset(ones_col[:], 1.0))
        sq.step("gpsimd", lambda e: e.memset(ones_row[:], 1.0))
        sq.step("gpsimd", lambda e: e.memset(eps_row[:], LN_EPS))
        for nm, dst in (("wkT", wk_s), ("wrT", wr_s), ("wvT", wv_s)):
            src = {"wkT": wkT, "wrT": wrT, "wvT": wvT}[nm]
            sq.dma(dst[:].rearrange("p (c d) -> p c d", c=NCH),
                   src.rearrange("(c p) d -> p c d", p=128), engine="gpsimd")
        sq.dma(mur_s[:], mu_r)
        sq.dma(muk_s[:], mu_k)

        c0 = 0
        while c0 < TE:
            w = min(512, TE - c0)
            sq.dma(Xt[:].rearrange("p (c t) -> p c t", c=NCH)[:, :, 0:w],
                   xe[:, :, c0:c0 + w],
                   **({"allow_slow_non_contiguous": True} if w < 8 else {}))
            sq.step("tensor", lambda e, w=w: [
                e.matmul(ps_a[:, 0:w], ones_col[:],
                         Xt[:, c * 512:c * 512 + w], start=(c == 0), stop=(c == NCH - 1))
                for c in range(NCH)])
            for c in range(NCH):
                sq.step("scalar", lambda e, c=c, w=w: e.activation(
                    sq32[:, 0:w], Xt[:, c * 512:c * 512 + w], AF.Square))
                sq.step("tensor", lambda e, c=c, w=w: e.matmul(
                    ps_b[:, 0:w], ones_col[:], sq32[:, 0:w],
                    start=(c == 0), stop=(c == NCH - 1)))
            sq.step("vector", lambda e, c0=c0, w=w: e.tensor_single_scalar(
                b_row[:, c0:c0 + w], ps_a[:, 0:w], 1.0 / D_MODEL, ALU.mult))
            sq.step("vector", lambda e, c0=c0, w=w: e.tensor_single_scalar(
                a_row[:, c0:c0 + w], ps_b[:, 0:w], 1.0 / D_MODEL, ALU.mult))
            sq.step("scalar", lambda e, c0=c0, w=w: e.activation(
                t_row[:, 0:w], b_row[:, c0:c0 + w], AF.Square))
            sq.step("vector", lambda e, c0=c0, w=w: e.tensor_sub(
                a_row[:, c0:c0 + w], a_row[:, c0:c0 + w], t_row[:, 0:w]))
            sq.step("scalar", lambda e, c0=c0, w=w: e.activation(
                a_row[:, c0:c0 + w], a_row[:, c0:c0 + w], AF.Ln, bias=eps_row[:]))
            sq.step("scalar", lambda e, c0=c0, w=w: e.activation(
                a_row[:, c0:c0 + w], a_row[:, c0:c0 + w], AF.Exp, scale=-0.5))
            sq.step("tensor", lambda e, c0=c0, w=w: e.matmul(
                ps_bc1[:, 0:w], ones_row[:], b_row[:, c0:c0 + w]))
            sq.step("tensor", lambda e, c0=c0, w=w: e.matmul(
                ps_bc2[:, 0:w], ones_row[:], a_row[:, c0:c0 + w]))
            for c in range(NCH):
                sq.step("vector", lambda e, c=c, w=w: e.tensor_sub(
                    dscr[:, 0:w], Xt[:, c * 512:c * 512 + w], ps_bc1[:, 0:w]))
                sq.step("vector", lambda e, c=c, c0=c0, w=w: e.tensor_mul(
                    xn[:, c * TE + c0:c * TE + c0 + w], dscr[:, 0:w], ps_bc2[:, 0:w]))
            c0 += w

        for i in range(NT):
            t0 = 1 + i * T
            for c in range(NCH):
                sq.step("vector", lambda e, c=c, t0=t0: e.tensor_sub(
                    dscr[:, 0:T], xn[:, c * TE + t0:c * TE + t0 + T],
                    xn[:, c * TE + t0 - 1:c * TE + t0 - 1 + T]))
                sq.step("vector", lambda e, c=c, t0=t0: e.scalar_tensor_tensor(
                    xk_s[:, c * T:(c + 1) * T], dscr[:, 0:T], muk_s[:, c:c + 1],
                    xn[:, c * TE + t0 - 1:c * TE + t0 - 1 + T], ALU.mult, ALU.add))
                sq.step("vector", lambda e, c=c, t0=t0: e.scalar_tensor_tensor(
                    xr_s[:, c * T:(c + 1) * T], dscr[:, 0:T], mur_s[:, c:c + 1],
                    xn[:, c * TE + t0 - 1:c * TE + t0 - 1 + T], ALU.mult, ALU.add))
            for d in range(NCH):
                sq.step("tensor", lambda e, d=d: [
                    e.matmul(ps_mm[:], wk_s[:, c * D_MODEL + d * 128:
                                            c * D_MODEL + (d + 1) * 128],
                             xk_s[:, c * T:(c + 1) * T],
                             start=(c == 0), stop=(c == NCH - 1))
                    for c in range(NCH)])
                sq.step("vector", lambda e, d=d: e.tensor_single_scalar(
                    dscr[:, 0:T], ps_mm[:], 0.0, ALU.max))
                sq.step("scalar", lambda e, d=d: e.activation(
                    kact[:, d * T:(d + 1) * T], dscr[:, 0:T], AF.Square))
            for d in range(NCH):
                sq.step("tensor", lambda e, d=d: [
                    e.matmul(ps_mm[:], wr_s[:, c * D_MODEL + d * 128:
                                            c * D_MODEL + (d + 1) * 128],
                             xr_s[:, c * T:(c + 1) * T],
                             start=(c == 0), stop=(c == NCH - 1))
                    for c in range(NCH)])
                sq.step("scalar", lambda e, d=d: e.activation(
                    r_s[:, d * T:(d + 1) * T], ps_mm[:], AF.Sigmoid))
            for d in range(NCH):
                sq.step("tensor", lambda e, d=d: [
                    e.matmul(ps_mm[:], wv_s[:, c * D_MODEL + d * 128:
                                            c * D_MODEL + (d + 1) * 128],
                             kact[:, c * T:(c + 1) * T],
                             start=(c == 0), stop=(c == NCH - 1))
                    for c in range(NCH)])
                sq.step("vector", lambda e, d=d: e.tensor_mul(
                    oscr[:, 0:T], r_s[:, d * T:(d + 1) * T], ps_mm[:]))
                sq.dma(Xt[:, 0:T], xe[:, d, t0:t0 + T])
                sq.step("vector", lambda e, d=d: e.tensor_add(
                    oscr[:, 0:T], oscr[:, 0:T], Xt[:, 0:T]))
                sq.dma(xo[:, d, i * T:(i + 1) * T], oscr[:, 0:T])

    if emit_only:
        return prog
    return seq_program(prog)


def host_l1_ffn_inputs(x, weights, core):
    b, h = core // 2, core % 2
    sl = core_slice_tokens(x, core)
    if h == 0:
        halo = np.zeros((1, D_MODEL), np.float32)
    else:
        halo = np.asarray(x[b, h * 2048 - 1:h * 2048])
    x_ext = fmT(np.concatenate([halo, sl], axis=0))
    return {
        "x_ext": x_ext,
        "wrT": fmT(weights["W_r"]),
        "wkT": fmT(weights["W_k"]),
        "wvT": fmT(weights["W_v"]),
        "mu_r": pack_vec(weights["mu_r"]),
        "mu_k": pack_vec(weights["mu_k"]),
    }


def unpack_fm_out(res_list, key="x_out"):
    out = np.empty((4, 4096, 1024), np.float32)
    for core in range(N_CORES):
        b, h = core // 2, core % 2
        out[b, h * 2048:(h + 1) * 2048] = res_list[core][key].T
    return out



def build_l0_ffn(io=None, emit_only=False):
    NEXP = 8
    HID = 4096
    NHT = HID // 128
    NHG = 4
    NT, T = 4, 512

    def prog(nc, sq, es):
        x_fm = _dram(nc, io, "x_fm", [D_MODEL, T_CORE], F32, "ExternalInput")
        router_wT = _dram(nc, io, "router_wT", [D_MODEL, 8], F32, "ExternalInput")
        w_gate = _dram(nc, io, "w_gate", [NEXP, D_MODEL, HID], F32, "ExternalInput")
        w_up = _dram(nc, io, "w_up", [NEXP, D_MODEL, HID], F32, "ExternalInput")
        w_down = _dram(nc, io, "w_down", [NEXP, HID, D_MODEL], F32, "ExternalInput")
        x_out = _dram(nc, io, "x_out", [D_MODEL, T_CORE], F32, "ExternalOutput")
        cmb_dram = _dram(nc, io, "cmb_dram", [8, T_CORE], F32, "Internal")

        xv = x_fm.rearrange("(c p) t -> p c t", p=128)
        xo = x_out.rearrange("(c p) t -> p c t", p=128)
        wg_v = w_gate.rearrange("e (c p) h -> e p c h", p=128)
        wu_v = w_up.rearrange("e (c p) h -> e p c h", p=128)
        wd_v = w_down.rearrange("e (ht p) d -> e p ht d", p=128)
        rw_v = router_wT.rearrange("(c p) e -> p c e", p=128)

        sb = lambda n, s, d=F32: es.enter_context(nc.sbuf_tensor(n, s, d))
        pt = lambda n, s, d=F32: es.enter_context(nc.psum_tensor(n, s, d))

        Xt = sb("Xt", [128, NCH * 512])
        xn = sb("xn", [128, NCH * T_CORE], BF16)
        sq32 = sb("sq32", [128, 512])
        a_row = sb("a_row", [1, 512])
        b_row = sb("b_row", [1, 512])
        t_row = sb("t_row", [1, 512])
        ones_col = sb("ones_col", [128, 1])
        ones_row = sb("ones_row", [1, 128])
        eps_row = sb("eps_row", [1, 1])
        rw_s = sb("rw_s", [128, NCH * 8], BF16)
        logit = sb("logit", [128, 8])
        m1 = sb("m1", [128, 1])
        m2 = sb("m2", [128, 1])
        nm1 = sb("nm1", [128, 1])
        e2 = sb("e2", [128, 1])
        rden = sb("rden", [128, 1])
        gemsk = sb("gemsk", [128, 8])
        expv = sb("expv", [128, 8])
        cmb = sb("cmb", [128, 8])
        neg_inf8 = sb("neg_inf8", [128, 8])
        msk8 = sb("msk8", [128, 8], U8)
        cmb_row1 = sb("cmb_row1", [1, 128])
        cmb_row = sb("cmb_row", [1, T_CORE])
        cmb_bc = sb("cmb_bc", [128, T_CORE])
        identw = sb("identw", [128, 128], F32)
        wgs = sb("wgs", [128, NCH * 1024], BF16)
        wus = sb("wus", [128, NCH * 1024], BF16)
        wds = sb("wds", [128, 8 * 1024], BF16)
        hact = sb("hact", [128, 8 * 512], BF16)
        silu_scr = sb("silu_scr", [128, 512])
        h_scr = sb("h_scr", [128, 512])
        mixed = sb("mixed", [128, NCH * T_CORE])
        oscr = sb("oscr", [128, 512])

        ps_a = pt("ps_a", [1, 512])
        ps_b = pt("ps_b", [1, 512])
        ps_bc1 = pt("ps_bc1", [128, 512])
        ps_bc2 = pt("ps_bc2", [128, 512])
        ps_g = pt("ps_g", [128, 512])
        ps_u = pt("ps_u", [128, 512])
        ps_dn = pt("ps_dn", [128, 512])
        ps_r = pt("ps_r", [128, 128])

        sq.step("gpsimd", lambda e: e.memset(ones_col[:], 1.0))
        sq.step("gpsimd", lambda e: e.memset(ones_row[:], 1.0))
        sq.step("gpsimd", lambda e: e.memset(eps_row[:], LN_EPS))
        sq.step("gpsimd", lambda e: e.memset(neg_inf8[:], -1e30))
        sq.step("gpsimd", lambda e: e.memset(identw[:], 1.0))
        sq.step("gpsimd", lambda e: e.affine_select(
            identw[:], identw[:], pattern=[[1, 128]], compare_op=ALU.is_equal,
            fill=0.0, base=0, channel_multiplier=-1))
        identb = sb("identb", [128, 128], BF16)
        sq.step("vector", lambda e: e.tensor_copy(identb[:], identw[:]))
        sq.dma(rw_s[:].rearrange("p (c e) -> p c e", c=NCH), rw_v, engine="gpsimd")

        for it in range(4):
            c0, w = it * 512, 512
            sq.dma(Xt[:].rearrange("p (c t) -> p c t", c=NCH), xv[:, :, c0:c0 + w])
            sq.step("tensor", lambda e, w=w: [
                e.matmul(ps_a[:, 0:w], ones_col[:], Xt[:, c * 512:c * 512 + w],
                         start=(c == 0), stop=(c == NCH - 1)) for c in range(NCH)])
            for c in range(NCH):
                sq.step("scalar", lambda e, c=c, w=w: e.activation(
                    sq32[:, 0:w], Xt[:, c * 512:c * 512 + w], AF.Square))
                sq.step("tensor", lambda e, c=c, w=w: e.matmul(
                    ps_b[:, 0:w], ones_col[:], sq32[:, 0:w],
                    start=(c == 0), stop=(c == NCH - 1)))
            sq.step("vector", lambda e, c0=c0, w=w: e.tensor_single_scalar(
                b_row[:, 0:w], ps_a[:, 0:w], 1.0 / D_MODEL, ALU.mult))
            sq.step("vector", lambda e, c0=c0, w=w: e.tensor_single_scalar(
                a_row[:, 0:w], ps_b[:, 0:w], 1.0 / D_MODEL, ALU.mult))
            sq.step("scalar", lambda e, c0=c0, w=w: e.activation(
                t_row[:, 0:w], b_row[:, 0:w], AF.Square))
            sq.step("vector", lambda e, c0=c0, w=w: e.tensor_sub(
                a_row[:, 0:w], a_row[:, 0:w], t_row[:, 0:w]))
            sq.step("scalar", lambda e, c0=c0, w=w: e.activation(
                a_row[:, 0:w], a_row[:, 0:w], AF.Ln, bias=eps_row[:]))
            sq.step("scalar", lambda e, c0=c0, w=w: e.activation(
                a_row[:, 0:w], a_row[:, 0:w], AF.Exp, scale=-0.5))
            sq.step("tensor", lambda e, c0=c0, w=w: e.matmul(
                ps_bc1[:, 0:w], ones_row[:], b_row[:, 0:w]))
            sq.step("tensor", lambda e, c0=c0, w=w: e.matmul(
                ps_bc2[:, 0:w], ones_row[:], a_row[:, 0:w]))
            for c in range(NCH):
                sq.step("vector", lambda e, c=c, w=w: e.tensor_sub(
                    sq32[:, 0:w], Xt[:, c * 512:c * 512 + w], ps_bc1[:, 0:w]))
                sq.step("vector", lambda e, c=c, c0=c0, w=w: e.tensor_mul(
                    xn[:, c * T_CORE + c0:c * T_CORE + c0 + w], sq32[:, 0:w],
                    ps_bc2[:, 0:w]))

        for q in range(T_CORE // 128):
            sq.step("tensor", lambda e, q=q: [
                e.matmul(ps_r[:, 0:8],
                         xn[:, c * T_CORE + q * 128:c * T_CORE + (q + 1) * 128],
                         rw_s[:, c * 8:(c + 1) * 8],
                         start=(c == 0), stop=(c == NCH - 1)) for c in range(NCH)])
            sq.step("vector", lambda e: e.tensor_copy(logit[:], ps_r[:, 0:8]))
            sq.step("vector", lambda e: e.reduce_max(m1[:], logit[:, 0:7], axis=AX.X))
            sq.step("vector", lambda e: e.tensor_scalar(
                msk8[:, 0:7], logit[:, 0:7], m1[:], None, ALU.is_ge))
            sq.step("vector", lambda e: e.select(
                sq32[:, 0:7], msk8[:, 0:7], neg_inf8[:, 0:7], logit[:, 0:7]))
            sq.step("vector", lambda e: e.reduce_max(m2[:], sq32[:, 0:7], axis=AX.X))
            sq.step("vector", lambda e: e.tensor_single_scalar(
                nm1[:], m1[:], -1.0, ALU.mult))
            sq.step("scalar", lambda e: e.activation(e2[:], m2[:], AF.Exp, bias=nm1[:]))
            sq.step("vector", lambda e: e.tensor_single_scalar(
                e2[:], e2[:], 1.0, ALU.add))
            sq.step("vector", lambda e: e.reciprocal(rden[:], e2[:]))
            sq.step("vector", lambda e: e.tensor_scalar(
                gemsk[:, 0:7], logit[:, 0:7], m2[:], None, ALU.is_ge))
            sq.step("scalar", lambda e: e.activation(
                expv[:, 0:7], logit[:, 0:7], AF.Exp, bias=nm1[:]))
            sq.step("vector", lambda e: e.tensor_mul(
                cmb[:, 0:7], expv[:, 0:7], gemsk[:, 0:7]))
            sq.step("vector", lambda e: e.tensor_scalar(
                cmb[:, 0:7], cmb[:, 0:7], rden[:], None, ALU.mult))
            sq.step("gpsimd", lambda e: e.memset(cmb[:, 7:8], 1.0))
            for ee in range(NEXP):
                sq.step("tensor", lambda e, ee=ee: e.transpose(
                    ps_r[0:1, 0:128], cmb[:, ee:ee + 1], identw[:]))
                sq.step("vector", lambda e: e.tensor_copy(
                    cmb_row1[:], ps_r[0:1, 0:128]))
                sq.dma(cmb_dram[ee:ee + 1, q * 128:(q + 1) * 128], cmb_row1[:])

        sq.step("gpsimd", lambda e: e.memset(mixed[:], 0.0))
        for ee in range(NEXP):
            sq.dma(cmb_row[:], cmb_dram[ee:ee + 1, :])
            for tt in range(NT):
                sq.step("tensor", lambda e, ee=ee, tt=tt: e.matmul(
                    ps_bc1[:], ones_row[:],
                    cmb_row[:, tt * T:(tt + 1) * T]))
                sq.step("vector", lambda e, tt=tt: e.tensor_copy(
                    cmb_bc[:, tt * T:(tt + 1) * T], ps_bc1[:]))
            for hg in range(NHG):
                sq.dma(wgs[:].rearrange("p (c h) -> p c h", c=NCH),
                       wg_v[ee, :, :, hg * 1024:(hg + 1) * 1024], engine="gpsimd")
                sq.dma(wus[:].rearrange("p (c h) -> p c h", c=NCH),
                       wu_v[ee, :, :, hg * 1024:(hg + 1) * 1024], engine="gpsimd")
                sq.dma(wds[:].rearrange("p (ht d) -> p ht d", ht=8),
                       wd_v[ee, :, hg * 8:(hg + 1) * 8, :], engine="gpsimd")
                for tt in range(NT):
                    for ht in range(8):
                        sq.step("tensor", lambda e, ht=ht, tt=tt: [
                            e.matmul(ps_g[:],
                                     wgs[:, c * 1024 + ht * 128:c * 1024 + (ht + 1) * 128],
                                     xn[:, c * T_CORE + tt * T:c * T_CORE + tt * T + T],
                                     start=(c == 0), stop=(c == NCH - 1))
                            for c in range(NCH)])
                        sq.step("tensor", lambda e, ht=ht, tt=tt: [
                            e.matmul(ps_u[:],
                                     wus[:, c * 1024 + ht * 128:c * 1024 + (ht + 1) * 128],
                                     xn[:, c * T_CORE + tt * T:c * T_CORE + tt * T + T],
                                     start=(c == 0), stop=(c == NCH - 1))
                            for c in range(NCH)])
                        sq.step("scalar", lambda e: e.activation(
                            silu_scr[:], ps_g[:], AF.Silu))
                        sq.step("vector", lambda e: e.tensor_mul(
                            h_scr[:], silu_scr[:], ps_u[:]))
                        sq.step("vector", lambda e, ht=ht, tt=tt: e.tensor_mul(
                            hact[:, ht * 512:(ht + 1) * 512], h_scr[:],
                            cmb_bc[:, tt * T:(tt + 1) * T]))
                    for dm in range(NCH):
                        sq.step("tensor", lambda e, dm=dm: [
                            e.matmul(ps_dn[:],
                                     wds[:, ht * 1024 + dm * 128:ht * 1024 + (dm + 1) * 128],
                                     hact[:, ht * 512:(ht + 1) * 512],
                                     start=(ht == 0), stop=(ht == 7))
                            for ht in range(8)])
                        sq.step("vector", lambda e, dm=dm, tt=tt: e.tensor_add(
                            mixed[:, dm * T_CORE + tt * T:dm * T_CORE + tt * T + T],
                            mixed[:, dm * T_CORE + tt * T:dm * T_CORE + tt * T + T],
                            ps_dn[:]))

        for tt in range(NT):
            sq.dma(Xt[:].rearrange("p (c t) -> p c t", c=NCH), xv[:, :, tt * T:(tt + 1) * T])
            for c in range(NCH):
                sq.step("vector", lambda e, c=c, tt=tt: e.tensor_add(
                    oscr[:], mixed[:, c * T_CORE + tt * T:c * T_CORE + tt * T + T],
                    Xt[:, c * 512:(c + 1) * 512]))
                sq.dma(xo[:, c, tt * T:(tt + 1) * T], oscr[:])

    if emit_only:
        return prog
    return seq_program(prog)


def host_l0_ffn_inputs(x, weights, core):
    sl = core_slice_tokens(x, core)
    rw = np.full((8, D_MODEL), -1e30, np.float32)
    rw[0:7] = np.asarray(weights["router_w"], np.float32)
    wg = np.concatenate([np.asarray(weights["w_gate"], np.float32),
                         np.asarray(weights["sw_gate"], np.float32)], axis=0)
    wu = np.concatenate([np.asarray(weights["w_up"], np.float32),
                         np.asarray(weights["sw_up"], np.float32)], axis=0)
    wd = np.concatenate([np.asarray(weights["w_down"], np.float32),
                         np.asarray(weights["sw_down"], np.float32)], axis=0)
    return {
        "x_fm": fmT(sl),
        "router_wT": fmT(rw),
        "w_gate": np.ascontiguousarray(wg),
        "w_up": np.ascontiguousarray(wu),
        "w_down": np.ascontiguousarray(wd),
    }



D_INNER = 2048
NEIN = D_INNER // 128
D_STATE = 16


def build_mamba_pre(io=None, emit_only=False):
    TE = T_CORE + 3
    COLT = [(0, 512), (512, 512), (1024, 512), (1536, 512), (2048, 3)]
    TT4 = [(0, 512), (512, 512), (1024, 512), (1536, 512)]

    def prog(nc, sq, es):
        x_ext = _dram(nc, io, "x_ext", [D_MODEL, TE], F32, "ExternalInput")
        in_projT = _dram(nc, io, "in_projT", [D_MODEL, 2 * D_INNER], F32, "ExternalInput")
        x_projT = _dram(nc, io, "x_projT", [D_INNER, 2080], F32, "ExternalInput")
        dt_projT = _dram(nc, io, "dt_projT", [D_INNER, D_INNER], F32, "ExternalInput")
        conv_w = _dram(nc, io, "conv_w", [128, 64], F32, "ExternalInput")
        conv_b = _dram(nc, io, "conv_b", [128, NEIN], F32, "ExternalInput")
        dt_proj_b = _dram(nc, io, "dt_proj_b", [128, NEIN], F32, "ExternalInput")
        mex = _dram(nc, io, "mex", [2 * D_INNER + 32, T_CORE], BF16,
                    "ExternalOutput")
        zs_o = _dram(nc, io, "zs_o", [D_INNER, T_CORE], F32, "ExternalOutput")
        dt_o = mex[0:D_INNER, :]
        xin_o = mex[D_INNER:2 * D_INNER, :]
        B_o = mex[2 * D_INNER:2 * D_INNER + 16, :]
        C_o = mex[2 * D_INNER + 16:2 * D_INNER + 32, :]

        xe = x_ext.rearrange("(c p) t -> p c t", p=128)
        ipv = in_projT.rearrange("(c p) e -> p c e", p=128)
        xpv = x_projT.rearrange("(c p) e -> p c e", p=128)
        dpv = dt_projT.rearrange("(c p) e -> p c e", p=128)
        dt_v = dt_o.rearrange("(c p) t -> p c t", p=128)
        xin_v = xin_o.rearrange("(c p) t -> p c t", p=128)
        zs_v = zs_o.rearrange("(c p) t -> p c t", p=128)

        sb = lambda n, s, d=F32: es.enter_context(nc.sbuf_tensor(n, s, d))
        pt = lambda n, s, d=F32: es.enter_context(nc.psum_tensor(n, s, d))

        ones_col = sb("ones_col", [128, 1])
        ones_row = sb("ones_row", [1, 128])
        eps_row = sb("eps_row", [1, 1])
        Xt = sb("Xt", [128, NCH * 512])
        sq32 = sb("sq32", [128, 512])
        scr2 = sb("scr2", [128, 512])
        a_row = sb("a_row", [1, 512])
        b_row = sb("b_row", [1, 512])
        t_row = sb("t_row", [1, 512])
        xn = sb("xn", [128, NCH * TE], BF16)
        wsl = sb("wsl", [128, NCH * 128], BF16)
        wsl2 = sb("wsl2", [128, NEIN * 128], BF16)
        xraw = sb("xraw", [128, TE], BF16)
        xconv = sb("xconv", [128, NEIN * T_CORE], BF16)
        din = sb("din", [128, NEIN * T_CORE], BF16)
        convw_s = sb("convw_s", [128, 64])
        convb_s = sb("convb_s", [128, NEIN])
        dtb_s = sb("dtb_s", [128, NEIN])
        bc_scr = sb("bc_scr", [16, 512])

        ps_a = pt("ps_a", [1, 512])
        ps_b = pt("ps_b", [1, 512])
        ps_bc1 = pt("ps_bc1", [128, 512])
        ps_bc2 = pt("ps_bc2", [128, 512])
        ps_mm = pt("ps_mm", [128, 512])
        ps_sm = pt("ps_sm", [16, 512])

        sq.step("gpsimd", lambda e: e.memset(ones_col[:], 1.0))
        sq.step("gpsimd", lambda e: e.memset(ones_row[:], 1.0))
        sq.step("gpsimd", lambda e: e.memset(eps_row[:], LN_EPS))
        sq.dma(convw_s[:], conv_w)
        sq.dma(convb_s[:], conv_b)
        sq.dma(dtb_s[:], dt_proj_b)

        for (c0, w) in COLT:
            slow = {"allow_slow_non_contiguous": True} if w < 8 else {}
            sq.dma(Xt[:].rearrange("p (c t) -> p c t", c=NCH)[:, :, 0:w],
                   xe[:, :, c0:c0 + w], **slow)
            sq.step("tensor", lambda e, w=w: [
                e.matmul(ps_a[:, 0:w], ones_col[:], Xt[:, c * 512:c * 512 + w],
                         start=(c == 0), stop=(c == NCH - 1)) for c in range(NCH)])
            for c in range(NCH):
                sq.step("scalar", lambda e, c=c, w=w: e.activation(
                    sq32[:, 0:w], Xt[:, c * 512:c * 512 + w], AF.Square))
                sq.step("tensor", lambda e, c=c, w=w: e.matmul(
                    ps_b[:, 0:w], ones_col[:], sq32[:, 0:w],
                    start=(c == 0), stop=(c == NCH - 1)))
            sq.step("vector", lambda e, w=w: e.tensor_single_scalar(
                b_row[:, 0:w], ps_a[:, 0:w], 1.0 / D_MODEL, ALU.mult))
            sq.step("vector", lambda e, w=w: e.tensor_single_scalar(
                a_row[:, 0:w], ps_b[:, 0:w], 1.0 / D_MODEL, ALU.mult))
            sq.step("scalar", lambda e, w=w: e.activation(
                t_row[:, 0:w], b_row[:, 0:w], AF.Square))
            sq.step("vector", lambda e, w=w: e.tensor_sub(
                a_row[:, 0:w], a_row[:, 0:w], t_row[:, 0:w]))
            sq.step("scalar", lambda e, w=w: e.activation(
                a_row[:, 0:w], a_row[:, 0:w], AF.Ln, bias=eps_row[:]))
            sq.step("scalar", lambda e, w=w: e.activation(
                a_row[:, 0:w], a_row[:, 0:w], AF.Exp, scale=-0.5))
            sq.step("tensor", lambda e, w=w: e.matmul(
                ps_bc1[:, 0:w], ones_row[:], b_row[:, 0:w]))
            sq.step("tensor", lambda e, w=w: e.matmul(
                ps_bc2[:, 0:w], ones_row[:], a_row[:, 0:w]))
            for c in range(NCH):
                sq.step("vector", lambda e, c=c, w=w: e.tensor_sub(
                    sq32[:, 0:w], Xt[:, c * 512:c * 512 + w], ps_bc1[:, 0:w]))
                sq.step("vector", lambda e, c=c, c0=c0, w=w: e.tensor_mul(
                    xn[:, c * TE + c0:c * TE + c0 + w], sq32[:, 0:w], ps_bc2[:, 0:w]))

        for et in range(NEIN):
            sq.dma(wsl[:].rearrange("p (c d) -> p c d", c=NCH),
                   ipv[:, :, et * 128:(et + 1) * 128], engine="gpsimd")
            for (c0, w) in COLT:
                sq.step("tensor", lambda e, et=et, c0=c0, w=w: [
                    e.matmul(ps_mm[:, 0:w], wsl[:, c * 128:(c + 1) * 128],
                             xn[:, c * TE + c0:c * TE + c0 + w],
                             start=(c == 0), stop=(c == NCH - 1)) for c in range(NCH)])
                sq.step("vector", lambda e, c0=c0, w=w: e.tensor_copy(
                    xraw[:, c0:c0 + w], ps_mm[:, 0:w]))
            for (t0, w) in TT4:
                sq.step("vector", lambda e, et=et, t0=t0, w=w: e.tensor_scalar(
                    sq32[:, 0:w], xraw[:, t0:t0 + w], convw_s[:, et * 4:et * 4 + 1],
                    None, ALU.mult))
                for j in range(1, 4):
                    sq.step("vector", lambda e, et=et, t0=t0, w=w, j=j:
                        e.scalar_tensor_tensor(
                            sq32[:, 0:w], xraw[:, t0 + j:t0 + j + w],
                            convw_s[:, et * 4 + j:et * 4 + j + 1],
                            sq32[:, 0:w], ALU.mult, ALU.add))
                sq.step("scalar", lambda e, et=et, t0=t0, w=w: e.activation(
                    xconv[:, et * T_CORE + t0:et * T_CORE + t0 + w], sq32[:, 0:w],
                    AF.Silu, bias=convb_s[:, et:et + 1]))
                sq.step("vector", lambda e, et=et, t0=t0, w=w: e.tensor_copy(
                    scr2[:, 0:w], xconv[:, et * T_CORE + t0:et * T_CORE + t0 + w]))
                sq.dma(xin_v[:, et, t0:t0 + w], scr2[:, 0:w], engine="gpsimd")

        for et in range(NEIN):
            sq.dma(wsl[:].rearrange("p (c d) -> p c d", c=NCH),
                   ipv[:, :, D_INNER + et * 128:D_INNER + (et + 1) * 128],
                   engine="gpsimd")
            for (t0, w) in TT4:
                sq.step("tensor", lambda e, t0=t0, w=w: [
                    e.matmul(ps_mm[:, 0:w], wsl[:, c * 128:(c + 1) * 128],
                             xn[:, c * TE + 3 + t0:c * TE + 3 + t0 + w],
                             start=(c == 0), stop=(c == NCH - 1)) for c in range(NCH)])
                sq.step("scalar", lambda e, w=w: e.activation(
                    scr2[:, 0:w], ps_mm[:, 0:w], AF.Silu))
                sq.dma(zs_v[:, et, t0:t0 + w], scr2[:, 0:w])

        for et in range(NEIN):
            sq.dma(wsl2[:].rearrange("p (c d) -> p c d", c=NEIN),
                   xpv[:, :, et * 128:(et + 1) * 128], engine="gpsimd")
            for (t0, w) in TT4:
                sq.step("tensor", lambda e, t0=t0, w=w: [
                    e.matmul(ps_mm[:, 0:w], wsl2[:, c * 128:(c + 1) * 128],
                             xconv[:, c * T_CORE + t0:c * T_CORE + t0 + w],
                             start=(c == 0), stop=(c == NEIN - 1)) for c in range(NEIN)])
                sq.step("vector", lambda e, et=et, t0=t0, w=w: e.tensor_copy(
                    din[:, et * T_CORE + t0:et * T_CORE + t0 + w], ps_mm[:, 0:w]))
        for bi, out_t in ((0, B_o), (1, C_o)):
            sq.dma(wsl2[:, 0:NEIN * 16].rearrange("p (c d) -> p c d", c=NEIN),
                   xpv[:, :, D_INNER + bi * 16:D_INNER + (bi + 1) * 16],
                   engine="gpsimd")
            for (t0, w) in TT4:
                sq.step("tensor", lambda e, t0=t0, w=w: [
                    e.matmul(ps_sm[:, 0:w], wsl2[:, c * 16:(c + 1) * 16],
                             xconv[:, c * T_CORE + t0:c * T_CORE + t0 + w],
                             start=(c == 0), stop=(c == NEIN - 1)) for c in range(NEIN)])
                sq.step("vector", lambda e, w=w: e.tensor_copy(
                    bc_scr[:, 0:w], ps_sm[:, 0:w]))
                sq.dma(out_t[:, t0:t0 + w], bc_scr[:, 0:w], engine="gpsimd")

        for et in range(NEIN):
            sq.dma(wsl2[:].rearrange("p (c d) -> p c d", c=NEIN),
                   dpv[:, :, et * 128:(et + 1) * 128], engine="gpsimd")
            for (t0, w) in TT4:
                sq.step("tensor", lambda e, t0=t0, w=w: [
                    e.matmul(ps_mm[:, 0:w], wsl2[:, c * 128:(c + 1) * 128],
                             din[:, c * T_CORE + t0:c * T_CORE + t0 + w],
                             start=(c == 0), stop=(c == NEIN - 1)) for c in range(NEIN)])
                sq.step("vector", lambda e, et=et, w=w: e.tensor_scalar(
                    sq32[:, 0:w], ps_mm[:, 0:w], dtb_s[:, et:et + 1], None, ALU.add))
                sq.step("scalar", lambda e, w=w: e.activation(
                    scr2[:, 0:w], sq32[:, 0:w], AF.Abs))
                sq.step("scalar", lambda e, w=w: e.activation(
                    scr2[:, 0:w], scr2[:, 0:w], AF.Exp, scale=-1.0))
                sq.step("scalar", lambda e, w=w: e.activation(
                    scr2[:, 0:w], scr2[:, 0:w], AF.Ln, bias=1.0))
                sq.step("vector", lambda e, w=w: e.tensor_single_scalar(
                    sq32[:, 0:w], sq32[:, 0:w], 0.0, ALU.max))
                sq.step("vector", lambda e, w=w: e.tensor_add(
                    sq32[:, 0:w], sq32[:, 0:w], scr2[:, 0:w]))
                sq.dma(dt_v[:, et, t0:t0 + w], sq32[:, 0:w], engine="gpsimd")

    if emit_only:
        return prog
    return seq_program(prog)


def build_mamba_scan(io=None, emit_only=False):
    L = 4096
    NDT = 8
    NTC = 8

    def prog(nc, sq, es):
        mexg = _dram(nc, io, "mexg", [2 * 4128, T_CORE], BF16, "ExternalInput")
        mA_i = _dram(nc, io, "mA_i", [128, 1], F32, "ExternalInput")
        mB_i = _dram(nc, io, "mB_i", [128, 1], F32, "ExternalInput")
        Alog_i = _dram(nc, io, "Alog_i", [128, NDT * D_STATE], F32, "ExternalInput")
        D_i = _dram(nc, io, "D_i", [128, NDT], F32, "ExternalInput")
        ymy = _dram(nc, io, "ymy", [2048, T_CORE], BF16, "ExternalOutput")

        def mex_view(rank, part, hh):
            base = rank * 4128 + part * 2048 + hh * 1024
            return mexg[base:base + 1024, :].rearrange("(c p) t -> p c t", p=128)

        ymy_v = ymy.rearrange("(hh c p) t -> hh p c t", p=128, hh=2)

        sb = lambda n, s, d=F32: es.enter_context(nc.sbuf_tensor(n, s, d))
        pt = lambda n, s, d=F32: es.enter_context(nc.psum_tensor(n, s, d))

        ones_row = sb("ones_row", [1, 128])
        dt_s = sb("dt_s", [128, NDT * 512])
        xin_s = sb("xin_s", [128, NDT * 512])
        dtx = sb("dtx", [128, NDT * 512])
        y_s = sb("y_s", [128, NDT * 512])
        A_s = sb("A_s", [128, NDT * D_STATE])
        D_s = sb("D_s", [128, NDT])
        h_st = sb("h_st", [128, NDT * D_STATE])
        B_row = sb("B_row", [1, 512])
        C_row = sb("C_row", [1, 512])
        Brep = sb("Brep", [128, 512])
        Crep = sb("Crep", [128, 512])
        dA = sb("dA", [128, 512])
        u_s = sb("u_s", [128, 512])
        hs = sb("hs", [128, 512])
        t1 = sb("t1", [128, 512])

        ps_b1 = pt("ps_b1", [128, 512])
        ps_b2 = pt("ps_b2", [128, 512])

        sq.step("gpsimd", lambda e: e.memset(ones_row[:], 1.0))
        sq.step("gpsimd", lambda e: e.memset(h_st[:], 0.0))
        sq.dma(A_s[:], Alog_i)
        sq.dma(D_s[:], D_i)
        sq.step("scalar", lambda e: e.activation(A_s[:], A_s[:], AF.Exp))
        sq.step("vector", lambda e: e.tensor_single_scalar(A_s[:], A_s[:], -1.0, ALU.mult))

        cnd0 = es.enter_context(nc.sbuf_tensor("cnd0", [128, NDT * 512], BF16))
        cnd1 = es.enter_context(nc.sbuf_tensor("cnd1", [128, NDT * 512], BF16))
        mA_s = es.enter_context(nc.sbuf_tensor("mA_s", [128, 1], F32))
        mB_s = es.enter_context(nc.sbuf_tensor("mB_s", [128, 1], F32))
        blsc = es.enter_context(nc.sbuf_tensor("blsc", [128, 512], F32))
        sq.dma(mA_s[:], mA_i)
        sq.dma(mB_s[:], mB_i)
        for tc in range(NTC):
            t0 = tc * 512
            rank, t0l = tc // 4, (tc % 4) * 512
            for part, dst in ((0, dt_s), (1, xin_s)):
                sq.dma(cnd0[:].rearrange("p (c t) -> p c t", c=NDT),
                       mex_view(rank, part, 0)[:, :, t0l:t0l + 512])
                sq.dma(cnd1[:].rearrange("p (c t) -> p c t", c=NDT),
                       mex_view(rank, part, 1)[:, :, t0l:t0l + 512])
                for d in range(NDT):
                    sl = slice(d * 512, (d + 1) * 512)
                    sq.step("vector", lambda e, sl=sl: e.tensor_scalar(
                        blsc[:], cnd0[:, sl], mA_s[:], None, ALU.mult))
                    sq.step("vector", lambda e, sl=sl, dst=dst: e.scalar_tensor_tensor(
                        dst[:, sl], cnd1[:, sl], mB_s[:], blsc[:],
                        ALU.mult, ALU.add))
            for d in range(NDT):
                sq.step("vector", lambda e, d=d: e.tensor_mul(
                    dtx[:, d * 512:(d + 1) * 512], dt_s[:, d * 512:(d + 1) * 512],
                    xin_s[:, d * 512:(d + 1) * 512]))
            sq.step("gpsimd", lambda e: e.memset(y_s[:], 0.0))
            for s in range(D_STATE):
                rb = rank * 4128 + 4096
                sq.dma(B_row[:], mexg[rb + s:rb + s + 1, t0l:t0l + 512],
                       engine="gpsimd")
                sq.dma(C_row[:], mexg[rb + 16 + s:rb + 16 + s + 1, t0l:t0l + 512],
                       engine="gpsimd")
                sq.step("tensor", lambda e: e.matmul(ps_b1[:], ones_row[:], B_row[:]))
                sq.step("tensor", lambda e: e.matmul(ps_b2[:], ones_row[:], C_row[:]))
                sq.step("vector", lambda e: e.tensor_copy(Brep[:], ps_b1[:]))
                sq.step("vector", lambda e: e.tensor_copy(Crep[:], ps_b2[:]))
                for d in range(NDT):
                    col = d * D_STATE + s
                    sq.step("scalar", lambda e, d=d, col=col: e.activation(
                        dA[:], dt_s[:, d * 512:(d + 1) * 512], AF.Exp,
                        scale=A_s[:, col:col + 1]))
                    sq.step("vector", lambda e, d=d: e.tensor_mul(
                        u_s[:], dtx[:, d * 512:(d + 1) * 512], Brep[:]))
                    sq.step("vector", lambda e, col=col: e.tensor_tensor_scan(
                        hs[:], dA[:], u_s[:], h_st[:, col:col + 1],
                        ALU.mult, ALU.add))
                    sq.step("vector", lambda e, col=col: e.tensor_copy(
                        h_st[:, col:col + 1], hs[:, 511:512]))
                    sq.step("vector", lambda e: e.tensor_mul(t1[:], hs[:], Crep[:]))
                    sq.step("vector", lambda e, d=d: e.tensor_add(
                        y_s[:, d * 512:(d + 1) * 512],
                        y_s[:, d * 512:(d + 1) * 512], t1[:]))
            for d in range(NDT):
                sq.step("vector", lambda e, d=d: e.scalar_tensor_tensor(
                    y_s[:, d * 512:(d + 1) * 512], xin_s[:, d * 512:(d + 1) * 512],
                    D_s[:, d:d + 1], y_s[:, d * 512:(d + 1) * 512],
                    ALU.mult, ALU.add))
                sq.dma(ymy_v[tc // 4, :, d, t0l:t0l + 512],
                       y_s[:, d * 512:(d + 1) * 512], engine="gpsimd")

    if emit_only:
        return prog
    return seq_program(prog)


def build_mamba_post(io=None, emit_only=False):
    NT, T = 4, 512

    def prog(nc, sq, es):
        x_fm = _dram(nc, io, "x_fm", [D_MODEL, T_CORE], F32, "ExternalInput")
        yag = _dram(nc, io, "yag", [2 * 2048, T_CORE], BF16, "ExternalInput")
        mA_i = _dram(nc, io, "mA_i", [128, 1], F32, "ExternalInput")
        mB_i = _dram(nc, io, "mB_i", [128, 1], F32, "ExternalInput")
        zs_fm = _dram(nc, io, "zs_fm", [D_INNER, T_CORE], F32, "ExternalInput")
        opT = _dram(nc, io, "opT", [D_INNER, D_MODEL], F32, "ExternalInput")
        x_out = _dram(nc, io, "x_out", [D_MODEL, T_CORE], F32, "ExternalOutput")

        xv = x_fm.rearrange("(c p) t -> p c t", p=128)
        yagv = yag.rearrange("(r hh c p) t -> r hh p c t", r=2, hh=2, p=128)
        zv = zs_fm.rearrange("(c p) t -> p c t", p=128)
        ov = opT.rearrange("(c p) d -> p c d", p=128)
        xo = x_out.rearrange("(c p) t -> p c t", p=128)

        sb = lambda n, s, d=F32: es.enter_context(nc.sbuf_tensor(n, s, d))
        pt = lambda n, s, d=F32: es.enter_context(nc.psum_tensor(n, s, d))

        w_s = sb("w_s", [128, NEIN * D_MODEL], BF16)
        y_t = sb("y_t", [128, NEIN * T])
        ycn0 = sb("ycn0", [128, NEIN * T], BF16)
        ycn1 = sb("ycn1", [128, NEIN * T], BF16)
        mA_s = sb("mA_s", [128, 1])
        mB_s = sb("mB_s", [128, 1])
        ybl = sb("ybl", [128, T])
        z_t = sb("z_t", [128, NEIN * T])
        g_t = sb("g_t", [128, NEIN * T], BF16)
        Xt = sb("Xt", [128, NCH * T])
        oscr = sb("oscr", [128, T])
        ps_mm = pt("ps_mm", [128, 512])

        sq.dma(w_s[:].rearrange("p (c d) -> p c d", c=NEIN), ov, engine="gpsimd")
        sq.dma(mA_s[:], mA_i)
        sq.dma(mB_s[:], mB_i)
        for tt in range(NT):
            t0 = tt * T
            for r in range(2):
                sq.dma(ycn0[:].rearrange("p (c t) -> p c t", c=NEIN)[:, r * 8:(r + 1) * 8],
                       yagv[r, 0, :, :, t0:t0 + T])
                sq.dma(ycn1[:].rearrange("p (c t) -> p c t", c=NEIN)[:, r * 8:(r + 1) * 8],
                       yagv[r, 1, :, :, t0:t0 + T])
            for c in range(NEIN):
                sl = slice(c * T, (c + 1) * T)
                sq.step("vector", lambda e, sl=sl: e.tensor_scalar(
                    ybl[:], ycn0[:, sl], mA_s[:], None, ALU.mult))
                sq.step("vector", lambda e, sl=sl: e.scalar_tensor_tensor(
                    y_t[:, sl], ycn1[:, sl], mB_s[:], ybl[:], ALU.mult, ALU.add))
            sq.dma(z_t[:].rearrange("p (c t) -> p c t", c=NEIN), zv[:, :, t0:t0 + T])
            sq.dma(Xt[:].rearrange("p (c t) -> p c t", c=NCH), xv[:, :, t0:t0 + T])
            for c in range(NEIN):
                sq.step("vector", lambda e, c=c: e.tensor_mul(
                    g_t[:, c * T:(c + 1) * T], y_t[:, c * T:(c + 1) * T],
                    z_t[:, c * T:(c + 1) * T]))
            for d in range(NCH):
                sq.step("tensor", lambda e, d=d: [
                    e.matmul(ps_mm[:], w_s[:, c * D_MODEL + d * 128:
                                           c * D_MODEL + (d + 1) * 128],
                             g_t[:, c * T:(c + 1) * T],
                             start=(c == 0), stop=(c == NEIN - 1))
                    for c in range(NEIN)])
                sq.step("vector", lambda e, d=d: e.tensor_add(
                    oscr[:], ps_mm[:], Xt[:, d * T:(d + 1) * T]))
                sq.dma(xo[:, d, t0:t0 + T], oscr[:])

    if emit_only:
        return prog
    return seq_program(prog)


def host_mamba_pre_inputs(x, W, core):
    b, h = core // 2, core % 2
    sl = core_slice_tokens(x, core)
    if h == 0:
        halo = np.zeros((3, D_MODEL), np.float32)
    else:
        halo = np.asarray(x[b, h * 2048 - 3:h * 2048])
    x_ext = fmT(np.concatenate([halo, sl], axis=0))
    cw = np.asarray(W["conv_w"], np.float32).reshape(D_INNER, 4)
    cw_p = np.ascontiguousarray(cw.reshape(NEIN, 128, 4).transpose(1, 0, 2).reshape(128, 64))
    return {
        "x_ext": x_ext,
        "in_projT": fmT(W["in_proj_w"]),
        "x_projT": fmT(W["x_proj_w"]),
        "dt_projT": fmT(W["dt_proj_w"]),
        "conv_w": cw_p,
        "conv_b": pack_vec(W["conv_b"]),
        "dt_proj_b": pack_vec(W["dt_proj_b"]),
    }


def host_mamba_scan_inputs(pre_outs, W, core):
    b, j = core // 2, core % 2
    def full(key, rows):
        return np.concatenate([pre_outs[2 * b][key], pre_outs[2 * b + 1][key]], axis=1)
    dt_f = full("dt_o", D_INNER)[j * 1024:(j + 1) * 1024]
    xin_f = full("xin_o", D_INNER)[j * 1024:(j + 1) * 1024]
    B_f = full("B_o", D_STATE)
    C_f = full("C_o", D_STATE)
    Alog = np.asarray(W["A_log"], np.float32)[j * 1024:(j + 1) * 1024]
    Alog_p = np.ascontiguousarray(
        Alog.reshape(8, 128, D_STATE).transpose(1, 0, 2).reshape(128, 8 * D_STATE))
    Dv = np.asarray(W["D"], np.float32)[j * 1024:(j + 1) * 1024]
    return {
        "dt_i": np.ascontiguousarray(dt_f), "xin_i": np.ascontiguousarray(xin_f),
        "B_i": np.ascontiguousarray(B_f), "C_i": np.ascontiguousarray(C_f),
        "Alog_i": Alog_p, "D_i": pack_vec(Dv),
    }


def host_mamba_post_inputs(x, scan_outs, pre_outs, W, core):
    b, h = core // 2, core % 2
    y_full = np.concatenate([scan_outs[2 * b]["y_o"], scan_outs[2 * b + 1]["y_o"]],
                            axis=0)
    y_slice = y_full[:, h * 2048:(h + 1) * 2048]
    zs = pre_outs[core]["zs_o"]
    return {
        "x_fm": fmT(core_slice_tokens(x, core)),
        "y_fm": np.ascontiguousarray(y_slice),
        "zs_fm": np.ascontiguousarray(zs),
        "opT": fmT(W["out_proj_w"]),
    }



def build_wkv_kv(io=None, emit_only=False):
    NT, T = 4, 512

    def prog(nc, sq, es):
        x_fm = _dram(nc, io, "x_fm", [D_MODEL, T_CORE], F32, "ExternalInput")
        xn_my = _dram(nc, io, "xn_my", [D_MODEL, T_CORE], BF16, "ExternalOutput")
        xnag = _dram(nc, io, "xnag", [8 * D_MODEL, T_CORE], BF16, "ExternalInput")
        idxp_i = _dram(nc, io, "idxp_i", [128, NCH], U32, "ExternalInput")
        mz_i = _dram(nc, io, "mz_i", [128, 1], F32, "ExternalInput")
        wkT = _dram(nc, io, "wkT", [D_MODEL, D_MODEL], F32, "ExternalInput")
        wvT = _dram(nc, io, "wvT", [D_MODEL, D_MODEL], F32, "ExternalInput")
        wrT = _dram(nc, io, "wrT", [D_MODEL, D_MODEL], F32, "ExternalInput")
        mu_k = _dram(nc, io, "mu_k", [128, NCH], F32, "ExternalInput")
        mu_v = _dram(nc, io, "mu_v", [128, NCH], F32, "ExternalInput")
        mu_r = _dram(nc, io, "mu_r", [128, NCH], F32, "ExternalInput")
        ek_o = _dram(nc, io, "ek_o", [D_MODEL, T_CORE], F32, "ExternalOutput")
        ekv_o = _dram(nc, io, "ekv_o", [D_MODEL, T_CORE], F32, "ExternalOutput")
        ekekv_my = _dram(nc, io, "ekekv_my", [2 * D_MODEL, T_CORE], BF16,
                         "ExternalOutput")
        r_o = _dram(nc, io, "r_o", [D_MODEL, T_CORE], F32, "ExternalOutput")

        xv_ = x_fm.rearrange("(c p) t -> p c t", p=128)
        eko = ek_o.rearrange("(c p) t -> p c t", p=128)
        ekvo = ekv_o.rearrange("(c p) t -> p c t", p=128)
        ekmy = ekekv_my[0:D_MODEL, :].rearrange("(c p) t -> p c t", p=128)
        ekvmy = ekekv_my[D_MODEL:2 * D_MODEL, :].rearrange("(c p) t -> p c t", p=128)
        ro = r_o.rearrange("(c p) t -> p c t", p=128)

        sb = lambda n, s, d=F32: es.enter_context(nc.sbuf_tensor(n, s, d))
        pt = lambda n, s, d=F32: es.enter_context(nc.psum_tensor(n, s, d))

        ones_col = sb("ones_col", [128, 1])
        ones_row = sb("ones_row", [1, 128])
        eps_row = sb("eps_row", [1, 1])
        Xt = sb("Xt", [128, NCH * 512])
        sq32 = sb("sq32", [128, 512])
        a_row = sb("a_row", [1, 512])
        b_row = sb("b_row", [1, 512])
        t_row = sb("t_row", [1, 512])
        xn = sb("xn", [128, NCH * T_CORE], BF16)
        xnp = sb("xnp", [128, NCH * T_CORE], BF16)
        w_s = sb("w_s", [128, NCH * D_MODEL], BF16)
        mus = {nm: sb(f"mus_{nm}", [128, NCH]) for nm in ("k", "v", "r")}
        xm = sb("xm", [128, NCH * T], BF16)
        ek_t = sb("ek_t", [128, 512])
        v_t = sb("v_t", [128, 512])
        oscr = sb("oscr", [128, 512])
        ps_a = pt("ps_a", [1, 512])
        ps_b = pt("ps_b", [1, 512])
        ps_bc1 = pt("ps_bc1", [128, 512])
        ps_bc2 = pt("ps_bc2", [128, 512])
        ps_mm = pt("ps_mm", [128, 512])
        ps_mm2 = pt("ps_mm2", [128, 512])

        sq.step("gpsimd", lambda e: e.memset(ones_col[:], 1.0))
        sq.step("gpsimd", lambda e: e.memset(ones_row[:], 1.0))
        sq.step("gpsimd", lambda e: e.memset(eps_row[:], LN_EPS))
        for nm, t in mus.items():
            sq.dma(t[:], {"k": mu_k, "v": mu_v, "r": mu_r}[nm])

        def ln_to(xsrc, dst):
            for it in range(4):
                c0, w = it * 512, 512
                sq.dma(Xt[:].rearrange("p (c t) -> p c t", c=NCH), xsrc[:, :, c0:c0 + w])
                sq.step("tensor", lambda e, w=w: [
                    e.matmul(ps_a[:, 0:w], ones_col[:], Xt[:, c * 512:c * 512 + w],
                             start=(c == 0), stop=(c == NCH - 1)) for c in range(NCH)])
                for c in range(NCH):
                    sq.step("scalar", lambda e, c=c, w=w: e.activation(
                        sq32[:, 0:w], Xt[:, c * 512:c * 512 + w], AF.Square))
                    sq.step("tensor", lambda e, c=c, w=w: e.matmul(
                        ps_b[:, 0:w], ones_col[:], sq32[:, 0:w],
                        start=(c == 0), stop=(c == NCH - 1)))
                sq.step("vector", lambda e, w=w: e.tensor_single_scalar(
                    b_row[:, 0:w], ps_a[:, 0:w], 1.0 / D_MODEL, ALU.mult))
                sq.step("vector", lambda e, w=w: e.tensor_single_scalar(
                    a_row[:, 0:w], ps_b[:, 0:w], 1.0 / D_MODEL, ALU.mult))
                sq.step("scalar", lambda e, w=w: e.activation(
                    t_row[:, 0:w], b_row[:, 0:w], AF.Square))
                sq.step("vector", lambda e, w=w: e.tensor_sub(
                    a_row[:, 0:w], a_row[:, 0:w], t_row[:, 0:w]))
                sq.step("scalar", lambda e, w=w: e.activation(
                    a_row[:, 0:w], a_row[:, 0:w], AF.Ln, bias=eps_row[:]))
                sq.step("scalar", lambda e, w=w: e.activation(
                    a_row[:, 0:w], a_row[:, 0:w], AF.Exp, scale=-0.5))
                sq.step("tensor", lambda e, w=w: e.matmul(
                    ps_bc1[:, 0:w], ones_row[:], b_row[:, 0:w]))
                sq.step("tensor", lambda e, w=w: e.matmul(
                    ps_bc2[:, 0:w], ones_row[:], a_row[:, 0:w]))
                for c in range(NCH):
                    sq.step("vector", lambda e, c=c, w=w: e.tensor_sub(
                        sq32[:, 0:w], Xt[:, c * 512:c * 512 + w], ps_bc1[:, 0:w]))
                    sq.step("vector", lambda e, c=c, c0=c0, w=w: e.tensor_mul(
                        dst[:, c * T_CORE + c0:c * T_CORE + c0 + w], sq32[:, 0:w],
                        ps_bc2[:, 0:w]))

        ln_to(xv_, xn)
        sq.dma(xn_my.rearrange("(c p) t -> p c t", p=128),
               xn[:].rearrange("p (c t) -> p c t", c=NCH))
        if io is not None:
            sq.step("gpsimd", lambda e: e.collective_compute(
                "AllGather", ALU.bypass, replica_groups=[list(range(8))],
                ins=[io["xn_my_t"].ap().opt()], outs=[io["xnag_t"].ap().opt()]),
                inc=1, sclass="cc0")
        idxp_s = es.enter_context(nc.sbuf_tensor("idxp_s", [128, NCH], U32))
        mz_s = es.enter_context(nc.sbuf_tensor("mz_s", [128, 1], F32))
        sq.dma(idxp_s[:], idxp_i)
        sq.dma(mz_s[:], mz_i)
        for c in range(NCH):
            sq.step("gpsimd", lambda e, c=c: e.indirect_dma_start(
                out=xnp[:, c * T_CORE:(c + 1) * T_CORE],
                out_offset=None, in_=xnag,
                in_offset=IndirectOffsetOnAxis(ap=idxp_s[:, c:c + 1], axis=0),
                oob_is_err=False), inc=16, dma=True)
            for tt4 in range(4):
                t0g = tt4 * 512
                sq.step("vector", lambda e, c=c, t0g=t0g: e.tensor_scalar(
                    xnp[:, c * T_CORE + t0g:c * T_CORE + t0g + 512],
                    xnp[:, c * T_CORE + t0g:c * T_CORE + t0g + 512],
                    mz_s[:], None, ALU.mult))

        for kind in ("k", "v", "r"):
            wsrc = {"k": wkT, "v": wvT, "r": wrT}[kind]
            sq.dma(w_s[:].rearrange("p (c d) -> p c d", c=NCH),
                   wsrc.rearrange("(c p) d -> p c d", p=128), engine="gpsimd")
            mu = mus[kind]
            for tt in range(NT):
                t0 = tt * T
                for c in range(NCH):
                    sq.step("vector", lambda e, c=c, t0=t0: e.tensor_sub(
                        sq32[:], xn[:, c * T_CORE + t0:c * T_CORE + t0 + T],
                        xnp[:, c * T_CORE + t0:c * T_CORE + t0 + T]))
                    sq.step("vector", lambda e, c=c, t0=t0, mu=mu: e.scalar_tensor_tensor(
                        xm[:, c * T:(c + 1) * T], sq32[:], mu[:, c:c + 1],
                        xnp[:, c * T_CORE + t0:c * T_CORE + t0 + T],
                        ALU.mult, ALU.add))
                for d in range(NCH):
                    sq.step("tensor", lambda e, d=d: [
                        e.matmul(ps_mm[:], w_s[:, c * D_MODEL + d * 128:
                                               c * D_MODEL + (d + 1) * 128],
                                 xm[:, c * T:(c + 1) * T],
                                 start=(c == 0), stop=(c == NCH - 1))
                        for c in range(NCH)])
                    if kind == "k":
                        sq.step("scalar", lambda e: e.activation(
                            ek_t[:], ps_mm[:], AF.Exp))
                        sq.dma(eko[:, d, t0:t0 + T], ek_t[:])
                        sq.dma(ekmy[:, d, t0:t0 + T], ek_t[:], engine="gpsimd")
                    elif kind == "v":
                        sq.dma(ek_t[:], eko[:, d, t0:t0 + T])
                        sq.step("vector", lambda e: e.tensor_copy(v_t[:], ps_mm[:]))
                        sq.step("vector", lambda e: e.tensor_mul(
                            oscr[:], ek_t[:], v_t[:]))
                        sq.dma(ekvo[:, d, t0:t0 + T], oscr[:])
                        sq.dma(ekvmy[:, d, t0:t0 + T], oscr[:], engine="gpsimd")
                    else:
                        sq.step("vector", lambda e: e.tensor_copy(oscr[:], ps_mm[:]))
                        sq.dma(ro[:, d, t0:t0 + T], oscr[:])
        if io is not None:
            sq.step("gpsimd", lambda e: e.collective_compute(
                "AllGather", ALU.bypass, replica_groups=[list(range(8))],
                ins=[io["ekekv_my_t"].ap().opt()],
                outs=[io["ekekvg_t"].ap().opt()]), inc=1, sclass="cc1")

    if emit_only:
        return prog
    return seq_program(prog)


def build_wkv_out(io=None, emit_only=False):
    NT, T = 4, 512

    def prog(nc, sq, es):
        x_fm = _dram(nc, io, "x_fm", [D_MODEL, T_CORE], F32, "ExternalInput")
        ekekvg = _dram(nc, io, "ekekvg", [8 * 2 * D_MODEL, T_CORE], BF16,
                       "ExternalInput")
        idxo1_i = _dram(nc, io, "idxo1_i", [128, NCH], U32, "ExternalInput")
        idxo2_i = _dram(nc, io, "idxo2_i", [128, NCH], U32, "ExternalInput")
        r_fm = _dram(nc, io, "r_fm", [D_MODEL, T_CORE], F32, "ExternalInput")
        woT = _dram(nc, io, "woT", [D_MODEL, D_MODEL], F32, "ExternalInput")
        wv_ = _dram(nc, io, "wv_", [128, NCH], F32, "ExternalInput")
        uv_ = _dram(nc, io, "uv_", [128, NCH], F32, "ExternalInput")
        steps = _dram(nc, io, "steps", [128, 8], F32, "ExternalInput")
        mask = _dram(nc, io, "mask", [128, 8], F32, "ExternalInput")
        x_out = _dram(nc, io, "x_out", [D_MODEL, T_CORE], F32, "ExternalOutput")

        xv_ = x_fm.rearrange("(c p) t -> p c t", p=128)
        ekg = ekekvg.rearrange("(r hh c p) t -> r hh p c t", r=8, hh=2, p=128)
        rv_ = r_fm.rearrange("(c p) t -> p c t", p=128)
        wov = woT.rearrange("(c p) d -> p c d", p=128)
        xo = x_out.rearrange("(c p) t -> p c t", p=128)

        sb = lambda n, s, d=F32: es.enter_context(nc.sbuf_tensor(n, s, d))
        pt = lambda n, s, d=F32: es.enter_context(nc.psum_tensor(n, s, d))

        w_s = sb("w_s", [128, NCH * D_MODEL], BF16)
        wneg = sb("wneg", [128, NCH])
        iew = sb("iew", [128, NCH])
        eu = sb("eu", [128, NCH])
        dec = sb("dec", [128, NCH * 8])
        st_s = sb("st_s", [128, 8])
        mk_s = sb("mk_s", [128, 8])
        ek_own = sb("ek_own", [128, NCH * T_CORE], BF16)
        ekv_own = sb("ekv_own", [128, NCH * T_CORE], BF16)
        bufb = sb("bufb", [128, NCH * T], BF16)
        idxo1_s = sb("idxo1_s", [128, NCH], U32)
        idxo2_s = sb("idxo2_s", [128, NCH], U32)
        cek = sb("cek", [128, NCH * T])
        cekv = sb("cekv", [128, NCH * T])
        buf = sb("buf", [128, NCH * T])
        num = sb("num", [128, NCH * T])
        wkb = sb("wkb", [128, NCH * T], BF16)
        sq32 = sb("sq32", [128, 512])
        scr2 = sb("scr2", [128, 512])
        Xt = sb("Xt", [128, NCH * T])
        oscr = sb("oscr", [128, 512])
        ps_mm = pt("ps_mm", [128, 512])

        sq.dma(wneg[:], wv_)
        sq.dma(eu[:], uv_)
        sq.dma(st_s[:], steps)
        sq.dma(mk_s[:], mask)
        sq.dma(w_s[:].rearrange("p (c d) -> p c d", c=NCH),
               wov, engine="gpsimd")
        sq.step("scalar", lambda e: e.activation(wneg[:], wneg[:], AF.Exp))
        sq.step("vector", lambda e: e.tensor_single_scalar(
            wneg[:], wneg[:], -1.0, ALU.mult))
        sq.step("scalar", lambda e: e.activation(iew[:], wneg[:], AF.Exp, scale=-1.0))
        sq.step("scalar", lambda e: e.activation(eu[:], eu[:], AF.Exp))
        sq.dma(idxo1_s[:], idxo1_i)
        sq.dma(idxo2_s[:], idxo2_i)
        for c in range(NCH):
            for bp in range(8):
                sq.step("vector", lambda e, c=c, bp=bp: e.tensor_mul(
                    dec[:, c * 8 + bp:c * 8 + bp + 1], wneg[:, c:c + 1],
                    st_s[:, bp:bp + 1]))
        sq.step("scalar", lambda e: e.activation(dec[:], dec[:], AF.Exp))
        for c in range(NCH):
            sq.step("vector", lambda e, c=c: e.tensor_mul(
                dec[:, c * 8:(c + 1) * 8], dec[:, c * 8:(c + 1) * 8], mk_s[:]))

        for tt in range(NT):
            t0 = tt * T
            sq.step("gpsimd", lambda e: e.memset(cek[:], 0.0))
            sq.step("gpsimd", lambda e: e.memset(cekv[:], 0.0))
            for bp in range(8):
                sq.dma(bufb[:].rearrange("p (c t) -> p c t", c=NCH),
                       ekg[bp, 0, :, :, t0:t0 + T])
                for c in range(NCH):
                    sq.step("vector", lambda e, c=c, bp=bp: e.scalar_tensor_tensor(
                        cek[:, c * T:(c + 1) * T], bufb[:, c * T:(c + 1) * T],
                        dec[:, c * 8 + bp:c * 8 + bp + 1],
                        cek[:, c * T:(c + 1) * T], ALU.mult, ALU.add))
                sq.dma(bufb[:].rearrange("p (c t) -> p c t", c=NCH),
                       ekg[bp, 1, :, :, t0:t0 + T])
                for c in range(NCH):
                    sq.step("vector", lambda e, c=c, bp=bp: e.scalar_tensor_tensor(
                        cekv[:, c * T:(c + 1) * T], bufb[:, c * T:(c + 1) * T],
                        dec[:, c * 8 + bp:c * 8 + bp + 1],
                        cekv[:, c * T:(c + 1) * T], ALU.mult, ALU.add))
            if tt == 0:
                for c in range(NCH):
                    sq.step("gpsimd", lambda e, c=c: e.indirect_dma_start(
                        out=ek_own[:, c * T_CORE:(c + 1) * T_CORE], out_offset=None,
                        in_=ekekvg,
                        in_offset=IndirectOffsetOnAxis(ap=idxo1_s[:, c:c + 1], axis=0),
                        oob_is_err=False), inc=16, dma=True)
                    sq.step("gpsimd", lambda e, c=c: e.indirect_dma_start(
                        out=ekv_own[:, c * T_CORE:(c + 1) * T_CORE], out_offset=None,
                        in_=ekekvg,
                        in_offset=IndirectOffsetOnAxis(ap=idxo2_s[:, c:c + 1], axis=0),
                        oob_is_err=False), inc=16, dma=True)
            for c in range(NCH):
                sq.step("vector", lambda e, c=c, t0=t0: e.tensor_sub(
                    sq32[:], cek[:, c * T:(c + 1) * T],
                    ek_own[:, c * T_CORE + t0:c * T_CORE + t0 + T]))
                sq.step("vector", lambda e, c=c: e.tensor_scalar(
                    sq32[:], sq32[:], iew[:, c:c + 1], None, ALU.mult))
                sq.step("vector", lambda e, c=c, t0=t0: e.scalar_tensor_tensor(
                    buf[:, c * T:(c + 1) * T],
                    ek_own[:, c * T_CORE + t0:c * T_CORE + t0 + T],
                    eu[:, c:c + 1], sq32[:], ALU.mult, ALU.add))
                sq.step("vector", lambda e, c=c, t0=t0: e.tensor_sub(
                    sq32[:], cekv[:, c * T:(c + 1) * T],
                    ekv_own[:, c * T_CORE + t0:c * T_CORE + t0 + T]))
                sq.step("vector", lambda e, c=c: e.tensor_scalar(
                    sq32[:], sq32[:], iew[:, c:c + 1], None, ALU.mult))
                sq.step("vector", lambda e, c=c, t0=t0: e.scalar_tensor_tensor(
                    num[:, c * T:(c + 1) * T],
                    ekv_own[:, c * T_CORE + t0:c * T_CORE + t0 + T],
                    eu[:, c:c + 1], sq32[:], ALU.mult, ALU.add))
                sq.step("vector", lambda e, c=c: e.reciprocal(
                    scr2[:], buf[:, c * T:(c + 1) * T]))
                sq.step("vector", lambda e, c=c: e.tensor_mul(
                    sq32[:], num[:, c * T:(c + 1) * T], scr2[:]))
                sq.dma(Xt[:, 0:T], rv_[:, c, t0:t0 + T])
                sq.step("scalar", lambda e: e.activation(scr2[:], Xt[:, 0:T], AF.Sigmoid))
                sq.step("vector", lambda e, c=c: e.tensor_mul(
                    wkb[:, c * T:(c + 1) * T], sq32[:], scr2[:]))
            sq.dma(Xt[:].rearrange("p (c t) -> p c t", c=NCH), xv_[:, :, t0:t0 + T])
            for d in range(NCH):
                sq.step("tensor", lambda e, d=d: [
                    e.matmul(ps_mm[:], w_s[:, c * D_MODEL + d * 128:
                                           c * D_MODEL + (d + 1) * 128],
                             wkb[:, c * T:(c + 1) * T],
                             start=(c == 0), stop=(c == NCH - 1))
                    for c in range(NCH)])
                sq.step("vector", lambda e, d=d: e.tensor_add(
                    oscr[:], ps_mm[:], Xt[:, d * T:(d + 1) * T]))
                sq.dma(xo[:, d, t0:t0 + T], oscr[:])

    if emit_only:
        return prog
    return seq_program(prog)


def host_wkv_kv_inputs(x, W, core):
    b, h = core // 2, core % 2
    if b == 0:
        xp = np.zeros((T_CORE, D_MODEL), np.float32)
    else:
        xp = np.asarray(x[b - 1, h * 2048:(h + 1) * 2048])
    return {
        "x_fm": fmT(core_slice_tokens(x, core)),
        "xp_fm": fmT(xp),
        "wkT": fmT(W["Wk"]), "wvT": fmT(W["Wv"]), "wrT": fmT(W["Wr"]),
        "mu_k": pack_vec(W["mu_k"]), "mu_v": pack_vec(W["mu_v"]),
        "mu_r": pack_vec(W["mu_r"]),
    }


def host_wkv_out_inputs(x, kv_outs, W, core):
    b, h = core // 2, core % 2
    order = [(b - j) % 4 for j in range(4)]
    ek = np.stack([kv_outs[2 * bp + h]["ek_o"] for bp in order])
    ekv = np.stack([kv_outs[2 * bp + h]["ekv_o"] for bp in order])
    steps = np.zeros((128, 4), np.float32)
    maskv = np.zeros((128, 4), np.float32)
    for j, bp in enumerate(order):
        steps[:, j] = b - bp
        maskv[:, j] = 1.0 if bp <= b else 0.0
    return {
        "x_fm": fmT(core_slice_tokens(x, core)),
        "ek_all": np.ascontiguousarray(ek),
        "ekv_all": np.ascontiguousarray(ekv),
        "r_fm": kv_outs[core]["r_o"],
        "woT": fmT(W["W_out"]),
        "wv_": pack_vec(W["w"]), "uv_": pack_vec(W["u"]),
        "steps": steps, "mask": maskv,
    }



ATT_SCALE = float(1.0 / np.sqrt(16 + 512 + 512))


def build_mla_qkv(io=None, emit_only=False):
    DQ = 512
    NQ = 4

    def prog(nc, sq, es):
        x_fm = _dram(nc, io, "x_fm", [D_MODEL, T_CORE], F32, "ExternalInput")
        wdq = _dram(nc, io, "wdq", [D_MODEL, DQ], F32, "ExternalInput")
        wdkv = _dram(nc, io, "wdkv", [D_MODEL, DQ], F32, "ExternalInput")
        wuq = _dram(nc, io, "wuq", [DQ, D_MODEL], F32, "ExternalInput")
        wqr = _dram(nc, io, "wqr", [DQ, D_MODEL], F32, "ExternalInput")
        wuk = _dram(nc, io, "wuk", [DQ, D_MODEL], F32, "ExternalInput")
        wuv = _dram(nc, io, "wuv", [DQ, D_MODEL], F32, "ExternalInput")
        wkr = _dram(nc, io, "wkr", [D_MODEL, D_MODEL], F32, "ExternalInput")
        b_dq = _dram(nc, io, "b_dq", [128, NQ], F32, "ExternalInput")
        b_dkv = _dram(nc, io, "b_dkv", [128, NQ], F32, "ExternalInput")
        brows = {nm: _dram(nc, io, f"brow_{nm}", [1, D_MODEL], F32, "ExternalInput")
                 for nm in ("uq", "qr", "kr", "uk", "uv")}
        cos_t = _dram(nc, io, "cos_t", [T_CORE, DQ], F32, "ExternalInput")
        sin_t = _dram(nc, io, "sin_t", [T_CORE, DQ], F32, "ExternalInput")
        q_o = _dram(nc, io, "q_o", [D_MODEL, T_CORE], BF16, "ExternalOutput")
        k_o = _dram(nc, io, "k_o", [D_MODEL, T_CORE], BF16, "ExternalOutput")
        v_o = _dram(nc, io, "v_o", [T_CORE, 16 * 65], BF16, "ExternalOutput")

        xv_ = x_fm.rearrange("(c p) t -> p c t", p=128)
        qo = q_o.rearrange("(c p) t -> p c t", p=128)
        ko = k_o.rearrange("(c p) t -> p c t", p=128)

        sb = lambda n, s, d=F32: es.enter_context(nc.sbuf_tensor(n, s, d))
        pt = lambda n, s, d=F32: es.enter_context(nc.psum_tensor(n, s, d))

        ones_col = sb("ones_col", [128, 1])
        ones_row = sb("ones_row", [1, 128])
        eps_row = sb("eps_row", [1, 1])
        identw = sb("identw", [128, 128])
        identb = sb("identb", [128, 128], BF16)
        Xt = sb("Xt", [128, NCH * 512])
        sq32 = sb("sq32", [128, 512])
        scr_a = sb("scr_a", [128, 512])
        scr_b = sb("scr_b", [128, 512])
        a_row = sb("a_row", [1, 512])
        b_row = sb("b_row", [1, 512])
        t_row = sb("t_row", [1, 512])
        xn = sb("xn", [128, NCH * T_CORE], BF16)
        ctq = sb("ctq", [128, NQ * T_CORE], BF16)
        ctkv = sb("ctkv", [128, NQ * T_CORE], BF16)
        w_dq = sb("w_dq", [128, NCH * DQ], BF16)
        w_dkv = sb("w_dkv", [128, NCH * DQ], BF16)
        w_uq = sb("w_uq", [128, NQ * D_MODEL], BF16)
        w_qr = sb("w_qr", [128, NQ * D_MODEL], BF16)
        w_uk = sb("w_uk", [128, NQ * D_MODEL], BF16)
        w_uv = sb("w_uv", [128, NQ * D_MODEL], BF16)
        w_kr = sb("w_kr", [128, NCH * D_MODEL], BF16)
        bdq_s = sb("bdq_s", [128, NQ])
        bdkv_s = sb("bdkv_s", [128, NQ])
        brep = {nm: sb(f"brep_{nm}", [128, D_MODEL]) for nm in
                ("uq", "qr", "kr", "uk", "uv")}
        brow_s = sb("brow_s", [1, D_MODEL])
        cs_t = sb("cs_t", [128, DQ])
        sn_t = sb("sn_t", [128, DQ])
        qtm = sb("qtm", [128, D_MODEL])
        rtm = sb("rtm", [128, D_MODEL])
        qtb = sb("qtb", [128, D_MODEL], BF16)
        vaug_tb = sb("vaug_tb", [128, 16 * 65], BF16)
        tfm = sb("tfm", [128, 128], BF16)

        ps_a = pt("ps_a", [1, 512])
        ps_b = pt("ps_b", [1, 512])
        ps_bc1 = pt("ps_bc1", [128, 512])
        ps_bc2 = pt("ps_bc2", [128, 512])
        ps_m1 = pt("ps_m1", [128, 512])
        ps_m2 = pt("ps_m2", [128, 512])
        ps_tr = pt("ps_tr", [128, 128], BF16)

        sq.step("gpsimd", lambda e: e.memset(ones_col[:], 1.0))
        sq.step("gpsimd", lambda e: e.memset(ones_row[:], 1.0))
        sq.step("gpsimd", lambda e: e.memset(eps_row[:], LN_EPS))
        sq.step("gpsimd", lambda e: e.memset(identw[:], 1.0))
        sq.step("gpsimd", lambda e: e.affine_select(
            identw[:], identw[:], pattern=[[1, 128]], compare_op=ALU.is_equal,
            fill=0.0, base=0, channel_multiplier=-1))
        sq.step("vector", lambda e: e.tensor_copy(identb[:], identw[:]))
        sq.dma(bdq_s[:], b_dq)
        sq.dma(bdkv_s[:], b_dkv)
        sq.dma(w_dq[:].rearrange("p (c d) -> p c d", c=NCH),
               wdq.rearrange("(c p) d -> p c d", p=128), engine="gpsimd")
        sq.dma(w_dkv[:].rearrange("p (c d) -> p c d", c=NCH),
               wdkv.rearrange("(c p) d -> p c d", p=128), engine="gpsimd")
        sq.dma(w_uq[:].rearrange("p (c d) -> p c d", c=NQ),
               wuq.rearrange("(c p) d -> p c d", p=128), engine="gpsimd")
        sq.dma(w_qr[:].rearrange("p (c d) -> p c d", c=NQ),
               wqr.rearrange("(c p) d -> p c d", p=128), engine="gpsimd")
        sq.dma(w_uk[:].rearrange("p (c d) -> p c d", c=NQ),
               wuk.rearrange("(c p) d -> p c d", p=128), engine="gpsimd")
        sq.dma(w_uv[:].rearrange("p (c d) -> p c d", c=NQ),
               wuv.rearrange("(c p) d -> p c d", p=128), engine="gpsimd")
        sq.dma(w_kr[:].rearrange("p (c d) -> p c d", c=NCH),
               wkr.rearrange("(c p) d -> p c d", p=128), engine="gpsimd")
        for nm in brep:
            sq.dma(brow_s[:], brows[nm])
            for n in range(2):
                sq.step("tensor", lambda e, n=n, nm=nm: e.matmul(
                    ps_bc1[:], ones_row[:], brow_s[:, n * 512:(n + 1) * 512]))
                sq.step("vector", lambda e, n=n, nm=nm: e.tensor_copy(
                    brep[nm][:, n * 512:(n + 1) * 512], ps_bc1[:]))

        for it in range(4):
            c0, w = it * 512, 512
            sq.dma(Xt[:].rearrange("p (c t) -> p c t", c=NCH), xv_[:, :, c0:c0 + w])
            sq.step("tensor", lambda e, w=w: [
                e.matmul(ps_a[:, 0:w], ones_col[:], Xt[:, c * 512:c * 512 + w],
                         start=(c == 0), stop=(c == NCH - 1)) for c in range(NCH)])
            for c in range(NCH):
                sq.step("scalar", lambda e, c=c, w=w: e.activation(
                    sq32[:, 0:w], Xt[:, c * 512:c * 512 + w], AF.Square))
                sq.step("tensor", lambda e, c=c, w=w: e.matmul(
                    ps_b[:, 0:w], ones_col[:], sq32[:, 0:w],
                    start=(c == 0), stop=(c == NCH - 1)))
            sq.step("vector", lambda e, w=w: e.tensor_single_scalar(
                b_row[:, 0:w], ps_a[:, 0:w], 1.0 / D_MODEL, ALU.mult))
            sq.step("vector", lambda e, w=w: e.tensor_single_scalar(
                a_row[:, 0:w], ps_b[:, 0:w], 1.0 / D_MODEL, ALU.mult))
            sq.step("scalar", lambda e, w=w: e.activation(
                t_row[:, 0:w], b_row[:, 0:w], AF.Square))
            sq.step("vector", lambda e, w=w: e.tensor_sub(
                a_row[:, 0:w], a_row[:, 0:w], t_row[:, 0:w]))
            sq.step("scalar", lambda e, w=w: e.activation(
                a_row[:, 0:w], a_row[:, 0:w], AF.Ln, bias=eps_row[:]))
            sq.step("scalar", lambda e, w=w: e.activation(
                a_row[:, 0:w], a_row[:, 0:w], AF.Exp, scale=-0.5))
            sq.step("tensor", lambda e, w=w: e.matmul(
                ps_bc1[:, 0:w], ones_row[:], b_row[:, 0:w]))
            sq.step("tensor", lambda e, w=w: e.matmul(
                ps_bc2[:, 0:w], ones_row[:], a_row[:, 0:w]))
            for c in range(NCH):
                sq.step("vector", lambda e, c=c, w=w: e.tensor_sub(
                    sq32[:, 0:w], Xt[:, c * 512:c * 512 + w], ps_bc1[:, 0:w]))
                sq.step("vector", lambda e, c=c, c0=c0, w=w: e.tensor_mul(
                    xn[:, c * T_CORE + c0:c * T_CORE + c0 + w], sq32[:, 0:w],
                    ps_bc2[:, 0:w]))

        for nm, wsl, bias, dst in (("q", w_dq, bdq_s, ctq), ("kv", w_dkv, bdkv_s, ctkv)):
            for dqt in range(NQ):
                for tt in range(4):
                    t0 = tt * 512
                    sq.step("tensor", lambda e, wsl=wsl, dqt=dqt, t0=t0: [
                        e.matmul(ps_m1[:], wsl[:, c * DQ + dqt * 128:
                                               c * DQ + (dqt + 1) * 128],
                                 xn[:, c * T_CORE + t0:c * T_CORE + t0 + 512],
                                 start=(c == 0), stop=(c == NCH - 1))
                        for c in range(NCH)])
                    sq.step("vector", lambda e, dst=dst, bias=bias, dqt=dqt, t0=t0:
                        e.tensor_scalar(
                            dst[:, dqt * T_CORE + t0:dqt * T_CORE + t0 + 512],
                            ps_m1[:], bias[:, dqt:dqt + 1], None, ALU.add))

        for qt in range(T_CORE // 128):
            t0 = qt * 128
            sq.dma(cs_t[:], cos_t[t0:t0 + 128, :])
            sq.dma(sn_t[:], sin_t[t0:t0 + 128, :])
            for which in ("q", "k"):
                ct = ctq if which == "q" else ctkv
                wc = w_uq if which == "q" else w_uk
                bc = brep["uq"] if which == "q" else brep["uk"]
                for n in range(2):
                    sq.step("tensor", lambda e, ct=ct, wc=wc, n=n, t0=t0: [
                        e.matmul(ps_m1[:], ct[:, c * T_CORE + t0:c * T_CORE + t0 + 128],
                                 wc[:, c * D_MODEL + n * 512:c * D_MODEL + n * 512 + 512],
                                 start=(c == 0), stop=(c == NQ - 1))
                        for c in range(NQ)])
                    sq.step("vector", lambda e, bc=bc, n=n: e.tensor_add(
                        qtm[:, n * 512:(n + 1) * 512], ps_m1[:],
                        bc[:, n * 512:(n + 1) * 512]))
                if which == "q":
                    for n in range(2):
                        sq.step("tensor", lambda e, n=n, t0=t0: [
                            e.matmul(ps_m2[:],
                                     ctq[:, c * T_CORE + t0:c * T_CORE + t0 + 128],
                                     w_qr[:, c * D_MODEL + n * 512:
                                          c * D_MODEL + n * 512 + 512],
                                     start=(c == 0), stop=(c == NQ - 1))
                            for c in range(NQ)])
                        sq.step("vector", lambda e, n=n: e.tensor_add(
                            rtm[:, n * 512:(n + 1) * 512], ps_m2[:],
                            brep["qr"][:, n * 512:(n + 1) * 512]))
                else:
                    for n in range(2):
                        sq.step("tensor", lambda e, n=n, t0=t0: [
                            e.matmul(ps_m2[:],
                                     xn[:, c * T_CORE + t0:c * T_CORE + t0 + 128],
                                     w_kr[:, c * D_MODEL + n * 512:
                                          c * D_MODEL + n * 512 + 512],
                                     start=(c == 0), stop=(c == NCH - 1))
                            for c in range(NCH)])
                        sq.step("vector", lambda e, n=n: e.tensor_add(
                            rtm[:, n * 512:(n + 1) * 512], ps_m2[:],
                            brep["kr"][:, n * 512:(n + 1) * 512]))
                re = rtm[:].rearrange("p (i two) -> p two i", two=2)[:, 0, :]
                ro = rtm[:].rearrange("p (i two) -> p two i", two=2)[:, 1, :]
                qe = qtm[:].rearrange("p (i two) -> p two i", two=2)[:, 0, :]
                qo_ = qtm[:].rearrange("p (i two) -> p two i", two=2)[:, 1, :]
                sq.step("vector", lambda e, re=re: e.tensor_mul(scr_a[:], re, cs_t[:]))
                sq.step("vector", lambda e, ro=ro: e.tensor_mul(scr_b[:], ro, sn_t[:]))
                sq.step("vector", lambda e: e.tensor_sub(scr_a[:], scr_a[:], scr_b[:]))
                sq.step("vector", lambda e, qe=qe: e.tensor_add(qe, qe, scr_a[:]))
                sq.step("vector", lambda e, ro=ro: e.tensor_mul(scr_a[:], ro, cs_t[:]))
                sq.step("vector", lambda e, re=re: e.tensor_mul(scr_b[:], re, sn_t[:]))
                sq.step("vector", lambda e: e.tensor_add(scr_a[:], scr_a[:], scr_b[:]))
                sq.step("vector", lambda e, qo_=qo_: e.tensor_add(qo_, qo_, scr_a[:]))
                sq.step("vector", lambda e: e.tensor_copy(qtb[:], qtm[:]))
                dst = qo if which == "q" else ko
                for c in range(NCH):
                    sq.step("tensor", lambda e, c=c: e.transpose(
                        ps_tr[:], qtb[:, c * 128:(c + 1) * 128], identb[:]))
                    sq.step("vector", lambda e: e.tensor_copy(tfm[:], ps_tr[:]))
                    sq.dma(dst[:, c, t0:t0 + 128], tfm[:])
            if qt == 0:
                sq.step("gpsimd", lambda e: e.memset(vaug_tb[:], 1.0))
            for n in range(2):
                sq.step("tensor", lambda e, n=n, t0=t0: [
                    e.matmul(ps_m1[:], ctkv[:, c * T_CORE + t0:c * T_CORE + t0 + 128],
                             w_uv[:, c * D_MODEL + n * 512:c * D_MODEL + n * 512 + 512],
                             start=(c == 0), stop=(c == NQ - 1))
                    for c in range(NQ)])
                for hh in range(8):
                    g = n * 8 + hh
                    sq.step("vector", lambda e, n=n, hh=hh, g=g: e.tensor_add(
                        vaug_tb[:, g * 65:g * 65 + 64], ps_m1[:, hh * 64:(hh + 1) * 64],
                        brep["uv"][:, g * 64:(g + 1) * 64]))
            sq.dma(v_o[t0:t0 + 128, :], vaug_tb[:])

    if emit_only:
        return prog
    return seq_program(prog)


def build_mla_attn(io=None, emit_only=False):
    S = 4096
    NKT = S // 128
    NH = 16

    def prog(nc, sq, es):
        q_i = _dram(nc, io, "q_i", [D_MODEL, T_CORE], BF16, "ExternalInput")
        k_i = _dram(nc, io, "k_i", [D_MODEL, S], BF16, "ExternalInput")
        v_i = _dram(nc, io, "v_i", [S, D_MODEL], BF16, "ExternalInput")
        x_fm = _dram(nc, io, "x_fm", [D_MODEL, T_CORE], F32, "ExternalInput")
        wo = _dram(nc, io, "wo", [D_MODEL, D_MODEL], F32, "ExternalInput")
        b_o = _dram(nc, io, "b_o", [128, NCH], F32, "ExternalInput")
        x_out = _dram(nc, io, "x_out", [D_MODEL, T_CORE], F32, "ExternalOutput")

        qv = q_i.rearrange("(c p) t -> p c t", p=128)
        kv_ = k_i.rearrange("(c p) t -> p c t", p=128)
        vv = v_i.rearrange("(kt p) d -> p kt d", p=128)
        xv_ = x_fm.rearrange("(c p) t -> p c t", p=128)
        xo = x_out.rearrange("(c p) t -> p c t", p=128)

        sb = lambda n, s, d=F32: es.enter_context(nc.sbuf_tensor(n, s, d))
        pt = lambda n, s, d=F32: es.enter_context(nc.psum_tensor(n, s, d))

        ones_col = sb("ones_col", [128, 1])
        onesb_col = sb("onesb_col", [128, 1], BF16)
        ones_row = sb("ones_row", [1, 128])
        identw = sb("identw", [128, 128])
        identb = sb("identb", [128, 128], BF16)
        q_sb = sb("q_sb", [128, NCH * T_CORE], BF16)
        k_sb = sb("k_sb", [128, NCH * S], BF16)
        v_sb = sb("v_sb", [128, NKT * D_MODEL], BF16)
        wo_sb = sb("wo_sb", [128, NCH * D_MODEL], BF16)
        bo_sb = sb("bo_sb", [128, NCH])
        Ktil = sb("Ktil", [128, S], BF16)
        Qtil = sb("Qtil", [128, 128], BF16)
        mm8 = sb("mm8", [128, 8])
        m_c = sb("m_c", [128, 1])
        P_T = sb("P_T", [128, 128], BF16)
        rs_row = sb("rs_row", [1, 128])
        rc_col = sb("rc_col", [128, 1])
        out_tm = sb("out_tm", [128, NH * 64])
        otb = sb("otb", [128, NH * 64], BF16)
        ofm = sb("ofm", [128, NCH * 128], BF16)
        Xt = sb("Xt", [128, NCH * 128])
        oscr = sb("oscr", [128, 128])
        tfm = sb("tfm", [128, 128], BF16)

        ps_sc = pt("ps_sc", [128, 512])
        ps_st = pt("ps_st", [128, 128])
        ps_o = pt("ps_o", [128, 64])
        ps_rs = pt("ps_rs", [1, 128])
        ps_tr = pt("ps_tr", [128, 128], BF16)
        ps_r1 = pt("ps_r1", [1, 128])
        ps_out = pt("ps_out", [128, 128])

        sq.step("gpsimd", lambda e: e.memset(ones_col[:], 1.0))
        sq.step("gpsimd", lambda e: e.memset(onesb_col[:], 1.0))
        sq.step("gpsimd", lambda e: e.memset(ones_row[:], 1.0))
        sq.step("gpsimd", lambda e: e.memset(identw[:], 1.0))
        sq.step("gpsimd", lambda e: e.affine_select(
            identw[:], identw[:], pattern=[[1, 128]], compare_op=ALU.is_equal,
            fill=0.0, base=0, channel_multiplier=-1))
        sq.step("vector", lambda e: e.tensor_copy(identb[:], identw[:]))
        sq.dma(q_sb[:].rearrange("p (c t) -> p c t", c=NCH), qv)
        sq.dma(k_sb[:].rearrange("p (c t) -> p c t", c=NCH), kv_)
        sq.dma(v_sb[:].rearrange("p (kt d) -> p kt d", kt=NKT), vv)
        sq.dma(wo_sb[:].rearrange("p (c d) -> p c d", c=NCH),
               wo.rearrange("(c p) d -> p c d", p=128), engine="gpsimd")
        sq.dma(bo_sb[:], b_o)

        for qt in range(T_CORE // 128):
            t0 = qt * 128
            for h in range(NH):
                hb = 64 * (h % 2)
                hc = h // 2
                sq.dma(Ktil[0:64, :], k_sb[hb:hb + 64, hc * S:(hc + 1) * S],
                       engine="gpsimd")
                sq.step("gpsimd", lambda e: e.memset(Ktil[64:65, 0:S], 1.0))
                for kt8 in range(8):
                    sq.step("tensor", lambda e, h=h, hb=hb, hc=hc, t0=t0, kt8=kt8:
                        e.matmul(ps_sc[:],
                                 q_sb[hb:hb + 64, hc * T_CORE + t0:hc * T_CORE + t0 + 128],
                                 k_sb[hb:hb + 64, hc * S + kt8 * 512:hc * S + (kt8 + 1) * 512],
                                 start=True, stop=True))
                    sq.step("vector", lambda e, kt8=kt8: e.reduce_max(
                        mm8[:, kt8:kt8 + 1], ps_sc[:], axis=AX.X))
                sq.step("vector", lambda e: e.reduce_max(m_c[:], mm8[:], axis=AX.X))
                sq.step("vector", lambda e: e.tensor_single_scalar(
                    m_c[:], m_c[:], -1.0, ALU.mult))
                sq.dma(Qtil[0:64, :],
                       q_sb[hb:hb + 64, hc * T_CORE + t0:hc * T_CORE + t0 + 128],
                       engine="gpsimd")
                sq.step("tensor", lambda e: e.transpose(
                    ps_r1[:], m_c[:], identw[:]))
                sq.step("vector", lambda e: e.tensor_copy(
                    Qtil[64:65, :], ps_r1[:]))
                for kt in range(NKT):
                    sq.step("tensor", lambda e, kt=kt: e.matmul(
                        ps_st[:], Ktil[0:65, kt * 128:(kt + 1) * 128],
                        Qtil[0:65, :], start=True, stop=True))
                    sq.step("scalar", lambda e: e.activation(
                        P_T[:], ps_st[:], AF.Exp, scale=ATT_SCALE))
                    sq.step("tensor", lambda e, kt=kt: e.matmul(
                        ps_rs[:], onesb_col[:], P_T[:],
                        start=(kt == 0), stop=(kt == NKT - 1)))
                    sq.step("tensor", lambda e, h=h, kt=kt: e.matmul(
                        ps_o[:], P_T[:], v_sb[:, kt * D_MODEL + h * 64:
                                              kt * D_MODEL + (h + 1) * 64],
                        start=(kt == 0), stop=(kt == NKT - 1)))
                sq.step("vector", lambda e: e.reciprocal(rs_row[:], ps_rs[:]))
                sq.step("tensor", lambda e: e.transpose(
                    ps_st[0:128, 0:1], rs_row[:], identw[0:1, 0:1]))
                sq.step("vector", lambda e: e.tensor_copy(
                    rc_col[:], ps_st[0:128, 0:1]))
                sq.step("vector", lambda e, h=h: e.tensor_scalar(
                    out_tm[:, h * 64:(h + 1) * 64], ps_o[:], rc_col[:],
                    None, ALU.mult))
                if h == NH - 1:
                    sq.step("vector", lambda e: e.tensor_copy(otb[:], out_tm[:]))
                    for c in range(NCH):
                        sq.step("tensor", lambda e, c=c: e.transpose(
                            ps_tr[:], otb[:, c * 128:(c + 1) * 128], identb[:]))
                        sq.step("vector", lambda e, c=c: e.tensor_copy(
                            ofm[:, c * 128:(c + 1) * 128], ps_tr[:]))
                    sq.dma(Xt[:].rearrange("p (c t) -> p c t", c=NCH),
                           xv_[:, :, t0:t0 + 128])
                    for d in range(NCH):
                        sq.step("tensor", lambda e, d=d: [
                            e.matmul(ps_out[:], wo_sb[:, c * D_MODEL + d * 128:
                                                      c * D_MODEL + (d + 1) * 128],
                                     ofm[:, c * 128:(c + 1) * 128],
                                     start=(c == 0), stop=(c == NCH - 1))
                            for c in range(NCH)])
                        sq.step("vector", lambda e, d=d: e.tensor_scalar(
                            oscr[:], ps_out[:], bo_sb[:, d:d + 1], None, ALU.add))
                        sq.step("vector", lambda e, d=d: e.tensor_add(
                            oscr[:], oscr[:], Xt[:, d * 128:(d + 1) * 128]))
                        sq.dma(xo[:, d, t0:t0 + 128], oscr[:])

    if emit_only:
        return prog
    return seq_program(prog)


def build_mla_attn2(io=None, emit_only=False):
    S = 4096
    NKT = 32
    NH = 16
    NH65 = NH * 65

    def prog(nc, sq, es):
        q_i = _dram(nc, io, "q_i", [D_MODEL, T_CORE], BF16, "ExternalInput")
        k_i = _dram(nc, io, "k_i", [2 * D_MODEL, T_CORE], BF16, "ExternalInput")
        vaug_i = _dram(nc, io, "vaug_i", [S, NH65], BF16, "ExternalInput")
        x_fm = _dram(nc, io, "x_fm", [D_MODEL, T_CORE], F32, "ExternalInput")
        wo = _dram(nc, io, "wo", [D_MODEL, D_MODEL], F32, "ExternalInput")
        b_o = _dram(nc, io, "b_o", [128, NCH], F32, "ExternalInput")
        x_out = _dram(nc, io, "x_out", [D_MODEL, T_CORE], F32, "ExternalOutput")

        qv = q_i.rearrange("(c p) t -> p c t", p=128)
        kagv = k_i.rearrange("(r c p) t -> r p c t", r=2, p=128)
        vv = vaug_i.rearrange("(kt p) d -> p kt d", p=128)
        xv_ = x_fm.rearrange("(c p) t -> p c t", p=128)
        xo = x_out.rearrange("(c p) t -> p c t", p=128)

        sb = lambda n, s, d=F32: es.enter_context(nc.sbuf_tensor(n, s, d))
        pt = lambda n, s, d=F32: es.enter_context(nc.psum_tensor(n, s, d))

        ones_col = sb("ones_col", [128, 1])
        ones_r64 = sb("ones_r64", [1, 64])
        vaug = sb("vaug", [128, NKT * NH65], BF16)
        wo_sb = sb("wo_sb", [128, NCH * D_MODEL], BF16)
        bo_sb = sb("bo_sb", [128, NCH])
        Ktil = sb("Ktil", [65, 2 * S], BF16)
        Qtil = sb("Qtil", [65, 2 * T_CORE], BF16)
        P_T = sb("P_T", [128, 2 * 512], BF16)
        ksq = sb("ksq", [128, 512])
        ktl_b = sb("ktl_b", [128, 512], BF16)
        kt8 = sb("kt8", [1, 16])
        negkm = sb("negkm", [1, NH])
        qn_row = sb("qn_row", [1, 512])
        rrow = sb("rrow", [1, 512])
        rbc = sb("rbc", [64, 512])
        odd_t = sb("odd_t", [64, 1024], BF16)
        ofm = sb("ofm", [128, NCH * T_CORE], BF16)
        Xt = sb("Xt", [128, NCH * 512])
        oscr = sb("oscr", [128, 512])

        ps_sc = pt("ps_sc", [128, 1024])
        ps_av = pt("ps_av", [65, 1024])
        ps_bc = pt("ps_bc", [64, 512])
        ps_q1 = pt("ps_q1", [1, 512])
        ps_out = pt("ps_out", [128, 1024])

        sq.step("gpsimd", lambda e: e.memset(ones_col[:], 1.0))
        sq.step("gpsimd", lambda e: e.memset(ones_r64[:], 1.0))
        t_init = sq.step("gpsimd", lambda e: e.memset(Ktil[64:65, :], 1.0))
        t_v = sq.dma(vaug[:].rearrange("p (kt d) -> p kt d", kt=NKT), vv,
                     engine="gpsimd")
        t_wo = sq.dma(wo_sb[:].rearrange("p (c d) -> p c d", c=NCH),
                      wo.rearrange("(c p) d -> p c d", p=128), engine="gpsimd")
        t_bo = sq.dma(bo_sb[:], b_o)

        t_p = t_init
        for hc in range(NCH):
            for tile in range(8):
                t_p = sq.dma(ktl_b[:, 0:512],
                             kagv[tile // 4, :, hc, (tile % 4) * 512:(tile % 4) * 512 + 512],
                             wait=t_p)
                t_p = sq.step("scalar", lambda e: e.activation(
                    ksq[:], ktl_b[:], AF.Square), wait=t_p)
                t_p = sq.step("tensor", lambda e: e.matmul(
                    ps_q1[:], ones_col[0:64, :], ksq[0:64, :],
                    start=True, stop=True), wait=t_p)
                t_p = sq.step("vector", lambda e, tile=tile: e.reduce_max(
                    kt8[:, tile:tile + 1], ps_q1[:], axis=AX.X), wait=t_p)
                t_p = sq.step("tensor", lambda e: e.matmul(
                    ps_q1[:], ones_col[64:128, :], ksq[64:128, :],
                    start=True, stop=True), wait=t_p)
                t_p = sq.step("vector", lambda e, tile=tile: e.reduce_max(
                    kt8[:, 8 + tile:8 + tile + 1], ps_q1[:], axis=AX.X), wait=t_p)
            for par in range(2):
                h = 2 * hc + par
                t_p = sq.step("vector", lambda e, par=par, h=h: e.reduce_max(
                    negkm[:, h:h + 1], kt8[:, par * 8:(par + 1) * 8], axis=AX.X),
                    wait=t_p)
        t_p = sq.step("scalar", lambda e: e.activation(
            negkm[:], negkm[:], AF.Sqrt), wait=t_p)
        t_km = sq.step("vector", lambda e: e.tensor_single_scalar(
            negkm[:], negkm[:], -1.0, ALU.mult), wait=t_p)

        tok_kload = [None] * (NH + 2)
        tok_headS = [None] * (NH + 2)
        tok_exp = {}
        tok_norm = [None, None]
        tok_bcmul = None
        t_odd_dma = [None, None]

        lh_chain = [t_km]

        def load_head(h):
            par = h % 2
            hc, hb = h // 2, 64 * (h % 2)
            w1 = tmax(tok_headS[h - 2] if h >= 2 else None, t_km, t_init)
            tk0 = sq.dma(Ktil[0:64, par * S:par * S + 2048],
                         kagv[0, hb:hb + 64, hc, :], engine="gpsimd", wait=w1)
            tk = sq.dma(Ktil[0:64, par * S + 2048:par * S + 4096],
                        kagv[1, hb:hb + 64, hc, :], engine="gpsimd")
            tq = sq.dma(Qtil[0:64, par * T_CORE:(par + 1) * T_CORE],
                        qv[hb:hb + 64, hc, :], engine="gpsimd", wait=w1)
            t_p = tmax(tq, lh_chain[0])
            for tile in range(4):
                qs = par * T_CORE + tile * 512
                t_p = sq.step("scalar", lambda e, qs=qs: e.activation(
                    ksq[0:64, 0:512], Qtil[0:64, qs:qs + 512], AF.Square), wait=t_p)
                t_p = sq.step("tensor", lambda e: e.matmul(
                    ps_q1[:], ones_col[0:64, :], ksq[0:64, :],
                    start=True, stop=True), wait=t_p)
                t_p = sq.step("scalar", lambda e: e.activation(
                    qn_row[:], ps_q1[:], AF.Sqrt), wait=t_p)
                t_p = sq.step("vector", lambda e, qs=qs, h=h: e.tensor_scalar(
                    Qtil[64:65, qs:qs + 512], qn_row[:], negkm[:, h:h + 1],
                    None, ALU.mult), wait=t_p)
            lh_chain[0] = t_p
            tok_kload[h] = tmax(tk, t_p)

        load_head(0)
        load_head(1)
        for h in range(NH):
            par = h % 2
            hc, hb = h // 2, 64 * (h % 2)
            for qt in range(4):
                pv = (h * 4 + qt) % 2
                qs = par * T_CORE + qt * 512
                t_S = None
                for kt in range(NKT):
                    sp = kt % 2
                    w = tmax(tok_kload[h], tok_exp.get((h, qt, kt - 2), None),
                             t_v if kt == 0 else None,
                             tok_norm[pv] if kt == 0 else None)
                    t_S = sq.step("tensor", lambda e, par=par, kt=kt, sp=sp, qs=qs:
                        e.matmul(ps_sc[:, sp * 512:sp * 512 + 512],
                                 Ktil[0:65, par * S + kt * 128:par * S + kt * 128 + 128],
                                 Qtil[0:65, qs:qs + 512], start=True, stop=True),
                        wait=w)
                    t_E = sq.step("scalar", lambda e, sp=sp: e.activation(
                        P_T[:, sp * 512:sp * 512 + 512],
                        ps_sc[:, sp * 512:sp * 512 + 512], AF.Exp,
                        scale=ATT_SCALE), wait=t_S)
                    tok_exp[(h, qt, kt)] = t_E
                    t_AV = sq.step("tensor", lambda e, h=h, kt=kt, sp=sp, pv=pv:
                        e.matmul(ps_av[0:65, pv * 512:pv * 512 + 512],
                                 vaug[:, kt * NH65 + h * 65:kt * NH65 + h * 65 + 65],
                                 P_T[:, sp * 512:sp * 512 + 512],
                                 start=(kt == 0), stop=(kt == NKT - 1)),
                        wait=t_E)
                tok_headS[h] = t_S
                t_r = sq.step("vector", lambda e, pv=pv: e.reciprocal(
                    rrow[:], ps_av[64:65, pv * 512:pv * 512 + 512]), wait=t_AV)
                t_b = sq.step("tensor", lambda e: e.matmul(
                    ps_bc[:], ones_r64[:], rrow[:], start=True, stop=True),
                    wait=tmax(t_r, tok_bcmul))
                t_c = sq.step("vector", lambda e: e.tensor_copy(rbc[:], ps_bc[:]),
                              wait=t_b)
                if hb == 0:
                    tok_bcmul = sq.step("vector",
                        lambda e, hc=hc, qt=qt, pv=pv: e.tensor_mul(
                            ofm[0:64, hc * T_CORE + qt * 512:hc * T_CORE + qt * 512 + 512],
                            ps_av[0:64, pv * 512:pv * 512 + 512], rbc[:]), wait=t_c)
                else:
                    tok_bcmul = sq.step("vector",
                        lambda e, pv=pv: e.tensor_mul(
                            odd_t[:, pv * 512:pv * 512 + 512],
                            ps_av[0:64, pv * 512:pv * 512 + 512], rbc[:]),
                        wait=tmax(t_c, t_odd_dma[pv]))
                    t_odd_dma[pv] = sq.dma(
                        ofm[64:128, hc * T_CORE + qt * 512:hc * T_CORE + qt * 512 + 512],
                        odd_t[:, pv * 512:pv * 512 + 512], wait=tok_bcmul)
                tok_norm[pv] = sq.barrier()
            if h + 2 < NH:
                load_head(h + 2)

        t_all = sq.barrier()
        t_evac = [None, None]
        t_lastadd = None
        t_store = None
        for qt in range(4):
            t0 = qt * 512
            tx = sq.dma(Xt[:].rearrange("p (c t) -> p c t", c=NCH),
                        xv_[:, :, t0:t0 + 512], wait=tmax(t_all, t_lastadd))
            for d in range(NCH):
                po = (qt * NCH + d) % 2
                t_mm = sq.step("tensor", lambda e, d=d, t0=t0, po=po: [
                    e.matmul(ps_out[:, po * 512:po * 512 + 512],
                             wo_sb[:, c * D_MODEL + d * 128:c * D_MODEL + (d + 1) * 128],
                             ofm[:, c * T_CORE + t0:c * T_CORE + t0 + 512],
                             start=(c == 0), stop=(c == NCH - 1))
                    for c in range(NCH)], wait=tmax(t_all, t_wo, t_evac[po]))
                t_evac[po] = sq.step("vector", lambda e, d=d, po=po: e.tensor_scalar(
                    oscr[:], ps_out[:, po * 512:po * 512 + 512],
                    bo_sb[:, d:d + 1], None, ALU.add),
                    wait=tmax(t_mm, tx, t_bo, t_store))
                t_lastadd = sq.step("vector", lambda e, d=d: e.tensor_add(
                    oscr[:], oscr[:], Xt[:, d * 512:(d + 1) * 512]))
                t_store = sq.dma(xo[:, d, t0:t0 + 512], oscr[:], wait=t_lastadd)

    if emit_only:
        return prog
    return seq_program(prog)


def host_mla_qkv_inputs(x, W, core):
    ids = np.arange(512, dtype=np.float64)
    theta = (10000.0 ** (-2.0 * ids)).astype(np.float32) / 512.0
    b, h = core // 2, core % 2
    pos = np.arange(h * 2048, (h + 1) * 2048, dtype=np.float32)
    ang = pos[:, None] * theta[None, :]
    return {
        "x_fm": fmT(core_slice_tokens(x, core)),
        "wdq": np.ascontiguousarray(W["Wd_q_w"]),
        "wdkv": np.ascontiguousarray(W["Wd_kv_w"]),
        "wuq": np.ascontiguousarray(W["Wu_q_w"]),
        "wqr": np.ascontiguousarray(W["Wq_r_w"]),
        "wuk": np.ascontiguousarray(W["Wu_k_w"]),
        "wuv": np.ascontiguousarray(W["Wu_v_w"]),
        "wkr": np.ascontiguousarray(W["Wk_r_w"]),
        "b_dq": pack_vec(W["Wd_q_b"]),
        "b_dkv": pack_vec(W["Wd_kv_b"]),
        "brow_uq": np.asarray(W["Wu_q_b"], np.float32)[None, :],
        "brow_qr": np.asarray(W["Wq_r_b"], np.float32)[None, :],
        "brow_kr": np.asarray(W["Wk_r_b"], np.float32)[None, :],
        "brow_uk": np.asarray(W["Wu_k_b"], np.float32)[None, :],
        "brow_uv": np.asarray(W["Wu_v_b"], np.float32)[None, :],
        "cos_t": np.cos(ang).astype(np.float32),
        "sin_t": np.sin(ang).astype(np.float32),
    }


def host_mla_attn_inputs(x, qkv_outs, W, core):
    b, h = core // 2, core % 2
    k_full = np.concatenate([qkv_outs[2 * b]["k_o"], qkv_outs[2 * b + 1]["k_o"]],
                            axis=1)
    v_full = np.concatenate([qkv_outs[2 * b]["v_o"], qkv_outs[2 * b + 1]["v_o"]],
                            axis=0)
    return {
        "q_i": qkv_outs[core]["q_o"],
        "k_i": np.ascontiguousarray(k_full),
        "vaug_i": np.ascontiguousarray(v_full),
        "x_fm": fmT(core_slice_tokens(x, core)),
        "wo": np.ascontiguousarray(W["Wo_w"]),
        "b_o": pack_vec(W["Wo_b"]),
    }



D_CKV = 256
D_CQ = 512
TOPK_NH = 8
D_HEAD = 64
D_ROPE = 32
K_TS = 96
LW = 64
ATT2_SCALE = float((D_HEAD + D_ROPE) ** -0.5)
PADK = 63


def build_topk_proj(io=None, emit_only=False):
    def prog(nc, sq, es):
        x_fm = _dram(nc, io, "x_fm", [D_MODEL, T_CORE], F32, "ExternalInput")
        wdkv = _dram(nc, io, "wdkv", [D_MODEL, D_CKV], F32, "ExternalInput")
        wuk = _dram(nc, io, "wuk", [D_CKV, 512], F32, "ExternalInput")
        wuv = _dram(nc, io, "wuv", [D_CKV, 512], F32, "ExternalInput")
        wdq = _dram(nc, io, "wdq", [D_MODEL, D_CQ], F32, "ExternalInput")
        wuq = _dram(nc, io, "wuq", [D_CQ, 512], F32, "ExternalInput")
        wqr = _dram(nc, io, "wqr", [D_CQ, 256], F32, "ExternalInput")
        wkr = _dram(nc, io, "wkr", [D_MODEL, 32], F32, "ExternalInput")
        wiq = _dram(nc, io, "wiq", [D_MODEL, 256], F32, "ExternalInput")
        wik = _dram(nc, io, "wik", [D_MODEL, 64], F32, "ExternalInput")
        q_c_tm = _dram(nc, io, "q_c_tm", [T_CORE, 512], BF16, "ExternalOutput")
        qr_tm = _dram(nc, io, "qr_tm", [T_CORE, 256], F32, "ExternalOutput")
        kv_tm = _dram(nc, io, "kv_tm", [T_CORE, 512], BF16, "ExternalOutput")
        v_tm = _dram(nc, io, "v_tm", [T_CORE, 512], BF16, "ExternalOutput")
        mu_tm = _dram(nc, io, "mu_tm", [T_CORE, 32], BF16, "ExternalOutput")
        q_c_fm = _dram(nc, io, "q_c_fm", [512, T_CORE], BF16, "ExternalOutput")
        q_i_fm = _dram(nc, io, "q_i_fm", [256, T_CORE], BF16, "ExternalOutput")
        ki_fm = _dram(nc, io, "ki_fm", [64, T_CORE], BF16, "ExternalOutput")
        kv_fm = _dram(nc, io, "kv_fm", [512, T_CORE], BF16, "ExternalOutput")
        mu_fm = _dram(nc, io, "mu_fm", [32, T_CORE], BF16, "ExternalOutput")

        xv_ = x_fm.rearrange("(c p) t -> p c t", p=128)

        sb = lambda n, s, d=F32: es.enter_context(nc.sbuf_tensor(n, s, d))
        pt = lambda n, s, d=F32: es.enter_context(nc.psum_tensor(n, s, d))

        ones_col = sb("ones_col", [128, 1])
        ones_row = sb("ones_row", [1, 128])
        eps_row = sb("eps_row", [1, 1])
        Xt = sb("Xt", [128, NCH * 512])
        sq32 = sb("sq32", [128, 512])
        a_row = sb("a_row", [1, 512])
        b_row = sb("b_row", [1, 512])
        t_row = sb("t_row", [1, 512])
        xn = sb("xn", [128, NCH * T_CORE], BF16)
        ctq = sb("ctq", [128, 4 * T_CORE], BF16)
        ctkv = sb("ctkv", [128, 2 * T_CORE], BF16)
        w_a = sb("w_a", [128, NCH * 512], BF16)
        w_b = sb("w_b", [128, NCH * 512], BF16)
        otm = sb("otm", [128, 512])
        otb = sb("otb", [128, 512], BF16)

        ps_a = pt("ps_a", [1, 512])
        ps_b = pt("ps_b", [1, 512])
        ps_bc1 = pt("ps_bc1", [128, 512])
        ps_bc2 = pt("ps_bc2", [128, 512])
        ps_m1 = pt("ps_m1", [128, 512])

        sq.step("gpsimd", lambda e: e.memset(ones_col[:], 1.0))
        sq.step("gpsimd", lambda e: e.memset(ones_row[:], 1.0))
        sq.step("gpsimd", lambda e: e.memset(eps_row[:], LN_EPS))

        for it in range(4):
            c0, w = it * 512, 512
            sq.dma(Xt[:].rearrange("p (c t) -> p c t", c=NCH), xv_[:, :, c0:c0 + w])
            sq.step("tensor", lambda e, w=w: [
                e.matmul(ps_a[:, 0:w], ones_col[:], Xt[:, c * 512:c * 512 + w],
                         start=(c == 0), stop=(c == NCH - 1)) for c in range(NCH)])
            for c in range(NCH):
                sq.step("scalar", lambda e, c=c, w=w: e.activation(
                    sq32[:, 0:w], Xt[:, c * 512:c * 512 + w], AF.Square))
                sq.step("tensor", lambda e, c=c, w=w: e.matmul(
                    ps_b[:, 0:w], ones_col[:], sq32[:, 0:w],
                    start=(c == 0), stop=(c == NCH - 1)))
            sq.step("vector", lambda e, w=w: e.tensor_single_scalar(
                b_row[:, 0:w], ps_a[:, 0:w], 1.0 / D_MODEL, ALU.mult))
            sq.step("vector", lambda e, w=w: e.tensor_single_scalar(
                a_row[:, 0:w], ps_b[:, 0:w], 1.0 / D_MODEL, ALU.mult))
            sq.step("scalar", lambda e, w=w: e.activation(
                t_row[:, 0:w], b_row[:, 0:w], AF.Square))
            sq.step("vector", lambda e, w=w: e.tensor_sub(
                a_row[:, 0:w], a_row[:, 0:w], t_row[:, 0:w]))
            sq.step("scalar", lambda e, w=w: e.activation(
                a_row[:, 0:w], a_row[:, 0:w], AF.Ln, bias=eps_row[:]))
            sq.step("scalar", lambda e, w=w: e.activation(
                a_row[:, 0:w], a_row[:, 0:w], AF.Exp, scale=-0.5))
            sq.step("tensor", lambda e, w=w: e.matmul(
                ps_bc1[:, 0:w], ones_row[:], b_row[:, 0:w]))
            sq.step("tensor", lambda e, w=w: e.matmul(
                ps_bc2[:, 0:w], ones_row[:], a_row[:, 0:w]))
            for c in range(NCH):
                sq.step("vector", lambda e, c=c, w=w: e.tensor_sub(
                    sq32[:, 0:w], Xt[:, c * 512:c * 512 + w], ps_bc1[:, 0:w]))
                sq.step("vector", lambda e, c=c, c0=c0, w=w: e.tensor_mul(
                    xn[:, c * T_CORE + c0:c * T_CORE + c0 + w], sq32[:, 0:w],
                    ps_bc2[:, 0:w]))

        for nm, wsrc, nd, dst in (("q", wdq, 4, ctq), ("kv", wdkv, 2, ctkv)):
            DOUT = nd * 128
            sq.dma(w_a[:, 0:NCH * DOUT].rearrange("p (c d) -> p c d", c=NCH),
                   wsrc.rearrange("(c p) d -> p c d", p=128), engine="gpsimd")
            for dt in range(nd):
                for tt in range(4):
                    t0 = tt * 512
                    sq.step("tensor", lambda e, dt=dt, t0=t0, DOUT=DOUT: [
                        e.matmul(ps_m1[:], w_a[:, c * DOUT + dt * 128:
                                               c * DOUT + (dt + 1) * 128],
                                 xn[:, c * T_CORE + t0:c * T_CORE + t0 + 512],
                                 start=(c == 0), stop=(c == NCH - 1))
                        for c in range(NCH)])
                    sq.step("vector", lambda e, dst=dst, dt=dt, t0=t0: e.tensor_copy(
                        dst[:, dt * T_CORE + t0:dt * T_CORE + t0 + 512], ps_m1[:]))

        def tm_proj(src, src_nc, wsrc, wid, out_t, post, wslot):
            sq.dma(wslot[:, 0:src_nc * wid].rearrange("p (c d) -> p c d", c=src_nc),
                   wsrc.rearrange("(c p) d -> p c d", p=128), engine="gpsimd")
            for qt in range(T_CORE // 128):
                t0 = qt * 128
                nn = (wid + 511) // 512
                for n in range(nn):
                    w = min(512, wid - n * 512)
                    sq.step("tensor", lambda e, src=src, wslot=wslot, src_nc=src_nc,
                            wid=wid, t0=t0, n=n, w=w: [
                        e.matmul(ps_m1[:, 0:w],
                                 src[:, c * T_CORE + t0:c * T_CORE + t0 + 128],
                                 wslot[:, c * wid + n * 512:c * wid + n * 512 + w],
                                 start=(c == 0), stop=(c == src_nc - 1))
                        for c in range(src_nc)])
                    post(t0, n, w)

        def mk_post_bf16(out_t):
            def post(t0, n, w):
                sq.step("vector", lambda e, t0=t0, n=n, w=w: e.tensor_copy(
                    otb[:, 0:w], ps_m1[:, 0:w]))
                sq.dma(out_t[t0:t0 + 128, n * 512:n * 512 + w], otb[:, 0:w])
            return post

        tm_proj(ctq, 4, wuq, 512, q_c_tm, mk_post_bf16(q_c_tm), w_b)

        def post_qr(t0, n, w):
            sq.step("vector", lambda e, w=w: e.tensor_copy(otm[:, 0:w], ps_m1[:, 0:w]))
            sq.dma(qr_tm[t0:t0 + 128, 0:w], otm[:, 0:w])
        tm_proj(ctq, 4, wqr, 256, qr_tm, post_qr, w_b)

        tm_proj(ctkv, 2, wuk, 512, kv_tm, mk_post_bf16(kv_tm), w_b)
        tm_proj(ctkv, 2, wuv, 512, v_tm, mk_post_bf16(v_tm), w_b)

        def post_mu(t0, n, w):
            sq.step("scalar", lambda e, w=w: e.activation(
                otm[:, 0:w], ps_m1[:, 0:w], AF.Abs))
            sq.step("scalar", lambda e, w=w: e.activation(
                otm[:, 0:w], otm[:, 0:w], AF.Exp, scale=-1.0))
            sq.step("scalar", lambda e, w=w: e.activation(
                otm[:, 0:w], otm[:, 0:w], AF.Ln, bias=1.0))
            sq.step("vector", lambda e, w=w: e.tensor_single_scalar(
                sq32[:, 0:w], ps_m1[:, 0:w], 0.0, ALU.max))
            sq.step("vector", lambda e, w=w: e.tensor_add(
                otm[:, 0:w], otm[:, 0:w], sq32[:, 0:w]))
            sq.dma(mu_tm[t0:t0 + 128, 0:w], otm[:, 0:w], engine="gpsimd")
        tm_proj(xn, NCH, wkr, 32, mu_tm, post_mu, w_b)

        def fm_proj(src, src_nc, wsrc, dout, out_t, softplus=False):
            sq.dma(w_b[:, 0:src_nc * dout].rearrange("p (c d) -> p c d", c=src_nc),
                   wsrc.rearrange("(c p) d -> p c d", p=128), engine="gpsimd")
            nrow = min(dout, 128)
            for dt in range(max(1, dout // 128)):
                for tt in range(4):
                    t0 = tt * 512
                    sq.step("tensor", lambda e, dt=dt, t0=t0, src=src,
                            src_nc=src_nc, dout=dout, nrow=nrow: [
                        e.matmul(ps_m1[0:nrow, :],
                                 w_b[:, c * dout + dt * 128:c * dout + dt * 128 + nrow],
                                 src[:, c * T_CORE + t0:c * T_CORE + t0 + 512],
                                 start=(c == 0), stop=(c == src_nc - 1))
                        for c in range(src_nc)])
                    if not softplus:
                        sq.step("vector", lambda e, nrow=nrow: e.tensor_copy(
                            otb[0:nrow, :], ps_m1[0:nrow, :]))
                    else:
                        sq.step("scalar", lambda e, nrow=nrow: e.activation(
                            otm[0:nrow, 0:512], ps_m1[0:nrow, :], AF.Abs))
                        sq.step("scalar", lambda e, nrow=nrow: e.activation(
                            otm[0:nrow, 0:512], otm[0:nrow, 0:512], AF.Exp, scale=-1.0))
                        sq.step("scalar", lambda e, nrow=nrow: e.activation(
                            otm[0:nrow, 0:512], otm[0:nrow, 0:512], AF.Ln, bias=1.0))
                        sq.step("vector", lambda e, nrow=nrow: e.tensor_single_scalar(
                            sq32[0:nrow, 0:512], ps_m1[0:nrow, :], 0.0, ALU.max))
                        sq.step("vector", lambda e, nrow=nrow: e.tensor_add(
                            otb[0:nrow, :], otm[0:nrow, 0:512], sq32[0:nrow, 0:512]))
                    sq.dma(out_t[dt * 128:dt * 128 + nrow, t0:t0 + 512],
                           otb[0:nrow, :])

        fm_proj(ctq, 4, wuq, 512, q_c_fm)
        fm_proj(xn, NCH, wiq, 256, q_i_fm)
        fm_proj(xn, NCH, wik, 64, ki_fm)
        fm_proj(ctkv, 2, wuk, 512, kv_fm)
        fm_proj(xn, NCH, wkr, 32, mu_fm, softplus=True)

    if emit_only:
        return prog
    return seq_program(prog)


def host_topk_proj_inputs(x, W, core):
    return {
        "x_fm": fmT(core_slice_tokens(x, core)),
        "wdkv": np.ascontiguousarray(W["W_dkv"]),
        "wuk": np.ascontiguousarray(W["W_uk"]),
        "wuv": np.ascontiguousarray(W["W_uv"]),
        "wdq": np.ascontiguousarray(W["W_dq"]),
        "wuq": np.ascontiguousarray(W["W_uq"]),
        "wqr": np.ascontiguousarray(W["W_qr"]),
        "wkr": np.ascontiguousarray(W["W_kr"]),
        "wiq": np.ascontiguousarray(W["idx_wq"]),
        "wik": np.ascontiguousarray(W["idx_wk"]),
    }


def build_topk_attn(debug=False, io=None, emit_only=False):
    S = 4096
    SPB = 2112
    NQT = T_CORE // 128

    def prog(nc, sq, es):
        dt_ = lambda nm, shp, dty, kind: _dram(nc, io, nm, shp, dty, kind)
        x_fm = dt_("x_fm", [D_MODEL, T_CORE], F32, "ExternalInput")
        q_c_tm = dt_("q_c_tm", [T_CORE, 512], BF16, "ExternalInput")
        q_c_fmi = dt_("q_c_fmi", [512, T_CORE], BF16, "ExternalInput")
        qr_tmi = dt_("qr_tmi", [T_CORE, 256], F32, "ExternalInput")
        q_i_fmi = dt_("q_i_fmi", [256, T_CORE], BF16, "ExternalInput")
        k_i_fmi = dt_("k_i_fmi", [128, S], BF16, "ExternalInput")
        kv_fm_pad = dt_("kv_fm_pad", [512, SPB], BF16, "ExternalInput")
        tkt_full = dt_("tkt_full", [S, 1056], BF16, "ExternalInput")
        kv_tm_full = dt_("kv_tm_full", [S, 512], BF16, "ExternalInput")
        v_tm_full = dt_("v_tm_full", [S, 512], BF16, "ExternalInput")
        v_pad = dt_("v_pad", [SPB, 512], BF16, "ExternalInput")
        mu1_pad = dt_("mu1_pad", [16, SPB], BF16, "ExternalInput")
        mu2_pad = dt_("mu2_pad", [16, SPB], BF16, "ExternalInput")
        mu_tm_full = dt_("mu_tm_full", [S, 32], BF16, "ExternalInput")
        w_out = dt_("w_out", [512, D_MODEL], F32, "ExternalInput")
        idxw_row = dt_("idxw_row", [1, 4], F32, "ExternalInput")
        rdelta = dt_("rdelta", [1, 16], F32, "ExternalInput")
        thr_i = dt_("thr_i", [128, NQT], F32, "ExternalInput")
        ramp0 = dt_("ramp0", [128, 96], F32, "ExternalInput")
        mask0 = dt_("mask0", [128, 512], U8, "ExternalInput")
        cth_q = dt_("cth_q", [T_CORE, 16], F32, "ExternalInput")
        sth_q = dt_("sth_q", [T_CORE, 16], F32, "ExternalInput")
        cth_q2 = dt_("cth_q2", [T_CORE, 16], F32, "ExternalInput")
        sth_q2 = dt_("sth_q2", [T_CORE, 16], F32, "ExternalInput")
        cjb_i = dt_("cjb_i", [16, 192], F32, "ExternalInput")
        sjb_i = dt_("sjb_i", [16, 192], F32, "ExternalInput")
        cabs96 = dt_("cabs96", [16, 96], F32, "ExternalInput")
        sabs96 = dt_("sabs96", [16, 96], F32, "ExternalInput")
        crk_i = dt_("crk_i", [128, 512], F32, "ExternalInput")
        srk_i = dt_("srk_i", [128, 512], F32, "ExternalInput")
        x_out = dt_("x_out", [D_MODEL, T_CORE], F32, "ExternalOutput")
        if debug:
            dbg = {nm: dt_(f"dbg_{nm}", shp, dty, kind="ExternalOutput").ap()
                   for nm, shp, dty in (
                       ("I", [128, 4096], F32), ("idx", [128, 32], U32),
                       ("sctop", [128, 32], F32), ("band", [128, 192], F32),
                       ("qh1", [128, 128], F32), ("qh2", [128, 128], F32),
                       ("mh1", [128, 512], F32), ("mh2", [128, 512], F32),
                       ("outtm", [128, 512], F32), ("rs", [128, 1], F32),
                       ("ka", [16, 192], F32), ("qpa", [16, 1024], F32),
                       ("mug", [128, 1024], F32))}

        xv_ = x_fm.rearrange("(c p) t -> p c t", p=128)
        xo = x_out.rearrange("(c p) t -> p c t", p=128)

        sb = lambda n, s, d=F32: es.enter_context(nc.sbuf_tensor(n, s, d))
        pt = lambda n, s, d=F32: es.enter_context(nc.psum_tensor(n, s, d))

        ones_row = sb("ones_row", [1, 128])
        identw = sb("identw", [128, 128])
        identb = sb("identb", [128, 128], BF16)
        kvf_sb = sb("kvf_sb", [128, 4 * SPB], BF16)
        qcf_sb = sb("qcf_sb", [128, 4 * T_CORE], BF16)
        qif_sb = sb("qif_sb", [128, 2 * T_CORE], BF16)
        kif_sb = sb("kif_sb", [128, S], BF16)
        mu1_sb = sb("mu1_sb", [16, SPB], BF16)
        mu2_sb = sb("mu2_sb", [16, SPB], BF16)
        wo_sb = sb("wo_sb", [128, 4 * D_MODEL], BF16)
        iw_bc = sb("iw_bc", [128, 4])
        thr_sb = sb("thr_sb", [128, NQT])
        ramp_sb = sb("ramp_sb", [128, 96])
        mask0_sb = sb("mask0_sb", [128, 512], U8)
        crkd = sb("crkd", [128, 512])
        srkd = sb("srkd", [128, 512])
        drow = sb("drow", [1, 16])
        cd_row = sb("cd_row", [1, 16])
        sd_row = sb("sd_row", [1, 16])
        cd_bc = sb("cd_bc", [128, 16])
        sd_bc = sb("sd_bc", [128, 16])
        cd_pp = sb("cd_pp", [16, 1])
        sd_pp = sb("sd_pp", [16, 1])
        tr16 = sb("tr16", [1, 16])
        cjb_sb = sb("cjb_sb", [16, 192])
        sjb_sb = sb("sjb_sb", [16, 192])
        Cjb = sb("Cjb", [16, 192], BF16)
        Sjb = sb("Sjb", [16, 192], BF16)
        c96 = sb("c96", [16, 96])
        s96 = sb("s96", [16, 96])
        C96 = sb("C96", [16, 96], BF16)
        S96 = sb("S96", [16, 96], BF16)
        ka_band = sb("ka_band", [16, 192], BF16)
        kb_band = sb("kb_band", [16, 192], BF16)
        ka96 = sb("ka96", [16, 96], BF16)
        kb96 = sb("kb96", [16, 96], BF16)
        I_sb = sb("I_sb", [128, S])
        scrB = sb("scrB", [128, 2048])
        scrC = sb("scrC", [128, 512])
        m8 = sb("m8", [128, 8])
        idx32 = sb("idx32", [128, 32], U32)
        gall = sb("gall", [128, 32 * 1056], BF16)
        qr_t = sb("qr_t", [128, 256])
        qm_t = sb("qm_t", [128, 256])
        cth_t = sb("cth_t", [128, 16])
        sth_t = sb("sth_t", [128, 16])
        ct2_t = sb("ct2_t", [128, 16])
        st2_t = sb("st2_t", [128, 16])
        ctd = sb("ctd", [128, 16])
        std_ = sb("std_", [128, 16])
        ct2d = sb("ct2d", [128, 16])
        st2d = sb("st2d", [128, 16])
        qh1 = sb("qh1", [128, 128])
        qh2 = sb("qh2", [128, 128])
        qp1 = sb("qp1", [128, 128])
        qp2 = sb("qp2", [128, 128])
        qp1b = sb("qp1b", [128, 128], BF16)
        qp2b = sb("qp2b", [128, 128], BF16)
        qpa_fm = sb("qpa_fm", [16, 8 * 128], BF16)
        qpb_fm = sb("qpb_fm", [16, 8 * 128], BF16)
        qsa_fm = sb("qsa_fm", [16, 8 * 128], BF16)
        qsb_fm = sb("qsb_fm", [16, 8 * 128], BF16)
        mh1 = sb("mh1", [128, 512])
        mh2 = sb("mh2", [128, 512])
        sb_band = sb("sb_band", [128, 192])
        sc_top = sb("sc_top", [128, 32])
        mb_c = sb("mb_c", [128, 1])
        mt_c = sb("mt_c", [128, 1])
        bias_c = sb("bias_c", [128, 1])
        rs1 = sb("rs1", [128, 1])
        rs2 = sb("rs2", [128, 1])
        rr_c = sb("rr_c", [128, 1])
        P_band = sb("P_band", [128, 192], BF16)
        P_top = sb("P_top", [128, 32])
        PbT1 = sb("PbT1", [128, 128], BF16)
        PbT2 = sb("PbT2", [64, 128], BF16)
        Vb1 = sb("Vb1", [128, 512], BF16)
        Vb2 = sb("Vb2", [64, 512], BF16)
        v96_sb = sb("v96_sb", [96, 512], BF16)
        ot_top = sb("ot_top", [128, 64])
        out_tm = sb("out_tm", [128, 512])
        dense_tm = sb("dense_tm", [128, 512])
        otb = sb("otb", [128, 512], BF16)
        ofm = sb("ofm", [128, 4 * 128], BF16)
        Xt = sb("Xt", [128, NCH * 128])
        oscr = sb("oscr", [128, 128])
        qct_t = sb("qct_t", [128, 512], BF16)

        ps_i = pt("ps_i", [128, 512])
        ps_band = pt("ps_band", [128, 192])
        ps_96 = pt("ps_96", [128, 96])
        ps_av = pt("ps_av", [128, 64])
        ps_tr = pt("ps_tr", [128, 128], BF16)
        ps_trf = pt("ps_trf", [128, 128])
        ps_out = pt("ps_out", [128, 128])

        sq.step("gpsimd", lambda e: e.memset(ones_row[:], 1.0))
        sq.step("gpsimd", lambda e: e.memset(identw[:], 1.0))
        sq.step("gpsimd", lambda e: e.affine_select(
            identw[:], identw[:], pattern=[[1, 128]], compare_op=ALU.is_equal,
            fill=0.0, base=0, channel_multiplier=-1))
        sq.step("vector", lambda e: e.tensor_copy(identb[:], identw[:]))
        sq.dma(kvf_sb[:].rearrange("p (c t) -> p c t", c=4),
               kv_fm_pad.rearrange("(c p) t -> p c t", p=128))
        sq.dma(qcf_sb[:].rearrange("p (c t) -> p c t", c=4),
               q_c_fmi.rearrange("(c p) t -> p c t", p=128))
        sq.dma(qif_sb[:].rearrange("p (c t) -> p c t", c=2),
               q_i_fmi.rearrange("(c p) t -> p c t", p=128))
        sq.dma(kif_sb[:], k_i_fmi)
        sq.dma(mu1_sb[:], mu1_pad)
        sq.dma(mu2_sb[:], mu2_pad)
        sq.dma(wo_sb[:].rearrange("p (c d) -> p c d", c=4),
               w_out.rearrange("(c p) d -> p c d", p=128), engine="gpsimd")
        sq.dma(thr_sb[:], thr_i)
        sq.dma(ramp_sb[:], ramp0)
        sq.dma(mask0_sb[:], mask0)
        sq.dma(crkd[:], crk_i)
        sq.dma(srkd[:], srk_i)
        sq.dma(cjb_sb[:], cjb_i)
        sq.dma(sjb_sb[:], sjb_i)
        sq.dma(c96[:], cabs96)
        sq.dma(s96[:], sabs96)
        sq.dma(drow[:], rdelta)
        sq.dma(tr16[:, 0:4], idxw_row)
        sq.step("tensor", lambda e: e.matmul(ps_i[:, 0:4], ones_row[:], tr16[:, 0:4]))
        sq.step("vector", lambda e: e.tensor_copy(iw_bc[:], ps_i[:, 0:4]))
        sq.step("scalar", lambda e: e.activation(drow[:], drow[:], AF.Sigmoid))
        sq.step("vector", lambda e: e.tensor_single_scalar(
            drow[:], drow[:], -2.0 * np.pi, ALU.mult))
        sq.step("vector", lambda e: e.tensor_single_scalar(
            tr16[:], drow[:], -np.pi, ALU.is_lt))
        sq.step("vector", lambda e: e.scalar_tensor_tensor(
            drow[:], tr16[:], 2.0 * np.pi, drow[:], ALU.mult, ALU.add))
        sq.step("scalar", lambda e: e.activation(sd_row[:], drow[:], AF.Sin))
        sq.step("vector", lambda e: e.tensor_single_scalar(
            cd_row[:], drow[:], np.pi / 2, ALU.add))
        sq.step("vector", lambda e: e.tensor_single_scalar(
            tr16[:], cd_row[:], np.pi, ALU.is_gt))
        sq.step("vector", lambda e: e.scalar_tensor_tensor(
            cd_row[:], tr16[:], -2.0 * np.pi, cd_row[:], ALU.mult, ALU.add))
        sq.step("scalar", lambda e: e.activation(cd_row[:], cd_row[:], AF.Sin))
        for src, dstb, dstp in ((cd_row, cd_bc, cd_pp), (sd_row, sd_bc, sd_pp)):
            sq.step("tensor", lambda e, src=src: e.matmul(
                ps_i[:, 0:16], ones_row[:], src[:]))
            sq.step("vector", lambda e, dstb=dstb: e.tensor_copy(dstb[:], ps_i[:, 0:16]))
            sq.step("tensor", lambda e, src=src: e.transpose(
                ps_trf[0:16, 0:1], src[:], identw[0:1, 0:1]))
            sq.step("vector", lambda e, dstp=dstp: e.tensor_copy(
                dstp[:], ps_trf[0:16, 0:1]))
        cdrep = cd_bc[:].rearrange("p (o i) -> p o i", o=1).to_broadcast([128, 32, 16])
        sdrep = sd_bc[:].rearrange("p (o i) -> p o i", o=1).to_broadcast([128, 32, 16])
        crkv = crkd[:].rearrange("p (r i) -> p r i", i=16)
        srkv = srkd[:].rearrange("p (r i) -> p r i", i=16)
        sq.step("vector", lambda e: e.tensor_mul(
            scrB[:, 0:512].rearrange("p (r i) -> p r i", i=16), crkv, cdrep))
        sq.step("vector", lambda e: e.tensor_mul(
            scrB[:, 512:1024].rearrange("p (r i) -> p r i", i=16), srkv, sdrep))
        sq.step("vector", lambda e: e.tensor_mul(
            scrB[:, 1024:1536].rearrange("p (r i) -> p r i", i=16), srkv, cdrep))
        sq.step("vector", lambda e: e.tensor_mul(
            scrB[:, 1536:2048].rearrange("p (r i) -> p r i", i=16), crkv, sdrep))
        sq.step("vector", lambda e: e.tensor_sub(
            crkd[:], scrB[:, 0:512], scrB[:, 512:1024]))
        sq.step("vector", lambda e: e.tensor_add(
            srkd[:], scrB[:, 1024:1536], scrB[:, 1536:2048]))
        sq.step("vector", lambda e: e.tensor_scalar(
            scrB[0:16, 0:192], cjb_sb[:], cd_pp[:], None, ALU.mult))
        sq.step("vector", lambda e: e.tensor_scalar(
            scrC[0:16, 0:192], sjb_sb[:], sd_pp[:], None, ALU.mult))
        sq.step("vector", lambda e: e.tensor_sub(
            Cjb[:], scrB[0:16, 0:192], scrC[0:16, 0:192]))
        sq.step("vector", lambda e: e.tensor_scalar(
            scrB[0:16, 0:192], sjb_sb[:], cd_pp[:], None, ALU.mult))
        sq.step("vector", lambda e: e.tensor_scalar(
            scrC[0:16, 0:192], cjb_sb[:], sd_pp[:], None, ALU.mult))
        sq.step("vector", lambda e: e.tensor_add(
            Sjb[:], scrB[0:16, 0:192], scrC[0:16, 0:192]))
        sq.step("vector", lambda e: e.tensor_scalar(
            scrB[0:16, 0:96], c96[:], cd_pp[:], None, ALU.mult))
        sq.step("vector", lambda e: e.tensor_scalar(
            scrC[0:16, 0:96], s96[:], sd_pp[:], None, ALU.mult))
        sq.step("vector", lambda e: e.tensor_sub(
            C96[:], scrB[0:16, 0:96], scrC[0:16, 0:96]))
        sq.step("vector", lambda e: e.tensor_scalar(
            scrB[0:16, 0:96], s96[:], cd_pp[:], None, ALU.mult))
        sq.step("vector", lambda e: e.tensor_scalar(
            scrC[0:16, 0:96], c96[:], sd_pp[:], None, ALU.mult))
        sq.step("vector", lambda e: e.tensor_add(
            S96[:], scrB[0:16, 0:96], scrC[0:16, 0:96]))
        sq.step("vector", lambda e: e.tensor_mul(
            scrB[0:16, 0:96], mu1_sb[0:16, PADK:PADK + 96], C96[:]))
        sq.step("vector", lambda e: e.tensor_mul(
            scrC[0:16, 0:96], mu2_sb[0:16, PADK:PADK + 96], S96[:]))
        sq.step("vector", lambda e: e.tensor_sub(
            ka96[:], scrB[0:16, 0:96], scrC[0:16, 0:96]))
        sq.step("vector", lambda e: e.tensor_mul(
            scrB[0:16, 0:96], mu1_sb[0:16, PADK:PADK + 96], S96[:]))
        sq.step("vector", lambda e: e.tensor_mul(
            scrC[0:16, 0:96], mu2_sb[0:16, PADK:PADK + 96], C96[:]))
        sq.step("vector", lambda e: e.tensor_add(
            kb96[:], scrB[0:16, 0:96], scrC[0:16, 0:96]))
        sq.dma(v96_sb[:], v_pad[PADK:PADK + 96, :])
        iota_c = sb("iota_c", [128, 512])
        bandmask = sb("bandmask", [128, 192])
        thr_adj = sb("thr_adj", [128, 1])
        sq.step("gpsimd", lambda e: e.iota(
            iota_c[:], pattern=[[1, 512]], base=0, channel_multiplier=0,
            allow_small_or_imprecise_dtypes=True))
        sq.step("gpsimd", lambda e: e.memset(bandmask[:], 0.0))
        sq.step("gpsimd", lambda e: e.affine_select(
            bandmask[:], bandmask[:], pattern=[[1, 192]], compare_op=ALU.is_ge,
            fill=-3e38, base=0, channel_multiplier=-1))
        sq.step("gpsimd", lambda e: e.affine_select(
            bandmask[:], bandmask[:], pattern=[[-1, 192]], compare_op=ALU.is_ge,
            fill=-3e38, base=63, channel_multiplier=1))

        for qt in range(NQT):
            t0 = qt * 128
            for kt in range(8):
                for hh in range(4):
                    hb = 64 * (hh % 2)
                    hc = hh // 2
                    sq.step("tensor", lambda e, hb=hb, hc=hc, t0=t0, kt=kt: e.matmul(
                        ps_i[:], qif_sb[hb:hb + 64,
                                        hc * T_CORE + t0:hc * T_CORE + t0 + 128],
                        kif_sb[hb:hb + 64, kt * 512:(kt + 1) * 512],
                        start=True, stop=True))
                    if hh == 0:
                        sq.step("vector", lambda e, kt=kt, hh=hh: e.tensor_scalar(
                            I_sb[:, kt * 512:(kt + 1) * 512], ps_i[:], 0.0,
                            iw_bc[:, hh:hh + 1], ALU.max, ALU.mult))
                    else:
                        sq.step("vector", lambda e, kt=kt, hh=hh: e.tensor_scalar(
                            scrC[:], ps_i[:], 0.0, iw_bc[:, hh:hh + 1],
                            ALU.max, ALU.mult))
                        sq.step("vector", lambda e, kt=kt: e.tensor_add(
                            I_sb[:, kt * 512:(kt + 1) * 512],
                            I_sb[:, kt * 512:(kt + 1) * 512], scrC[:]))
            for kt in range(8):
                sq.step("vector", lambda e, qt=qt, kt=kt: e.tensor_single_scalar(
                    thr_adj[:], thr_sb[:, qt:qt + 1], float(-512 * kt), ALU.add))
                sq.step("vector", lambda e: e.tensor_scalar(
                    scrC[:], iota_c[:], thr_adj[:], None, ALU.is_le))
                sq.step("vector", lambda e: e.tensor_scalar(
                    scrC[:], scrC[:], 2e38, -1e38, ALU.mult, ALU.add))
                sq.step("vector", lambda e, kt=kt: e.tensor_tensor(
                    I_sb[:, kt * 512:(kt + 1) * 512],
                    I_sb[:, kt * 512:(kt + 1) * 512], scrC[:], ALU.min))
            if qt == 0:
                sq.step("vector", lambda e: e.tensor_max(
                    I_sb[:, 0:96], I_sb[:, 0:96], ramp_sb[:]))
            for rd in range(4):
                sq.step("vector", lambda e: e.max(out=m8[:], in_=I_sb[:]))
                sq.step("vector", lambda e, rd=rd: e.max_index(
                    out=idx32[:, rd * 8:(rd + 1) * 8], in_max=m8[:], in_values=I_sb[:]))
                sq.step("vector", lambda e: e.match_replace(
                    out=I_sb[:], in_to_replace=m8[:], in_values=I_sb[:],
                    imm_value=-3e38))
            for r in range(32):
                sq.step("gpsimd", lambda e, r=r: e.indirect_dma_start(
                    out=gall[:, r * 1056:(r + 1) * 1056], out_offset=None,
                    in_=tkt_full,
                    in_offset=IndirectOffsetOnAxis(ap=idx32[:, r:r + 1], axis=0),
                    oob_is_err=False), inc=16, dma=True)
            if debug and qt == 8:
                sq.dma(dbg["I"], I_sb[:])
                sq.dma(dbg["idx"], idx32[:])
            sq.dma(qr_t[:], qr_tmi[t0:t0 + 128, :])
            sq.dma(qct_t[:], q_c_tm[t0:t0 + 128, :])
            sq.dma(cth_t[:], cth_q[t0:t0 + 128, :])
            sq.dma(sth_t[:], sth_q[t0:t0 + 128, :])
            sq.dma(ct2_t[:], cth_q2[t0:t0 + 128, :])
            sq.dma(st2_t[:], sth_q2[t0:t0 + 128, :])
            sq.step("scalar", lambda e: e.activation(qm_t[:], qr_t[:], AF.Abs))
            sq.step("scalar", lambda e: e.activation(qm_t[:], qm_t[:], AF.Exp, scale=-1.0))
            sq.step("scalar", lambda e: e.activation(qm_t[:], qm_t[:], AF.Ln, bias=1.0))
            sq.step("vector", lambda e: e.tensor_single_scalar(
                scrB[:, 0:256], qr_t[:], 0.0, ALU.max))
            sq.step("vector", lambda e: e.tensor_add(qm_t[:], qm_t[:], scrB[:, 0:256]))
            for (csrc, ssrc, cdst, sdst) in ((cth_t, sth_t, ctd, std_),
                                             (ct2_t, st2_t, ct2d, st2d)):
                sq.step("vector", lambda e, csrc=csrc, cdst=cdst: e.tensor_mul(
                    cdst[:], csrc[:], cd_bc[:]))
                sq.step("vector", lambda e, ssrc=ssrc: e.tensor_mul(
                    scrC[:, 0:16], ssrc[:], sd_bc[:]))
                sq.step("vector", lambda e, cdst=cdst: e.tensor_sub(
                    cdst[:], cdst[:], scrC[:, 0:16]))
                sq.step("vector", lambda e, ssrc=ssrc, sdst=sdst: e.tensor_mul(
                    sdst[:], ssrc[:], cd_bc[:]))
                sq.step("vector", lambda e, csrc=csrc: e.tensor_mul(
                    scrC[:, 0:16], csrc[:], sd_bc[:]))
                sq.step("vector", lambda e, sdst=sdst: e.tensor_add(
                    sdst[:], sdst[:], scrC[:, 0:16]))
            qm1 = qm_t[:].rearrange("p (h two i) -> p h two i", two=2, i=16)[:, :, 0, :]
            qm2 = qm_t[:].rearrange("p (h two i) -> p h two i", two=2, i=16)[:, :, 1, :]
            for (cc, ss, d1, d2) in ((ctd, std_, qh1, qh2), (ct2d, st2d, qp1, qp2)):
                ccb = cc[:].rearrange("p (o i) -> p o i", o=1).to_broadcast([128, 8, 16])
                ssb = ss[:].rearrange("p (o i) -> p o i", o=1).to_broadcast([128, 8, 16])
                d1v = d1[:].rearrange("p (h i) -> p h i", i=16)
                d2v = d2[:].rearrange("p (h i) -> p h i", i=16)
                sB = scrB[:, 0:128].rearrange("p (h i) -> p h i", i=16)
                sq.step("vector", lambda e, d1v=d1v, ccb=ccb: e.tensor_mul(d1v, qm1, ccb))
                sq.step("vector", lambda e, sB=sB, ssb=ssb: e.tensor_mul(sB, qm2, ssb))
                sq.step("vector", lambda e, d1=d1: e.tensor_sub(
                    d1[:], d1[:], scrB[:, 0:128]))
                sq.step("vector", lambda e, d2v=d2v, ssb=ssb: e.tensor_mul(d2v, qm1, ssb))
                sq.step("vector", lambda e, sB=sB, ccb=ccb: e.tensor_mul(sB, qm2, ccb))
                sq.step("vector", lambda e, d2=d2: e.tensor_add(
                    d2[:], d2[:], scrB[:, 0:128]))
            sq.step("vector", lambda e: e.tensor_copy(qp1b[:], qp1[:]))
            sq.step("vector", lambda e: e.tensor_copy(qp2b[:], qp2[:]))
            for h in range(8):
                sq.step("tensor", lambda e, h=h: e.transpose(
                    ps_tr[0:16, 0:128], qp1b[:, h * 16:(h + 1) * 16], identb[:]))
                sq.step("vector", lambda e, h=h: e.tensor_copy(
                    qpa_fm[:, h * 128:(h + 1) * 128], ps_tr[0:16, 0:128]))
                sq.step("tensor", lambda e, h=h: e.transpose(
                    ps_tr[0:16, 0:128], qp2b[:, h * 16:(h + 1) * 16], identb[:]))
                sq.step("vector", lambda e, h=h: e.tensor_copy(
                    qpb_fm[:, h * 128:(h + 1) * 128], ps_tr[0:16, 0:128]))
            if qt == 0:
                sq.step("vector", lambda e: e.tensor_copy(qp1b[:], qh1[:]))
                sq.step("vector", lambda e: e.tensor_copy(qp2b[:], qh2[:]))
                for h in range(8):
                    sq.step("tensor", lambda e, h=h: e.transpose(
                        ps_tr[0:16, 0:128], qp1b[:, h * 16:(h + 1) * 16], identb[:]))
                    sq.step("vector", lambda e, h=h: e.tensor_copy(
                        qsa_fm[:, h * 128:(h + 1) * 128], ps_tr[0:16, 0:128]))
                    sq.step("tensor", lambda e, h=h: e.transpose(
                        ps_tr[0:16, 0:128], qp2b[:, h * 16:(h + 1) * 16], identb[:]))
                    sq.step("vector", lambda e, h=h: e.tensor_copy(
                        qsb_fm[:, h * 128:(h + 1) * 128], ps_tr[0:16, 0:128]))
            mg1 = gall[:].rearrange("p (r v) -> p r v", v=1056)[:, :, 1024:1040]
            mg2 = gall[:].rearrange("p (r v) -> p r v", v=1056)[:, :, 1040:1056]
            crv = crkd[:].rearrange("p (r i) -> p r i", i=16)
            srv = srkd[:].rearrange("p (r i) -> p r i", i=16)
            m1v = mh1[:].rearrange("p (r i) -> p r i", i=16)
            m2v = mh2[:].rearrange("p (r i) -> p r i", i=16)
            sBv = scrB[:, 0:512].rearrange("p (r i) -> p r i", i=16)
            sq.step("vector", lambda e: e.tensor_mul(m1v, mg1, crv))
            sq.step("vector", lambda e: e.tensor_mul(sBv, mg2, srv))
            sq.step("vector", lambda e: e.tensor_sub(mh1[:], mh1[:], scrB[:, 0:512]))
            sq.step("vector", lambda e: e.tensor_mul(m2v, mg1, srv))
            sq.step("vector", lambda e: e.tensor_mul(sBv, mg2, crv))
            sq.step("vector", lambda e: e.tensor_add(mh2[:], mh2[:], scrB[:, 0:512]))
            for (dst, A, Bt) in ((ka_band, Cjb, Sjb), (kb_band, Sjb, Cjb)):
                sq.step("vector", lambda e, t0=t0, A=A: e.tensor_mul(
                    scrB[0:16, 0:192], mu1_sb[0:16, t0:t0 + 192], A[:]))
                sq.step("vector", lambda e, t0=t0, Bt=Bt: e.tensor_mul(
                    scrC[0:16, 0:192], mu2_sb[0:16, t0:t0 + 192], Bt[:]))
                if dst is ka_band:
                    sq.step("vector", lambda e, dst=dst: e.tensor_sub(
                        dst[:], scrB[0:16, 0:192], scrC[0:16, 0:192]))
                else:
                    sq.step("vector", lambda e, dst=dst: e.tensor_add(
                        dst[:], scrB[0:16, 0:192], scrC[0:16, 0:192]))
            sq.dma(Vb1[:], v_pad[t0:t0 + 128, :])
            sq.dma(Vb2[:], v_pad[t0 + 128:t0 + 192, :])

            for h in range(8):
                hb = 64 * (h % 2)
                hc = h // 2
                sq.step("tensor", lambda e, hb=hb, hc=hc, t0=t0: e.matmul(
                    ps_band[:], qcf_sb[hb:hb + 64, hc * T_CORE + t0:hc * T_CORE + t0 + 128],
                    kvf_sb[hb:hb + 64, hc * SPB + t0:hc * SPB + t0 + 192],
                    start=True, stop=False))
                sq.step("tensor", lambda e, h=h: e.matmul(
                    ps_band[:], qpa_fm[:, h * 128:(h + 1) * 128], ka_band[:],
                    start=False, stop=False))
                sq.step("tensor", lambda e, h=h: e.matmul(
                    ps_band[:], qpb_fm[:, h * 128:(h + 1) * 128], kb_band[:],
                    start=False, stop=True))
                sq.step("vector", lambda e: e.tensor_add(
                    sb_band[:], ps_band[:], bandmask[:]))
                sq.step("vector", lambda e: e.reduce_max(mb_c[:], sb_band[:], axis=AX.X))
                qcb = qct_t[:].rearrange("p (h i) -> p h i", i=64)[
                    :, h:h + 1, :].to_broadcast([128, 32, 64])
                kgv = gall[:].rearrange("p (r v) -> p r v", v=1056)[:, :, h * 64:(h + 1) * 64]
                sq.step("vector", lambda e, qcb=qcb, kgv=kgv: e.tensor_mul(
                    scrB[:, 0:2048].rearrange("p (r i) -> p r i", i=64), kgv, qcb))
                sq.step("vector", lambda e: e.reduce_sum(
                    sc_top[:], scrB[:, 0:2048].rearrange("p (r i) -> p r i", i=64),
                    axis=AX.X))
                q1b = qh1[:].rearrange("p (h i) -> p h i", i=16)[
                    :, h:h + 1, :].to_broadcast([128, 32, 16])
                q2b = qh2[:].rearrange("p (h i) -> p h i", i=16)[
                    :, h:h + 1, :].to_broadcast([128, 32, 16])
                sq.step("vector", lambda e, q1b=q1b: e.tensor_mul(
                    scrB[:, 0:512].rearrange("p (r i) -> p r i", i=16), m1v, q1b))
                sq.step("vector", lambda e, q2b=q2b: e.tensor_mul(
                    scrB[:, 512:1024].rearrange("p (r i) -> p r i", i=16), m2v, q2b))
                sq.step("vector", lambda e: e.tensor_add(
                    scrB[:, 0:512], scrB[:, 0:512], scrB[:, 512:1024]))
                sq.step("vector", lambda e: e.reduce_sum(
                    scrC[:, 0:32], scrB[:, 0:512].rearrange("p (r i) -> p r i", i=16),
                    axis=AX.X))
                sq.step("vector", lambda e: e.tensor_add(
                    sc_top[:], sc_top[:], scrC[:, 0:32]))
                if debug and qt == 8 and h == 0:
                    sq.dma(dbg["sctop"], sc_top[:])
                    sq.dma(dbg["band"], sb_band[:])
                    sq.dma(dbg["qh1"], qh1[:])
                    sq.dma(dbg["qh2"], qh2[:])
                    sq.dma(dbg["mh1"], mh1[:])
                    sq.dma(dbg["mh2"], mh2[:])
                sq.step("vector", lambda e: e.reduce_max(mt_c[:], sc_top[:], axis=AX.X))
                sq.step("vector", lambda e: e.tensor_max(mb_c[:], mb_c[:], mt_c[:]))
                sq.step("vector", lambda e: e.tensor_single_scalar(
                    bias_c[:], mb_c[:], -ATT2_SCALE, ALU.mult))
                sq.step("scalar", lambda e: e.activation(
                    P_band[:], sb_band[:], AF.Exp, bias=bias_c[:],
                    scale=ATT2_SCALE, accum_out=rs1[:]))
                sq.step("scalar", lambda e: e.activation(
                    P_top[:], sc_top[:], AF.Exp, bias=bias_c[:],
                    scale=ATT2_SCALE, accum_out=rs2[:]))
                sq.step("vector", lambda e: e.tensor_add(rs1[:], rs1[:], rs2[:]))
                sq.step("vector", lambda e: e.reciprocal(rr_c[:], rs1[:]))
                sq.step("tensor", lambda e: e.transpose(
                    ps_tr[:], P_band[:, 0:128], identb[:]))
                sq.step("vector", lambda e: e.tensor_copy(PbT1[:], ps_tr[:]))
                sq.step("tensor", lambda e: e.transpose(
                    ps_tr[0:64, 0:128], P_band[:, 128:192], identb[:]))
                sq.step("vector", lambda e: e.tensor_copy(PbT2[:], ps_tr[0:64, 0:128]))
                sq.step("tensor", lambda e, h=h: e.matmul(
                    ps_av[:], PbT1[:], Vb1[:, h * 64:(h + 1) * 64],
                    start=True, stop=False))
                sq.step("tensor", lambda e, h=h: e.matmul(
                    ps_av[:], PbT2[:], Vb2[0:64, h * 64:(h + 1) * 64],
                    start=False, stop=True))
                vgv = gall[:].rearrange("p (r v) -> p v r", v=1056)[:, 512 + h * 64:512 + (h + 1) * 64, :]
                ptb = P_top[:].rearrange("p (o r) -> p o r", o=1).to_broadcast([128, 64, 32])
                sq.step("vector", lambda e, vgv=vgv, ptb=ptb: e.tensor_mul(
                    scrB[:, 0:2048].rearrange("p (v r) -> p v r", r=32), vgv, ptb))
                sq.step("vector", lambda e: e.reduce_sum(
                    ot_top[:], scrB[:, 0:2048].rearrange("p (v r) -> p v r", r=32),
                    axis=AX.X))
                sq.step("vector", lambda e: e.tensor_add(ot_top[:], ot_top[:], ps_av[:]))
                sq.step("vector", lambda e, h=h: e.tensor_scalar(
                    out_tm[:, h * 64:(h + 1) * 64], ot_top[:], rr_c[:], None, ALU.mult))
                if qt == 0:
                    sq.step("tensor", lambda e, hb=hb, hc=hc: e.matmul(
                        ps_96[:], qcf_sb[hb:hb + 64, hc * T_CORE:hc * T_CORE + 128],
                        kvf_sb[hb:hb + 64, hc * SPB + PADK:hc * SPB + PADK + 96],
                        start=True, stop=False))
                    sq.step("tensor", lambda e, h=h: e.matmul(
                        ps_96[:], qsa_fm[:, h * 128:(h + 1) * 128], ka96[:],
                        start=False, stop=False))
                    sq.step("tensor", lambda e, h=h: e.matmul(
                        ps_96[:], qsb_fm[:, h * 128:(h + 1) * 128], kb96[:],
                        start=False, stop=True))
                    sq.step("vector", lambda e: e.tensor_copy(
                        scrB[:, 0:96], ps_96[:]))
                    sq.step("vector", lambda e: e.reduce_max(
                        mb_c[:], scrB[:, 0:96], axis=AX.X))
                    sq.step("vector", lambda e: e.tensor_single_scalar(
                        bias_c[:], mb_c[:], -ATT2_SCALE, ALU.mult))
                    sq.step("scalar", lambda e: e.activation(
                        P_band[:, 0:96], scrB[:, 0:96], AF.Exp, bias=bias_c[:],
                        scale=ATT2_SCALE, accum_out=rs1[:]))
                    sq.step("vector", lambda e: e.reciprocal(rr_c[:], rs1[:]))
                    sq.step("tensor", lambda e: e.transpose(
                        ps_tr[0:96, 0:128], P_band[:, 0:96], identb[:]))
                    sq.step("vector", lambda e: e.tensor_copy(
                        PbT1[0:96, :], ps_tr[0:96, 0:128]))
                    sq.step("tensor", lambda e, h=h: e.matmul(
                        ps_av[:], PbT1[0:96, :], v96_sb[0:96, h * 64:(h + 1) * 64],
                        start=True, stop=True))
                    sq.step("vector", lambda e, h=h: e.tensor_scalar(
                        dense_tm[:, h * 64:(h + 1) * 64], ps_av[:], rr_c[:],
                        None, ALU.mult))
            if qt == 0:
                sq.step("vector", lambda e: e.tensor_copy(scrB[:, 0:512], out_tm[:]))
                sq.step("vector", lambda e: e.select(
                    out_tm[:], mask0_sb[:], dense_tm[:], scrB[:, 0:512]))
            if debug and qt == 8:
                sq.dma(dbg["outtm"], out_tm[:])
                sq.dma(dbg["rs"], rs1[:])
                sq.step("vector", lambda e: e.tensor_copy(scrC[0:16, 0:192], ka_band[:]))
                sq.dma(dbg["ka"], scrC[0:16, 0:192])
                sq.step("vector", lambda e: e.tensor_copy(scrB[0:16, 0:1024], qpa_fm[:]))
                sq.dma(dbg["qpa"], scrB[0:16, 0:1024])
                sq.dma(dbg["mug"], gall[:, 0:1024])
            sq.step("vector", lambda e: e.tensor_copy(otb[:], out_tm[:]))
            for c in range(4):
                sq.step("tensor", lambda e, c=c: e.transpose(
                    ps_tr[:], otb[:, c * 128:(c + 1) * 128], identb[:]))
                sq.step("vector", lambda e, c=c: e.tensor_copy(
                    ofm[:, c * 128:(c + 1) * 128], ps_tr[:]))
            sq.dma(Xt[:].rearrange("p (c t) -> p c t", c=NCH), xv_[:, :, t0:t0 + 128])
            for d in range(NCH):
                sq.step("tensor", lambda e, d=d: [
                    e.matmul(ps_out[:], wo_sb[:, c * D_MODEL + d * 128:
                                              c * D_MODEL + (d + 1) * 128],
                             ofm[:, c * 128:(c + 1) * 128],
                             start=(c == 0), stop=(c == 3))
                    for c in range(4)])
                sq.step("vector", lambda e, d=d: e.tensor_add(
                    oscr[:], ps_out[:], Xt[:, d * 128:(d + 1) * 128]))
                sq.dma(xo[:, d, t0:t0 + 128], oscr[:])

    if emit_only:
        return prog
    return seq_program(prog)


THETA16 = (10000.0 ** (-2.0 * np.arange(16) / 32.0)).astype(np.float64)


def host_topk_attn_inputs(x, proj_outs, W, core):
    b, h = core // 2, core % 2
    t0g = h * 2048
    full = lambda key: np.concatenate(
        [proj_outs[2 * b][key], proj_outs[2 * b + 1][key]], axis=0)
    kv_full = full("kv_tm")
    v_full = full("v_tm")
    mu_full = full("mu_tm")
    ki_full = full("k_i_tm")
    SPB = 2112
    zc = np.zeros((512, 1), kv_full.dtype)
    kv_fm_pad = np.concatenate(
        [np.zeros((512, PADK), kv_full.dtype), np.ascontiguousarray(kv_full.T),
         zc], axis=1)[:, t0g:t0g + SPB]
    v_pad = np.concatenate([np.zeros((PADK, 512), v_full.dtype), v_full,
                            np.zeros((1, 512), v_full.dtype)],
                           axis=0)[t0g:t0g + SPB]
    mu1_pad = np.concatenate(
        [np.zeros((16, PADK), kv_full.dtype),
         np.ascontiguousarray(mu_full[:, 0:16].T).astype(kv_full.dtype),
         zc[0:16]], axis=1)[:, t0g:t0g + SPB]
    mu2_pad = np.concatenate(
        [np.zeros((16, PADK), kv_full.dtype),
         np.ascontiguousarray(mu_full[:, 16:32].T).astype(kv_full.dtype),
         zc[0:16]], axis=1)[:, t0g:t0g + SPB]

    tg = np.arange(t0g, t0g + T_CORE, dtype=np.float64)
    t0tile = (np.arange(T_CORE) // 128 * 128) + t0g
    ang_q = tg[:, None] * THETA16[None, :]
    ang_q2 = (2 * tg - t0tile)[:, None] * THETA16[None, :]
    jj = np.arange(192, dtype=np.float64)
    ang_j = jj[None, :] * THETA16[:, None]
    ss = np.arange(96, dtype=np.float64)
    ang_s = ss[None, :] * THETA16[:, None]
    rr = np.arange(64, 96, dtype=np.float64)
    ang_r = rr[:, None] * THETA16[None, :]
    crk = np.tile(np.cos(ang_r).astype(np.float32).reshape(1, 512), (128, 1))
    srk = np.tile(np.sin(ang_r).astype(np.float32).reshape(1, 512), (128, 1))

    thr = np.zeros((128, 16), np.float32)
    for qt in range(16):
        thr[:, qt] = t0g + qt * 128 + np.arange(128) - 64.0
    ramp0 = np.full((128, 96), -3e38, np.float32)
    if h == 0:
        for p in range(64, 95):
            t = p
            for s in range(t + 1, 96):
                ramp0[p, s] = -1e30 - s * 1e26
    mask0 = np.zeros((128, 512), np.uint8)
    if h == 0:
        mask0[0:64, :] = 1

    return {
        "x_fm": fmT(core_slice_tokens(x, core)),
        "q_c_tm": proj_outs[core]["q_c_tm"],
        "q_c_fmi": np.ascontiguousarray(proj_outs[core]["q_c_tm"].T),
        "qr_tmi": proj_outs[core]["qr_tm"],
        "q_i_fmi": np.ascontiguousarray(proj_outs[core]["q_i_tm"].T),
        "k_i_fmi": np.ascontiguousarray(
            np.concatenate([ki_full.T, ki_full.T], axis=0)),
        "kv_fm_pad": np.ascontiguousarray(kv_fm_pad),
        "kv_tm_full": np.ascontiguousarray(kv_full),
        "v_tm_full": np.ascontiguousarray(v_full),
        "v_pad": np.ascontiguousarray(v_pad),
        "mu1_pad": np.ascontiguousarray(mu1_pad),
        "mu2_pad": np.ascontiguousarray(mu2_pad),
        "mu_tm_full": np.ascontiguousarray(mu_full),
        "w_out": np.ascontiguousarray(W["W_out"]),
        "idxw_row": np.asarray(W["idx_w"], np.float32)[None, :],
        "rdelta": np.asarray(W["raw_delta"], np.float32)[None, :],
        "thr_i": thr,
        "ramp0": ramp0,
        "mask0": mask0,
        "cth_q": np.cos(ang_q).astype(np.float32),
        "sth_q": np.sin(ang_q).astype(np.float32),
        "cth_q2": np.cos(ang_q2).astype(np.float32),
        "sth_q2": np.sin(ang_q2).astype(np.float32),
        "cjb_i": np.cos(ang_j).astype(np.float32),
        "sjb_i": np.sin(ang_j).astype(np.float32),
        "cabs96": np.cos(ang_s).astype(np.float32),
        "sabs96": np.sin(ang_s).astype(np.float32),
        "crk_i": crk,
        "srk_i": srk,
    }



def _run(nc, in_maps):
    res = run_bass_kernel_spmd(nc, in_maps, core_ids=list(range(N_CORES)))
    return res.results


_PROG_CACHE = {}


def _prog(name, builder):
    if name not in _PROG_CACHE:
        _PROG_CACHE[name] = builder()
    return _PROG_CACHE[name]



PAIRS = [[0, 1], [2, 3], [4, 5], [6, 7]]
ALL8 = [list(range(8))]


def build_fused(tap=None):
    nc = bass.Bass("TRN2", target_bir_lowering=False, debug=False)
    sq = Seq(nc)
    with ExitStack() as es:
        T = nc.dram_tensor
        x_ext_m = T("x_ext_m", [D_MODEL, T_CORE + 3], F32, kind="ExternalInput")
        x_outF = T("x_outF", [D_MODEL, T_CORE], F32, kind="ExternalOutput")
        r1 = T("r1", [D_MODEL, T_CORE], F32, kind="Internal")
        r2 = T("r2", [D_MODEL, T_CORE], F32, kind="Internal")
        r3 = T("r3", [D_MODEL, T_CORE], F32, kind="Internal")
        r4 = T("r4", [D_MODEL, T_CORE], F32, kind="Internal")
        xe5 = T("xe5", [D_MODEL, T_CORE + 1], F32, kind="Internal")
        r6 = T("r6", [D_MODEL, T_CORE], F32, kind="Internal")
        xe7 = T("xe7", [D_MODEL, T_CORE + 1], F32, kind="Internal")
        mex = T("mex", [4128, T_CORE], BF16, kind="Internal")
        mexg = T("mexg", [8256, T_CORE], BF16, kind="Internal")
        zs = T("zs", [D_INNER, T_CORE], F32, kind="Internal")
        ymy = T("ymy", [2048, T_CORE], BF16, kind="Internal")
        yag = T("yag", [4096, T_CORE], BF16, kind="Internal")
        q_ot = T("q_ot", [D_MODEL, T_CORE], BF16, kind="Internal")
        k_ot = T("k_ot", [D_MODEL, T_CORE], BF16, kind="Internal")
        kag = T("kag", [2 * D_MODEL, T_CORE], BF16, kind="Internal")
        vaug_my = T("vaug_my", [T_CORE, 1040], BF16, kind="Internal")
        vag = T("vag", [2 * T_CORE, 1040], BF16, kind="Internal")
        cmb1 = T("cmb1", [8, T_CORE], F32, kind="Internal")
        cmb2 = T("cmb2", [8, T_CORE], F32, kind="Internal")
        q_c_tmt = T("q_c_tmt", [T_CORE, 512], BF16, kind="Internal")
        qr_tmt = T("qr_tmt", [T_CORE, 256], F32, kind="Internal")
        q_c_fmt = T("q_c_fmt", [512, T_CORE], BF16, kind="Internal")
        q_i_fmt = T("q_i_fmt", [256, T_CORE], BF16, kind="Internal")
        tkt = T("tkt", [T_CORE, 1056], BF16, kind="Internal")
        tktg = T("tktg", [2 * T_CORE, 1056], BF16, kind="Internal")
        tkf = T("tkf", [608, T_CORE], BF16, kind="Internal")
        tkfg = T("tkfg", [1216, T_CORE], BF16, kind="Internal")
        kv_fm_pad = T("kv_fm_padI", [512, 2112], BF16, kind="Internal")
        mu1_pad = T("mu1_padI", [16, 2112], BF16, kind="Internal")
        mu2_pad = T("mu2_padI", [16, 2112], BF16, kind="Internal")
        v_padI = T("v_padI", [2112, 512], BF16, kind="Internal")
        va_dI = T("va_dI", [2112, 512], BF16, kind="Internal")
        vb_dI = T("vb_dI", [2112, 512], BF16, kind="Internal")
        kifmi = T("kifmiI", [128, 4096], BF16, kind="Internal")
        tgmA = T("tg_mA", [128, 1], F32, kind="ExternalInput")
        tgmB = T("tg_mB", [128, 1], F32, kind="ExternalInput")
        hcol1 = T("hcol1", [D_MODEL, 1], F32, kind="Internal")
        hag1 = T("hag1", [2 * D_MODEL, 1], F32, kind="Internal")
        hcol2 = T("hcol2", [D_MODEL, 1], F32, kind="Internal")
        hag2 = T("hag2", [2 * D_MODEL, 1], F32, kind="Internal")
        hm = T("hm", [128, 1], F32, kind="ExternalInput")
        xn_my = T("xn_myI", [D_MODEL, T_CORE], BF16, kind="Internal")
        xnag = T("xnagI", [8 * D_MODEL, T_CORE], BF16, kind="Internal")
        ek_ot = T("ek_oI", [D_MODEL, T_CORE], F32, kind="Internal")
        ekv_ot = T("ekv_oI", [D_MODEL, T_CORE], F32, kind="Internal")
        ekekv_my = T("ekekv_myI", [2 * D_MODEL, T_CORE], BF16, kind="Internal")
        ekekvg = T("ekekvgI", [16 * D_MODEL, T_CORE], BF16, kind="Internal")
        r_ot = T("r_oI", [D_MODEL, T_CORE], F32, kind="Internal")

        class NcProxy:
            def __init__(self, base, prefix):
                self._b, self._p = base, prefix

            def sbuf_tensor(self, name, *a, **k):
                return self._b.sbuf_tensor(self._p + name, *a, **k)

            def psum_tensor(self, name, *a, **k):
                return self._b.psum_tensor(self._p + name, *a, **k)

            def __getattr__(self, a):
                return getattr(self._b, a)

        def run(build, io, **kw):
            prog = build(io=io, emit_only=True, **kw)
            with ExitStack() as ses:
                prog(NcProxy(nc, io.nprefix), sq, ses)

        def ag(cls, src, dst, groups=PAIRS):
            sq.step("gpsimd", lambda e: e.collective_compute(
                "AllGather", ALU.bypass, replica_groups=groups,
                ins=[src.ap().opt()], outs=[dst.ap().opt()]),
                inc=1, sclass=cls)

        run(build_mamba_pre, FusedIO("mp_", {
            "x_ext": x_ext_m.ap(), "mex": mex.ap(), "zs_o": zs.ap()}))
        ag("ccA", mex, mexg)
        run(build_mamba_scan, FusedIO("ms_", {
            "mexg": mexg.ap(), "ymy": ymy.ap()}))
        ag("ccB", ymy, yag)
        run(build_mamba_post, FusedIO("mo_", {
            "x_fm": x_ext_m.ap()[:, 3:T_CORE + 3], "yag": yag.ap(),
            "zs_fm": zs.ap(), "x_out": r1.ap()}))

        io_ff = FusedIO("ff_", {"x_fm": r1.ap(), "x_out": r2.ap(),
                                "cmb_dram": cmb1.ap()})
        run(build_l0_ffn, io_ff)

        run(build_mla_qkv, FusedIO("mq_", {
            "x_fm": r2.ap(), "q_o": q_ot.ap(), "k_o": k_ot.ap(),
            "v_o": vaug_my.ap()}))
        ag("ccC", k_ot, kag)
        ag("ccD", vaug_my, vag)
        run(build_mla_attn2, FusedIO("ma_", {
            "q_i": q_ot.ap(), "k_i": kag.ap(), "vaug_i": vag.ap(),
            "x_fm": r2.ap(), "x_out": r3.ap()}))

        io_ff2 = FusedIO("ff_", dict(io_ff), nprefix="ff2_")
        io_ff2.update({"x_fm": r3.ap(), "x_out": r4.ap(), "cmb_dram": cmb2.ap()})
        run(build_l0_ffn, io_ff2)

        run(build_topk_proj, FusedIO("tp_", {
            "x_fm": r4.ap(), "q_c_tm": q_c_tmt.ap(), "qr_tm": qr_tmt.ap(),
            "kv_tm": tkt.ap()[:, 0:512], "v_tm": tkt.ap()[:, 512:1024],
            "mu_tm": tkt.ap()[:, 1024:1056],
            "q_c_fm": q_c_fmt.ap(), "q_i_fm": q_i_fmt.ap(),
            "kv_fm": tkf.ap()[0:512, :], "mu_fm": tkf.ap()[512:544, :],
            "ki_fm": tkf.ap()[544:608, :]}))
        ag("ccE", tkt, tktg)
        ag("ccF", tkf, tkfg)
        emit_topk_glue(nc, sq, tktg.ap(), tkfg.ap(), kv_fm_pad.ap(),
                       mu1_pad.ap(), mu2_pad.ap(), v_padI.ap(),
                       va_dI.ap(), vb_dI.ap(), kifmi.ap(),
                       tgmA.ap(), tgmB.ap())
        run(build_topk_attn, FusedIO("ta_", {
            "x_fm": r4.ap(), "tkt_full": tktg.ap(),
            "q_c_tm": q_c_tmt.ap(), "q_c_fmi": q_c_fmt.ap(),
            "qr_tmi": qr_tmt.ap(), "q_i_fmi": q_i_fmt.ap(),
            "k_i_fmi": kifmi.ap(), "kv_fm_pad": kv_fm_pad.ap(),
            "kv_tm_full": tktg.ap()[:, 0:512], "v_tm_full": tktg.ap()[:, 512:1024],
            "mu_tm_full": tktg.ap()[:, 1024:1056], "v_pad": v_padI.ap(),
            "mu1_pad": mu1_pad.ap(), "mu2_pad": mu2_pad.ap(),
            "x_out": xe5.ap()[:, 1:T_CORE + 1]}))

        emit_halo(nc, sq, "ccG", xe5.ap(), hcol1, hag1, hm.ap())
        io_lf = FusedIO("lf_", {"x_ext": xe5.ap(), "x_out": r6.ap()})
        run(build_l1_ffn, io_lf)

        run(build_wkv_kv, FusedIO("wk_", {
            "x_fm": r6.ap(), "xn_my": xn_my.ap(), "xnag": xnag.ap(),
            "xn_my_t": xn_my, "xnag_t": xnag,
            "ek_o": ek_ot.ap(), "ekv_o": ekv_ot.ap(),
            "ekekv_my": ekekv_my.ap(), "ekekv_my_t": ekekv_my,
            "ekekvg_t": ekekvg, "r_o": r_ot.ap()}))
        run(build_wkv_out, FusedIO("wo_", {
            "x_fm": r6.ap(), "ekekvg": ekekvg.ap(), "r_fm": r_ot.ap(),
            "x_out": xe7.ap()[:, 1:T_CORE + 1]}))

        emit_halo(nc, sq, "ccH", xe7.ap(), hcol2, hag2, hm.ap())
        io_lf2 = FusedIO("lf_", dict(io_lf), nprefix="lf2_")
        io_lf2.update({"x_ext": xe7.ap(), "x_out": x_outF.ap()})
        run(build_l1_ffn, io_lf2)

        if tap is not None:
            taps = {"r1": r1, "r2": r2, "r3": r3, "r4": r4, "r6": r6}
            if tap in taps:
                sq.dma(x_outF.ap(), taps[tap].ap())
            elif tap == "xe5":
                sq.dma(x_outF.ap(), xe5.ap()[:, 1:T_CORE + 1])
            elif tap == "xe7":
                sq.dma(x_outF.ap(), xe7.ap()[:, 1:T_CORE + 1])
            elif tap == "ek2":
                sq.dma(x_outF.ap(), ekekvg.ap()[4096:5120, :], engine="gpsimd")
            elif tap == "ek1":
                sq.dma(x_outF.ap(), ekekvg.ap()[2048:3072, :], engine="gpsimd")
            elif tap == "ekv1":
                sq.dma(x_outF.ap(), ekekvg.ap()[3072:4096, :], engine="gpsimd")
            elif tap == "ro":
                sq.dma(x_outF.ap(), r_ot.ap())
            elif tap == "xn1":
                sq.dma(x_outF.ap(), xnag.ap()[1024:2048, :], engine="gpsimd")

        sems = {}
        for st in sq.counts:
            sems[st] = es.enter_context(nc.semaphore(f"s_{st[0]}_{st[1]}"))
        block = es.enter_context(nc.Block())
        sq.build_into(block, sems)
        sq.finish(block, sems)
    return nc


def emit_topk_glue(nc, sq, tktg, tkfg, kv_fm_pad, mu1_pad, mu2_pad, v_pad_d,
                   va_d, vb_d, kifmi, mA_ap, mB_ap):
    with ExitStack() as es:
        sb = lambda n, s, d: es.enter_context(nc.sbuf_tensor(n, s, d))
        za = sb("tg_za", [128, 2112], BF16)
        zb = sb("tg_zb", [128, 2112], BF16)
        zo = sb("tg_zo", [128, 2112], BF16)
        zf = sb("tg_zf", [128, 2112], F32)
        mA = sb("tg_mAs", [128, 1], F32)
        mB = sb("tg_mBs", [128, 1], F32)
        sq.dma(mA[:], mA_ap)
        sq.dma(mB[:], mB_ap)

        def blend_pad(rows, base_a, base_b, dst):
            sq.step("gpsimd", lambda e: e.memset(za[0:rows, :], 0.0))
            sq.step("gpsimd", lambda e: e.memset(zb[0:rows, :], 0.0))
            sq.dma(za[0:rows, 63:2111], tkfg[base_a:base_a + rows, 0:2048])
            sq.dma(za[0:rows, 2111:2112], tkfg[base_b:base_b + rows, 0:1],
                   allow_slow_non_contiguous=True)
            sq.dma(zb[0:rows, 0:63], tkfg[base_a:base_a + rows, 1985:2048])
            sq.dma(zb[0:rows, 63:2111], tkfg[base_b:base_b + rows, 0:2048])
            sq.step("vector", lambda e: e.tensor_scalar(
                zf[0:rows, :], za[0:rows, :], mA[0:rows, :], None, ALU.mult))
            sq.step("vector", lambda e: e.scalar_tensor_tensor(
                zo[0:rows, :], zb[0:rows, :], mB[0:rows, :], zf[0:rows, :],
                ALU.mult, ALU.add))
            sq.dma(dst, zo[0:rows, :])

        for c in range(4):
            blend_pad(128, c * 128, 608 + c * 128,
                      kv_fm_pad[c * 128:(c + 1) * 128, :])
        blend_pad(16, 512, 1120, mu1_pad[0:16, :])
        blend_pad(16, 528, 1136, mu2_pad[0:16, :])
        zz = sb("tg_zz", [63, 512], BF16)
        sq.step("gpsimd", lambda e: e.memset(zz[:], 0.0))
        sq.dma(va_d[0:63, :], zz[0:63, :])
        sq.dma(va_d[63:2111, :], tktg[0:2048, 512:1024])
        sq.dma(va_d[2111:2112, :], tktg[2048:2049, 512:1024])
        sq.dma(vb_d[0:63, :], tktg[1985:2048, 512:1024])
        sq.dma(vb_d[63:2111, :], tktg[2048:4096, 512:1024])
        sq.dma(vb_d[2111:2112, :], zz[0:1, :])
        for i in range(17):
            r0 = i * 128
            nr = min(128, 2112 - r0)
            sq.dma(za[0:nr, 0:512], va_d[r0:r0 + nr, :])
            sq.dma(zb[0:nr, 0:512], vb_d[r0:r0 + nr, :])
            sq.step("vector", lambda e, nr=nr: e.tensor_scalar(
                zf[0:nr, 0:512], za[0:nr, 0:512], mA[0:nr, :], None, ALU.mult))
            sq.step("vector", lambda e, nr=nr: e.scalar_tensor_tensor(
                zo[0:nr, 0:512], zb[0:nr, 0:512], mB[0:nr, :], zf[0:nr, 0:512],
                ALU.mult, ALU.add))
            sq.dma(v_pad_d[r0:r0 + nr, :], zo[0:nr, 0:512])
        sq.dma(kifmi[0:64, 0:2048], tkfg[544:608, :])
        sq.dma(kifmi[0:64, 2048:4096], tkfg[1152:1216, :])
        sq.dma(kifmi[64:128, 0:2048], tkfg[544:608, :])
        sq.dma(kifmi[64:128, 2048:4096], tkfg[1152:1216, :])


def emit_halo(nc, sq, cls, xe, hcol_t, hag_t, mH_ap):
    with ExitStack() as es:
        sb = lambda n, s, d: es.enter_context(nc.sbuf_tensor(n, s, d))
        hb = sb(f"hb_{cls}", [128, NCH], F32)
        mH = sb(f"mH_{cls}", [128, 1], F32)
        sq.dma(hcol_t.ap(), xe[:, T_CORE:T_CORE + 1],
               allow_slow_non_contiguous=True)
        sq.step("gpsimd", lambda e: e.collective_compute(
            "AllGather", ALU.bypass, replica_groups=PAIRS,
            ins=[hcol_t.ap().opt()], outs=[hag_t.ap().opt()]),
            inc=1, sclass=cls)
        sq.dma(hb[:], hag_t.ap()[0:D_MODEL, :].rearrange(
            "(c p) one -> p (c one)", p=128), allow_slow_non_contiguous=True)
        sq.dma(mH[:], mH_ap)
        sq.step("vector", lambda e: e.tensor_scalar(
            hb[:], hb[:], mH[:], None, ALU.mult))
        sq.dma(xe[:, 0:1].rearrange("(c p) one -> p (c one)", p=128), hb[:],
               allow_slow_non_contiguous=True)


def host_topk_aux(W, core):
    b, h = core // 2, core % 2
    t0g = h * 2048
    tg = np.arange(t0g, t0g + T_CORE, dtype=np.float64)
    t0tile = (np.arange(T_CORE) // 128 * 128) + t0g
    ang_q = tg[:, None] * THETA16[None, :]
    ang_q2 = (2 * tg - t0tile)[:, None] * THETA16[None, :]
    jj = np.arange(192, dtype=np.float64)
    ang_j = jj[None, :] * THETA16[:, None]
    ss = np.arange(96, dtype=np.float64)
    ang_s = ss[None, :] * THETA16[:, None]
    rr = np.arange(64, 96, dtype=np.float64)
    ang_r = rr[:, None] * THETA16[None, :]
    crk = np.tile(np.cos(ang_r).astype(np.float32).reshape(1, 512), (128, 1))
    srk = np.tile(np.sin(ang_r).astype(np.float32).reshape(1, 512), (128, 1))
    thr = np.zeros((128, 16), np.float32)
    for qt in range(16):
        thr[:, qt] = t0g + qt * 128 + np.arange(128) - 64.0
    ramp0 = np.full((128, 96), -3e38, np.float32)
    if h == 0:
        for p in range(64, 95):
            for s_ in range(p + 1, 96):
                ramp0[p, s_] = -1e30 - s_ * 1e26
    mask0 = np.zeros((128, 512), np.uint8)
    if h == 0:
        mask0[0:64, :] = 1
    return {
        "w_out": np.ascontiguousarray(W["W_out"]),
        "idxw_row": np.asarray(W["idx_w"], np.float32)[None, :],
        "rdelta": np.asarray(W["raw_delta"], np.float32)[None, :],
        "thr_i": thr, "ramp0": ramp0, "mask0": mask0,
        "cth_q": np.cos(ang_q).astype(np.float32),
        "sth_q": np.sin(ang_q).astype(np.float32),
        "cth_q2": np.cos(ang_q2).astype(np.float32),
        "sth_q2": np.sin(ang_q2).astype(np.float32),
        "cjb_i": np.cos(ang_j).astype(np.float32),
        "sjb_i": np.sin(ang_j).astype(np.float32),
        "cabs96": np.cos(ang_s).astype(np.float32),
        "sabs96": np.sin(ang_s).astype(np.float32),
        "crk_i": crk, "srk_i": srk,
    }


def host_fused_inputs(inp, core):
    x = np.asarray(inp["L0_attn_a_x"], np.float32)
    b, h = core // 2, core % 2
    P = np.arange(128, dtype=np.uint32)
    mcol = lambda v: np.full((128, 1), v, np.float32)
    m = {}

    Wm = {k: inp[f"L0_attn_a_{k}"] for k in
          ["in_proj_w", "conv_w", "conv_b", "x_proj_w", "dt_proj_w",
           "dt_proj_b", "A_log", "D", "out_proj_w"]}
    pre = host_mamba_pre_inputs(x, Wm, core)
    m["x_ext_m"] = pre.pop("x_ext")
    for k, v in pre.items():
        m["mp_" + k] = v
    Alog = np.asarray(Wm["A_log"], np.float32)[h * 1024:(h + 1) * 1024]
    m["ms_Alog_i"] = np.ascontiguousarray(
        Alog.reshape(8, 128, 16).transpose(1, 0, 2).reshape(128, 128))
    m["ms_D_i"] = pack_vec(np.asarray(Wm["D"], np.float32)[h * 1024:(h + 1) * 1024])
    m["ms_mA_i"] = mcol(1.0 if h == 0 else 0.0)
    m["ms_mB_i"] = mcol(0.0 if h == 0 else 1.0)
    m["mo_opT"] = fmT(Wm["out_proj_w"])
    m["mo_mA_i"] = mcol(1.0 if h == 0 else 0.0)
    m["mo_mB_i"] = mcol(0.0 if h == 0 else 1.0)

    Wf = {k: inp[f"L0_ffn_{k}"] for k in
          ["router_w", "w_gate", "w_up", "w_down", "sw_gate", "sw_up", "sw_down"]}
    ff = host_l0_ffn_inputs(x, Wf, core)
    ff.pop("x_fm")
    for k, v in ff.items():
        m["ff_" + k] = v

    Wb = {k: inp[f"L0_attn_b_{k}"] for k in
          ["Wd_q_w", "Wd_q_b", "Wu_q_w", "Wu_q_b", "Wq_r_w", "Wq_r_b",
           "Wk_r_w", "Wk_r_b", "Wd_kv_w", "Wd_kv_b", "Wu_k_w", "Wu_k_b",
           "Wu_v_w", "Wu_v_b", "Wo_w", "Wo_b"]}
    qkv = host_mla_qkv_inputs(x, Wb, core)
    qkv.pop("x_fm")
    for k, v in qkv.items():
        m["mq_" + k] = v
    m["ma_wo"] = np.ascontiguousarray(np.asarray(Wb["Wo_w"], np.float32))
    m["ma_b_o"] = pack_vec(Wb["Wo_b"])

    Wa = {k: np.asarray(inp[f"L1_attn_a_{k}"], np.float32) for k in
          ["W_dkv", "W_uk", "W_uv", "W_dq", "W_uq", "W_qr", "W_kr", "W_out",
           "idx_wq", "idx_wk", "idx_w", "raw_delta"]}
    tp = host_topk_proj_inputs(x, Wa, core)
    tp.pop("x_fm")
    for k, v in tp.items():
        m["tp_" + k] = v
    for k, v in host_topk_aux(Wa, core).items():
        m["ta_" + k] = v
    m["tg_mA"] = mcol(1.0 if h == 0 else 0.0)
    m["tg_mB"] = mcol(0.0 if h == 0 else 1.0)
    m["hm"] = mcol(1.0 if h == 1 else 0.0)

    Wl = {k: inp[f"L1_ffn_{k}"] for k in ["W_r", "W_k", "W_v", "mu_r", "mu_k"]}
    m["lf_wrT"] = fmT(Wl["W_r"])
    m["lf_wkT"] = fmT(Wl["W_k"])
    m["lf_wvT"] = fmT(Wl["W_v"])
    m["lf_mu_r"] = pack_vec(Wl["mu_r"])
    m["lf_mu_k"] = pack_vec(Wl["mu_k"])

    Ww = {k: inp[f"L1_attn_b_{k}"] for k in
          ["mu_k", "mu_v", "mu_r", "Wk", "Wv", "Wr", "W_out", "w", "u"]}
    m["wk_wkT"] = fmT(Ww["Wk"])
    m["wk_wvT"] = fmT(Ww["Wv"])
    m["wk_wrT"] = fmT(Ww["Wr"])
    m["wk_mu_k"] = pack_vec(Ww["mu_k"])
    m["wk_mu_v"] = pack_vec(Ww["mu_v"])
    m["wk_mu_r"] = pack_vec(Ww["mu_r"])
    base_p = (2 * (b - 1) + h) * 1024 if b > 0 else 0
    m["wk_idxp_i"] = (base_p + np.arange(8, dtype=np.uint32)[None, :] * 128
                      + P[:, None]).astype(np.uint32)
    m["wk_mz_i"] = mcol(0.0 if b == 0 else 1.0)
    m["wo_woT"] = fmT(Ww["W_out"])
    m["wo_wv_"] = pack_vec(Ww["w"])
    m["wo_uv_"] = pack_vec(Ww["u"])
    steps8 = np.zeros((128, 8), np.float32)
    mask8 = np.zeros((128, 8), np.float32)
    for r in range(8):
        bp, hp = r // 2, r % 2
        if hp == h and bp <= b:
            steps8[:, r] = float(b - bp)
            mask8[:, r] = 1.0
    m["wo_steps"] = steps8
    m["wo_mask"] = mask8
    own = (2 * b + h) * 2048
    m["wo_idxo1_i"] = (own + np.arange(8, dtype=np.uint32)[None, :] * 128
                       + P[:, None]).astype(np.uint32)
    m["wo_idxo2_i"] = (own + 1024 + np.arange(8, dtype=np.uint32)[None, :] * 128
                       + P[:, None]).astype(np.uint32)
    return m


def kernel(**inp):
    nc = _prog("fused", build_fused)
    in_maps = [host_fused_inputs(inp, c) for c in range(N_CORES)]
    res = run_bass_kernel_spmd(nc, in_maps, core_ids=list(range(N_CORES)))
    out = np.empty((4, 4096, 1024), np.float32)
    for core in range(N_CORES):
        b, h = core // 2, core % 2
        out[b, h * 2048:(h + 1) * 2048] = res.results[core]["x_outF"].T
    return out


# version 05 (final)
# speedup vs baseline: 1.1247x; 1.1247x over previous
import numpy as np
import ml_dtypes
from contextlib import ExitStack

import concourse.bass as bass
import concourse.mybir as mybir
from concourse.bass import IndirectOffsetOnAxis
from concourse.bass_utils import run_bass_kernel_spmd

F32 = mybir.dt.float32
BF16 = mybir.dt.bfloat16
U32 = mybir.dt.uint32
I16 = mybir.dt.int16
U8 = mybir.dt.uint8
AF = mybir.ActivationFunctionType
ALU = mybir.AluOpType
AX = mybir.AxisListType

D_MODEL = 1024
NCH = D_MODEL // 128
T_CORE = 2048
N_CORES = 8
LN_EPS = 1e-5

BF = ml_dtypes.bfloat16

ENGINES = ("sync", "scalar", "vector", "tensor", "gpsimd")



def fmT(a):
    return np.ascontiguousarray(np.asarray(a, dtype=np.float32).T)


def pack_vec(v):
    return np.ascontiguousarray(np.asarray(v, dtype=np.float32).reshape(-1, 128).T)


def core_slice_tokens(x, core):
    b, h = core // 2, core % 2
    return np.asarray(x[b, h * 2048:(h + 1) * 2048])



class Seq:

    def __init__(self, nc):
        self.nc = nc
        self.steps = {e: [] for e in ENGINES}
        self.counts = {}
        self.waited = {e: {} for e in ENGINES}

    def _tok_list(self, wait):
        if wait is None:
            return [(s, c) for s, c in self.counts.items()]
        if isinstance(wait, tuple):
            return [wait]
        out = {}
        for s, c in wait:
            if c > out.get(s, 0):
                out[s] = c
        return list(out.items())

    def step(self, engine, fn, inc=1, wait=None, dma=False, sclass=None):
        stream = (engine, sclass if sclass else ('q' if dma else 'c'))
        toks = self._tok_list(wait)
        waits = []
        wt = self.waited[engine]
        for s, c in toks:
            assert c <= self.counts.get(s, 0), f"future wait {s}:{c}"
            if wt.get(s, 0) < c:
                waits.append((s, c))
                wt[s] = c
        self.counts[stream] = self.counts.get(stream, 0) + inc
        self.steps[engine].append((waits, fn, inc, stream))
        return (stream, self.counts[stream])

    def dma(self, out, in_, engine="sync", wait=None, **kw):
        return self.step(engine,
                         lambda e: e.dma_start(out=out, in_=in_, **kw),
                         inc=16, wait=wait, dma=True)

    def barrier(self):
        return [(s, c) for s, c in self.counts.items()]

    def build_into(self, block, sems):
        for ename in ENGINES:
            steps = self.steps[ename]
            if not steps:
                continue

            def make(steps):
                def prog(eng):
                    for waits, fn, inc, stream in steps:
                        for s, c in waits:
                            eng.wait_ge(sems[s], c)
                        ins = fn(eng)
                        if isinstance(ins, list):
                            ins = ins[-1]
                        ins.then_inc(sems[stream], inc)
                return prog
            getattr(block, ename)(make(steps))

    def finish(self, block, sems):
        toks = [(s, c) for s, c in self.counts.items()]
        wt = self.waited["sync"]

        def prog(eng):
            for s, c in toks:
                if wt.get(s, 0) < c or s == ("sync", "q"):
                    eng.wait_ge(sems[s], c)
        block.sync(prog)


def tmax(*toks):
    out = {}
    for t in toks:
        if t is None:
            continue
        items = [t] if isinstance(t, tuple) and len(t) == 2 and \
            isinstance(t[1], int) else t
        for s, c in items:
            if c > out.get(s, 0):
                out[s] = c
    return list(out.items())


class FusedIO(dict):

    def __init__(self, prefix, mapping, nprefix=None):
        super().__init__(mapping)
        self.prefix = prefix
        self.nprefix = nprefix or prefix
        self.auto = []


def _dram(nc, io, name, shape, dtype, kind):
    if io is None:
        return nc.dram_tensor(name, shape, dtype, kind=kind).ap()
    if name in io:
        return io[name]
    assert kind == "ExternalInput", f"unmapped non-input tensor {name} ({kind})"
    io.auto.append(name)
    ap = nc.dram_tensor(io.prefix + name, shape, dtype, kind=kind).ap()
    io[name] = ap
    return ap


def seq_program(build_fn):
    nc = bass.Bass("TRN2", target_bir_lowering=False, debug=False)
    sq = Seq(nc)
    with ExitStack() as es:
        build_fn(nc, sq, es)
        sems = {}
        for s in sq.counts:
            sems[s] = es.enter_context(nc.semaphore(f"s_{s[0]}_{s[1]}"))
        block = es.enter_context(nc.Block())
        sq.build_into(block, sems)
        sq.finish(block, sems)
    return nc



def emit_ln_fm(nc, sq, X_view, TT, xn, xn_off, scr, rows, psA, psB, ones_col,
               ones_row, load_chunk):
    raise NotImplementedError



def build_l1_ffn(io=None, emit_only=False):
    TE = T_CORE + 1
    NT, T = 4, 512

    def prog(nc, sq, es):
        x_ext = _dram(nc, io, "x_ext", [D_MODEL, TE], F32, "ExternalInput")
        wrT = _dram(nc, io, "wrT", [D_MODEL, D_MODEL], F32, "ExternalInput")
        wkT = _dram(nc, io, "wkT", [D_MODEL, D_MODEL], F32, "ExternalInput")
        wvT = _dram(nc, io, "wvT", [D_MODEL, D_MODEL], F32, "ExternalInput")
        mu_r = _dram(nc, io, "mu_r", [128, NCH], F32, "ExternalInput")
        mu_k = _dram(nc, io, "mu_k", [128, NCH], F32, "ExternalInput")
        x_out = _dram(nc, io, "x_out", [D_MODEL, T_CORE], F32, "ExternalOutput")

        xe = x_ext.rearrange("(c p) t -> p c t", p=128)
        xo = x_out.rearrange("(c p) t -> p c t", p=128)

        sb = lambda n, s, d=F32: es.enter_context(nc.sbuf_tensor(n, s, d))
        pt = lambda n, s, d=F32: es.enter_context(nc.psum_tensor(n, s, d))

        Xt = sb("Xt", [128, NCH * 512])
        xn = sb("xn", [128, NCH * TE], BF16)
        sq32 = sb("sq32", [128, 512])
        a_row = sb("a_row", [1, TE])
        b_row = sb("b_row", [1, TE])
        t_row = sb("t_row", [1, 512])
        wk_s = sb("wk_s", [128, NCH * D_MODEL], BF16)
        wr_s = sb("wr_s", [128, NCH * D_MODEL], BF16)
        wv_s = sb("wv_s", [128, NCH * D_MODEL], BF16)
        mur_s = sb("mur_s", [128, NCH])
        muk_s = sb("muk_s", [128, NCH])
        dscr = sb("dscr", [128, 512])
        oscr = sb("oscr", [128, 512])
        xk_s = sb("xk_s", [128, NCH * T], BF16)
        xr_s = sb("xr_s", [128, NCH * T], BF16)
        kact = sb("kact", [128, NCH * T], BF16)
        r_s = sb("r_s", [128, NCH * T])
        ones_col = sb("ones_col", [128, 1])
        ones_row = sb("ones_row", [1, 128])
        eps_row = sb("eps_row", [1, 1])

        ps_a = pt("ps_a", [1, 512])
        ps_b = pt("ps_b", [1, 512])
        ps_bc1 = pt("ps_bc1", [128, 512])
        ps_bc2 = pt("ps_bc2", [128, 512])
        ps_mm = pt("ps_mm", [128, 512])

        sq.step("gpsimd", lambda e: e.memset(ones_col[:], 1.0))
        sq.step("gpsimd", lambda e: e.memset(ones_row[:], 1.0))
        sq.step("gpsimd", lambda e: e.memset(eps_row[:], LN_EPS))
        for nm, dst in (("wkT", wk_s), ("wrT", wr_s), ("wvT", wv_s)):
            src = {"wkT": wkT, "wrT": wrT, "wvT": wvT}[nm]
            sq.dma(dst[:].rearrange("p (c d) -> p c d", c=NCH),
                   src.rearrange("(c p) d -> p c d", p=128), engine="gpsimd")
        sq.dma(mur_s[:], mu_r)
        sq.dma(muk_s[:], mu_k)

        c0 = 0
        while c0 < TE:
            w = min(512, TE - c0)
            sq.dma(Xt[:].rearrange("p (c t) -> p c t", c=NCH)[:, :, 0:w],
                   xe[:, :, c0:c0 + w],
                   **({"allow_slow_non_contiguous": True} if w < 8 else {}))
            sq.step("tensor", lambda e, w=w: [
                e.matmul(ps_a[:, 0:w], ones_col[:],
                         Xt[:, c * 512:c * 512 + w], start=(c == 0), stop=(c == NCH - 1))
                for c in range(NCH)])
            for c in range(NCH):
                sq.step("scalar", lambda e, c=c, w=w: e.activation(
                    sq32[:, 0:w], Xt[:, c * 512:c * 512 + w], AF.Square))
                sq.step("tensor", lambda e, c=c, w=w: e.matmul(
                    ps_b[:, 0:w], ones_col[:], sq32[:, 0:w],
                    start=(c == 0), stop=(c == NCH - 1)))
            sq.step("vector", lambda e, c0=c0, w=w: e.tensor_single_scalar(
                b_row[:, c0:c0 + w], ps_a[:, 0:w], 1.0 / D_MODEL, ALU.mult))
            sq.step("vector", lambda e, c0=c0, w=w: e.tensor_single_scalar(
                a_row[:, c0:c0 + w], ps_b[:, 0:w], 1.0 / D_MODEL, ALU.mult))
            sq.step("scalar", lambda e, c0=c0, w=w: e.activation(
                t_row[:, 0:w], b_row[:, c0:c0 + w], AF.Square))
            sq.step("vector", lambda e, c0=c0, w=w: e.tensor_sub(
                a_row[:, c0:c0 + w], a_row[:, c0:c0 + w], t_row[:, 0:w]))
            sq.step("scalar", lambda e, c0=c0, w=w: e.activation(
                a_row[:, c0:c0 + w], a_row[:, c0:c0 + w], AF.Ln, bias=eps_row[:]))
            sq.step("scalar", lambda e, c0=c0, w=w: e.activation(
                a_row[:, c0:c0 + w], a_row[:, c0:c0 + w], AF.Exp, scale=-0.5))
            sq.step("tensor", lambda e, c0=c0, w=w: e.matmul(
                ps_bc1[:, 0:w], ones_row[:], b_row[:, c0:c0 + w]))
            sq.step("tensor", lambda e, c0=c0, w=w: e.matmul(
                ps_bc2[:, 0:w], ones_row[:], a_row[:, c0:c0 + w]))
            for c in range(NCH):
                sq.step("vector", lambda e, c=c, w=w: e.tensor_sub(
                    dscr[:, 0:w], Xt[:, c * 512:c * 512 + w], ps_bc1[:, 0:w]))
                sq.step("vector", lambda e, c=c, c0=c0, w=w: e.tensor_mul(
                    xn[:, c * TE + c0:c * TE + c0 + w], dscr[:, 0:w], ps_bc2[:, 0:w]))
            c0 += w

        for i in range(NT):
            t0 = 1 + i * T
            for c in range(NCH):
                sq.step("vector", lambda e, c=c, t0=t0: e.tensor_sub(
                    dscr[:, 0:T], xn[:, c * TE + t0:c * TE + t0 + T],
                    xn[:, c * TE + t0 - 1:c * TE + t0 - 1 + T]))
                sq.step("vector", lambda e, c=c, t0=t0: e.scalar_tensor_tensor(
                    xk_s[:, c * T:(c + 1) * T], dscr[:, 0:T], muk_s[:, c:c + 1],
                    xn[:, c * TE + t0 - 1:c * TE + t0 - 1 + T], ALU.mult, ALU.add))
                sq.step("vector", lambda e, c=c, t0=t0: e.scalar_tensor_tensor(
                    xr_s[:, c * T:(c + 1) * T], dscr[:, 0:T], mur_s[:, c:c + 1],
                    xn[:, c * TE + t0 - 1:c * TE + t0 - 1 + T], ALU.mult, ALU.add))
            for d in range(NCH):
                sq.step("tensor", lambda e, d=d: [
                    e.matmul(ps_mm[:], wk_s[:, c * D_MODEL + d * 128:
                                            c * D_MODEL + (d + 1) * 128],
                             xk_s[:, c * T:(c + 1) * T],
                             start=(c == 0), stop=(c == NCH - 1))
                    for c in range(NCH)])
                sq.step("vector", lambda e, d=d: e.tensor_single_scalar(
                    dscr[:, 0:T], ps_mm[:], 0.0, ALU.max))
                sq.step("scalar", lambda e, d=d: e.activation(
                    kact[:, d * T:(d + 1) * T], dscr[:, 0:T], AF.Square))
            for d in range(NCH):
                sq.step("tensor", lambda e, d=d: [
                    e.matmul(ps_mm[:], wr_s[:, c * D_MODEL + d * 128:
                                            c * D_MODEL + (d + 1) * 128],
                             xr_s[:, c * T:(c + 1) * T],
                             start=(c == 0), stop=(c == NCH - 1))
                    for c in range(NCH)])
                sq.step("scalar", lambda e, d=d: e.activation(
                    r_s[:, d * T:(d + 1) * T], ps_mm[:], AF.Sigmoid))
            for d in range(NCH):
                sq.step("tensor", lambda e, d=d: [
                    e.matmul(ps_mm[:], wv_s[:, c * D_MODEL + d * 128:
                                            c * D_MODEL + (d + 1) * 128],
                             kact[:, c * T:(c + 1) * T],
                             start=(c == 0), stop=(c == NCH - 1))
                    for c in range(NCH)])
                sq.step("vector", lambda e, d=d: e.tensor_mul(
                    oscr[:, 0:T], r_s[:, d * T:(d + 1) * T], ps_mm[:]))
                sq.dma(Xt[:, 0:T], xe[:, d, t0:t0 + T])
                sq.step("vector", lambda e, d=d: e.tensor_add(
                    oscr[:, 0:T], oscr[:, 0:T], Xt[:, 0:T]))
                sq.dma(xo[:, d, i * T:(i + 1) * T], oscr[:, 0:T])

    if emit_only:
        return prog
    return seq_program(prog)


def host_l1_ffn_inputs(x, weights, core):
    b, h = core // 2, core % 2
    sl = core_slice_tokens(x, core)
    if h == 0:
        halo = np.zeros((1, D_MODEL), np.float32)
    else:
        halo = np.asarray(x[b, h * 2048 - 1:h * 2048])
    x_ext = fmT(np.concatenate([halo, sl], axis=0))
    return {
        "x_ext": x_ext,
        "wrT": fmT(weights["W_r"]),
        "wkT": fmT(weights["W_k"]),
        "wvT": fmT(weights["W_v"]),
        "mu_r": pack_vec(weights["mu_r"]),
        "mu_k": pack_vec(weights["mu_k"]),
    }


def unpack_fm_out(res_list, key="x_out"):
    out = np.empty((4, 4096, 1024), np.float32)
    for core in range(N_CORES):
        b, h = core // 2, core % 2
        out[b, h * 2048:(h + 1) * 2048] = res_list[core][key].T
    return out



def build_l0_ffn(io=None, emit_only=False):
    NEXP = 8
    HID = 4096
    NHT = HID // 128
    NHG = 4
    NT, T = 4, 512

    def prog(nc, sq, es):
        x_fm = _dram(nc, io, "x_fm", [D_MODEL, T_CORE], F32, "ExternalInput")
        router_wT = _dram(nc, io, "router_wT", [D_MODEL, 8], F32, "ExternalInput")
        w_gate = _dram(nc, io, "w_gate", [NEXP, D_MODEL, HID], F32, "ExternalInput")
        w_up = _dram(nc, io, "w_up", [NEXP, D_MODEL, HID], F32, "ExternalInput")
        w_down = _dram(nc, io, "w_down", [NEXP, HID, D_MODEL], F32, "ExternalInput")
        x_out = _dram(nc, io, "x_out", [D_MODEL, T_CORE], F32, "ExternalOutput")
        cmb_dram = _dram(nc, io, "cmb_dram", [8, T_CORE], F32, "Internal")

        xv = x_fm.rearrange("(c p) t -> p c t", p=128)
        xo = x_out.rearrange("(c p) t -> p c t", p=128)
        wg_v = w_gate.rearrange("e (c p) h -> e p c h", p=128)
        wu_v = w_up.rearrange("e (c p) h -> e p c h", p=128)
        wd_v = w_down.rearrange("e (ht p) d -> e p ht d", p=128)
        rw_v = router_wT.rearrange("(c p) e -> p c e", p=128)

        sb = lambda n, s, d=F32: es.enter_context(nc.sbuf_tensor(n, s, d))
        pt = lambda n, s, d=F32: es.enter_context(nc.psum_tensor(n, s, d))

        Xt = sb("Xt", [128, NCH * 512])
        xn = sb("xn", [128, NCH * T_CORE], BF16)
        sq32 = sb("sq32", [128, 512])
        a_row = sb("a_row", [1, 512])
        b_row = sb("b_row", [1, 512])
        t_row = sb("t_row", [1, 512])
        ones_col = sb("ones_col", [128, 1])
        ones_row = sb("ones_row", [1, 128])
        eps_row = sb("eps_row", [1, 1])
        rw_s = sb("rw_s", [128, NCH * 8], BF16)
        logit = sb("logit", [128, 8])
        m1 = sb("m1", [128, 1])
        m2 = sb("m2", [128, 1])
        nm1 = sb("nm1", [128, 1])
        e2 = sb("e2", [128, 1])
        rden = sb("rden", [128, 1])
        gemsk = sb("gemsk", [128, 8])
        expv = sb("expv", [128, 8])
        cmb = sb("cmb", [128, 8])
        neg_inf8 = sb("neg_inf8", [128, 8])
        msk8 = sb("msk8", [128, 8], U8)
        cmb_row1 = sb("cmb_row1", [1, 128])
        cmb_row = sb("cmb_row", [1, T_CORE])
        cmb_bc = sb("cmb_bc", [128, T_CORE])
        identw = sb("identw", [128, 128], F32)
        wgs = sb("wgs", [128, NCH * 1024], BF16)
        wus = sb("wus", [128, NCH * 1024], BF16)
        wds = sb("wds", [128, 8 * 1024], BF16)
        hact = sb("hact", [128, 8 * 512], BF16)
        silu_scr = sb("silu_scr", [128, 2 * 512])
        h_scr = sb("h_scr", [128, 2 * 512])
        mixed = sb("mixed", [128, NCH * T_CORE])
        oscr = sb("oscr", [128, 512])

        ps_a = pt("ps_a", [128, 512])
        ps_b = pt("ps_b", [128, 512])
        ps_bc1 = pt("ps_bc1", [128, 512])
        ps_bc2 = pt("ps_bc2", [128, 512])
        ps_g = pt("ps_g", [128, 512])
        ps_u = pt("ps_u", [128, 512])
        ps_dn = pt("ps_dn", [128, 512])
        ps_r = pt("ps_r", [128, 128])

        sq.step("gpsimd", lambda e: e.memset(ones_col[:], 1.0))
        sq.step("gpsimd", lambda e: e.memset(ones_row[:], 1.0))
        sq.step("gpsimd", lambda e: e.memset(eps_row[:], LN_EPS))
        sq.step("gpsimd", lambda e: e.memset(neg_inf8[:], -1e30))
        sq.step("gpsimd", lambda e: e.memset(identw[:], 1.0))
        sq.step("gpsimd", lambda e: e.affine_select(
            identw[:], identw[:], pattern=[[1, 128]], compare_op=ALU.is_equal,
            fill=0.0, base=0, channel_multiplier=-1))
        identb = sb("identb", [128, 128], BF16)
        sq.step("vector", lambda e: e.tensor_copy(identb[:], identw[:]))
        sq.dma(rw_s[:].rearrange("p (c e) -> p c e", c=NCH), rw_v, engine="gpsimd")

        for it in range(4):
            c0, w = it * 512, 512
            sq.dma(Xt[:].rearrange("p (c t) -> p c t", c=NCH), xv[:, :, c0:c0 + w])
            sq.step("tensor", lambda e, w=w: [
                e.matmul(ps_a[0:1, 0:w], ones_col[:], Xt[:, c * 512:c * 512 + w],
                         start=(c == 0), stop=(c == NCH - 1)) for c in range(NCH)])
            for c in range(NCH):
                sq.step("scalar", lambda e, c=c, w=w: e.activation(
                    sq32[:, 0:w], Xt[:, c * 512:c * 512 + w], AF.Square))
                sq.step("tensor", lambda e, c=c, w=w: e.matmul(
                    ps_b[0:1, 0:w], ones_col[:], sq32[:, 0:w],
                    start=(c == 0), stop=(c == NCH - 1)))
            sq.step("vector", lambda e, c0=c0, w=w: e.tensor_single_scalar(
                b_row[:, 0:w], ps_a[0:1, 0:w], 1.0 / D_MODEL, ALU.mult))
            sq.step("vector", lambda e, c0=c0, w=w: e.tensor_single_scalar(
                a_row[:, 0:w], ps_b[0:1, 0:w], 1.0 / D_MODEL, ALU.mult))
            sq.step("scalar", lambda e, c0=c0, w=w: e.activation(
                t_row[:, 0:w], b_row[:, 0:w], AF.Square))
            sq.step("vector", lambda e, c0=c0, w=w: e.tensor_sub(
                a_row[:, 0:w], a_row[:, 0:w], t_row[:, 0:w]))
            sq.step("scalar", lambda e, c0=c0, w=w: e.activation(
                a_row[:, 0:w], a_row[:, 0:w], AF.Ln, bias=eps_row[:]))
            sq.step("scalar", lambda e, c0=c0, w=w: e.activation(
                a_row[:, 0:w], a_row[:, 0:w], AF.Exp, scale=-0.5))
            sq.step("tensor", lambda e, c0=c0, w=w: e.matmul(
                ps_bc1[:, 0:w], ones_row[:], b_row[:, 0:w]))
            sq.step("tensor", lambda e, c0=c0, w=w: e.matmul(
                ps_bc2[:, 0:w], ones_row[:], a_row[:, 0:w]))
            for c in range(NCH):
                sq.step("vector", lambda e, c=c, w=w: e.tensor_sub(
                    sq32[:, 0:w], Xt[:, c * 512:c * 512 + w], ps_bc1[:, 0:w]))
                sq.step("vector", lambda e, c=c, c0=c0, w=w: e.tensor_mul(
                    xn[:, c * T_CORE + c0:c * T_CORE + c0 + w], sq32[:, 0:w],
                    ps_bc2[:, 0:w]))

        for q in range(T_CORE // 128):
            sq.step("tensor", lambda e, q=q: [
                e.matmul(ps_r[:, 0:8],
                         xn[:, c * T_CORE + q * 128:c * T_CORE + (q + 1) * 128],
                         rw_s[:, c * 8:(c + 1) * 8],
                         start=(c == 0), stop=(c == NCH - 1)) for c in range(NCH)])
            sq.step("vector", lambda e: e.tensor_copy(logit[:], ps_r[:, 0:8]))
            sq.step("vector", lambda e: e.reduce_max(m1[:], logit[:, 0:7], axis=AX.X))
            sq.step("vector", lambda e: e.tensor_scalar(
                msk8[:, 0:7], logit[:, 0:7], m1[:], None, ALU.is_ge))
            sq.step("vector", lambda e: e.select(
                sq32[:, 0:7], msk8[:, 0:7], neg_inf8[:, 0:7], logit[:, 0:7]))
            sq.step("vector", lambda e: e.reduce_max(m2[:], sq32[:, 0:7], axis=AX.X))
            sq.step("vector", lambda e: e.tensor_single_scalar(
                nm1[:], m1[:], -1.0, ALU.mult))
            sq.step("scalar", lambda e: e.activation(e2[:], m2[:], AF.Exp, bias=nm1[:]))
            sq.step("vector", lambda e: e.tensor_single_scalar(
                e2[:], e2[:], 1.0, ALU.add))
            sq.step("vector", lambda e: e.reciprocal(rden[:], e2[:]))
            sq.step("vector", lambda e: e.tensor_scalar(
                gemsk[:, 0:7], logit[:, 0:7], m2[:], None, ALU.is_ge))
            sq.step("scalar", lambda e: e.activation(
                expv[:, 0:7], logit[:, 0:7], AF.Exp, bias=nm1[:]))
            sq.step("vector", lambda e: e.tensor_mul(
                cmb[:, 0:7], expv[:, 0:7], gemsk[:, 0:7]))
            sq.step("vector", lambda e: e.tensor_scalar(
                cmb[:, 0:7], cmb[:, 0:7], rden[:], None, ALU.mult))
            sq.step("gpsimd", lambda e: e.memset(cmb[:, 7:8], 1.0))
            for ee in range(NEXP):
                sq.step("tensor", lambda e, ee=ee: e.transpose(
                    ps_r[0:1, 0:128], cmb[:, ee:ee + 1], identw[:]))
                sq.step("vector", lambda e: e.tensor_copy(
                    cmb_row1[:], ps_r[0:1, 0:128]))
                sq.dma(cmb_dram[ee:ee + 1, q * 128:(q + 1) * 128], cmb_row1[:])

        sq.step("gpsimd", lambda e: e.memset(mixed[:], 0.0))
        for ee in range(NEXP):
            sq.dma(cmb_row[:], cmb_dram[ee:ee + 1, :])
            for tt in range(NT):
                sq.step("tensor", lambda e, ee=ee, tt=tt: e.matmul(
                    ps_bc1[:], ones_row[:],
                    cmb_row[:, tt * T:(tt + 1) * T]))
                sq.step("vector", lambda e, tt=tt: e.tensor_copy(
                    cmb_bc[:, tt * T:(tt + 1) * T], ps_bc1[:]))
            for hg in range(NHG):
                sq.dma(wgs[:].rearrange("p (c h) -> p c h", c=NCH),
                       wg_v[ee, :, :, hg * 1024:(hg + 1) * 1024], engine="gpsimd")
                sq.dma(wus[:].rearrange("p (c h) -> p c h", c=NCH),
                       wu_v[ee, :, :, hg * 1024:(hg + 1) * 1024], engine="gpsimd")
                sq.dma(wds[:].rearrange("p (ht d) -> p ht d", ht=8),
                       wd_v[ee, :, hg * 8:(hg + 1) * 8, :], engine="gpsimd")
                t_entry = sq.barrier()
                t_m2p = {}
                t_dn_last = None
                t_dnevac = None
                for tt in range(NT):
                    t_m2_all = []
                    for ht in range(8):
                        pv = ht % 2
                        pg = ps_g if pv == 0 else ps_a
                        pu = ps_u if pv == 0 else ps_b
                        ss = slice(pv * 512, pv * 512 + 512)
                        wv_ = tmax(t_entry, t_m2p.get((pv), None))
                        t_g = sq.step("tensor", lambda e, ht=ht, tt=tt, pg=pg: [
                            e.matmul(pg[:],
                                     wgs[:, c * 1024 + ht * 128:c * 1024 + (ht + 1) * 128],
                                     xn[:, c * T_CORE + tt * T:c * T_CORE + tt * T + T],
                                     start=(c == 0), stop=(c == NCH - 1))
                            for c in range(NCH)], wait=wv_)
                        t_u = sq.step("tensor", lambda e, ht=ht, tt=tt, pu=pu: [
                            e.matmul(pu[:],
                                     wus[:, c * 1024 + ht * 128:c * 1024 + (ht + 1) * 128],
                                     xn[:, c * T_CORE + tt * T:c * T_CORE + tt * T + T],
                                     start=(c == 0), stop=(c == NCH - 1))
                            for c in range(NCH)], wait=wv_)
                        t_s = sq.step("scalar", lambda e, pg=pg, ss=ss: e.activation(
                            silu_scr[:, ss], pg[:], AF.Silu), wait=t_g)
                        t_m1 = sq.step("vector", lambda e, pu=pu, ss=ss: e.tensor_mul(
                            h_scr[:, ss], silu_scr[:, ss], pu[:]),
                            wait=tmax(t_s, t_u))
                        t_m2 = sq.step("vector", lambda e, ht=ht, tt=tt, ss=ss:
                            e.tensor_mul(
                                hact[:, ht * 512:(ht + 1) * 512], h_scr[:, ss],
                                cmb_bc[:, tt * T:(tt + 1) * T]),
                            wait=tmax(t_m1, t_dn_last))
                        t_m2p[pv] = t_m2
                        t_m2_all.append(t_m2)
                    for dm in range(NCH):
                        t_dn = sq.step("tensor", lambda e, dm=dm: [
                            e.matmul(ps_dn[:],
                                     wds[:, ht * 1024 + dm * 128:ht * 1024 + (dm + 1) * 128],
                                     hact[:, ht * 512:(ht + 1) * 512],
                                     start=(ht == 0), stop=(ht == 7))
                            for ht in range(8)],
                            wait=tmax(t_m2_all[-1], t_dnevac))
                        t_dnevac = sq.step("vector", lambda e, dm=dm, tt=tt: e.tensor_add(
                            mixed[:, dm * T_CORE + tt * T:dm * T_CORE + tt * T + T],
                            mixed[:, dm * T_CORE + tt * T:dm * T_CORE + tt * T + T],
                            ps_dn[:]), wait=t_dn)
                        if dm == NCH - 1:
                            t_dn_last = t_dnevac

        for tt in range(NT):
            sq.dma(Xt[:].rearrange("p (c t) -> p c t", c=NCH), xv[:, :, tt * T:(tt + 1) * T])
            for c in range(NCH):
                sq.step("vector", lambda e, c=c, tt=tt: e.tensor_add(
                    oscr[:], mixed[:, c * T_CORE + tt * T:c * T_CORE + tt * T + T],
                    Xt[:, c * 512:(c + 1) * 512]))
                sq.dma(xo[:, c, tt * T:(tt + 1) * T], oscr[:])

    if emit_only:
        return prog
    return seq_program(prog)


def host_l0_ffn_inputs(x, weights, core):
    sl = core_slice_tokens(x, core)
    rw = np.full((8, D_MODEL), -1e30, np.float32)
    rw[0:7] = np.asarray(weights["router_w"], np.float32)
    wg = np.concatenate([np.asarray(weights["w_gate"], np.float32),
                         np.asarray(weights["sw_gate"], np.float32)], axis=0)
    wu = np.concatenate([np.asarray(weights["w_up"], np.float32),
                         np.asarray(weights["sw_up"], np.float32)], axis=0)
    wd = np.concatenate([np.asarray(weights["w_down"], np.float32),
                         np.asarray(weights["sw_down"], np.float32)], axis=0)
    return {
        "x_fm": fmT(sl),
        "router_wT": fmT(rw),
        "w_gate": np.ascontiguousarray(wg),
        "w_up": np.ascontiguousarray(wu),
        "w_down": np.ascontiguousarray(wd),
    }



D_INNER = 2048
NEIN = D_INNER // 128
D_STATE = 16


def build_mamba_pre(io=None, emit_only=False):
    TE = T_CORE + 3
    COLT = [(0, 512), (512, 512), (1024, 512), (1536, 512), (2048, 3)]
    TT4 = [(0, 512), (512, 512), (1024, 512), (1536, 512)]

    def prog(nc, sq, es):
        x_ext = _dram(nc, io, "x_ext", [D_MODEL, TE], F32, "ExternalInput")
        in_projT = _dram(nc, io, "in_projT", [D_MODEL, 2 * D_INNER], F32, "ExternalInput")
        x_projT = _dram(nc, io, "x_projT", [D_INNER, 2080], F32, "ExternalInput")
        dt_projT = _dram(nc, io, "dt_projT", [D_INNER, D_INNER], F32, "ExternalInput")
        conv_w = _dram(nc, io, "conv_w", [128, 64], F32, "ExternalInput")
        conv_b = _dram(nc, io, "conv_b", [128, NEIN], F32, "ExternalInput")
        dt_proj_b = _dram(nc, io, "dt_proj_b", [128, NEIN], F32, "ExternalInput")
        mex = _dram(nc, io, "mex", [2 * D_INNER + 32, T_CORE], BF16,
                    "ExternalOutput")
        zs_o = _dram(nc, io, "zs_o", [D_INNER, T_CORE], F32, "ExternalOutput")
        dt_o = mex[0:D_INNER, :]
        xin_o = mex[D_INNER:2 * D_INNER, :]
        B_o = mex[2 * D_INNER:2 * D_INNER + 16, :]
        C_o = mex[2 * D_INNER + 16:2 * D_INNER + 32, :]

        xe = x_ext.rearrange("(c p) t -> p c t", p=128)
        ipv = in_projT.rearrange("(c p) e -> p c e", p=128)
        xpv = x_projT.rearrange("(c p) e -> p c e", p=128)
        dpv = dt_projT.rearrange("(c p) e -> p c e", p=128)
        dt_v = dt_o.rearrange("(c p) t -> p c t", p=128)
        xin_v = xin_o.rearrange("(c p) t -> p c t", p=128)
        zs_v = zs_o.rearrange("(c p) t -> p c t", p=128)

        sb = lambda n, s, d=F32: es.enter_context(nc.sbuf_tensor(n, s, d))
        pt = lambda n, s, d=F32: es.enter_context(nc.psum_tensor(n, s, d))

        ones_col = sb("ones_col", [128, 1])
        ones_row = sb("ones_row", [1, 128])
        eps_row = sb("eps_row", [1, 1])
        Xt = sb("Xt", [128, NCH * 512])
        sq32 = sb("sq32", [128, 512])
        scr2 = sb("scr2", [128, 512])
        a_row = sb("a_row", [1, 512])
        b_row = sb("b_row", [1, 512])
        t_row = sb("t_row", [1, 512])
        xn = sb("xn", [128, NCH * TE], BF16)
        wsl = sb("wsl", [128, NCH * 128], BF16)
        wsl2 = sb("wsl2", [128, NEIN * 128], BF16)
        xraw = sb("xraw", [128, TE], BF16)
        xconv = sb("xconv", [128, NEIN * T_CORE], BF16)
        din = sb("din", [128, NEIN * T_CORE], BF16)
        convw_s = sb("convw_s", [128, 64])
        convb_s = sb("convb_s", [128, NEIN])
        dtb_s = sb("dtb_s", [128, NEIN])
        bc_scr = sb("bc_scr", [16, 512])

        ps_a = pt("ps_a", [1, 512])
        ps_b = pt("ps_b", [1, 512])
        ps_bc1 = pt("ps_bc1", [128, 512])
        ps_bc2 = pt("ps_bc2", [128, 512])
        ps_mm = pt("ps_mm", [128, 512])
        ps_sm = pt("ps_sm", [16, 512])

        sq.step("gpsimd", lambda e: e.memset(ones_col[:], 1.0))
        sq.step("gpsimd", lambda e: e.memset(ones_row[:], 1.0))
        sq.step("gpsimd", lambda e: e.memset(eps_row[:], LN_EPS))
        sq.dma(convw_s[:], conv_w)
        sq.dma(convb_s[:], conv_b)
        sq.dma(dtb_s[:], dt_proj_b)

        for (c0, w) in COLT:
            slow = {"allow_slow_non_contiguous": True} if w < 8 else {}
            sq.dma(Xt[:].rearrange("p (c t) -> p c t", c=NCH)[:, :, 0:w],
                   xe[:, :, c0:c0 + w], **slow)
            sq.step("tensor", lambda e, w=w: [
                e.matmul(ps_a[:, 0:w], ones_col[:], Xt[:, c * 512:c * 512 + w],
                         start=(c == 0), stop=(c == NCH - 1)) for c in range(NCH)])
            for c in range(NCH):
                sq.step("scalar", lambda e, c=c, w=w: e.activation(
                    sq32[:, 0:w], Xt[:, c * 512:c * 512 + w], AF.Square))
                sq.step("tensor", lambda e, c=c, w=w: e.matmul(
                    ps_b[:, 0:w], ones_col[:], sq32[:, 0:w],
                    start=(c == 0), stop=(c == NCH - 1)))
            sq.step("vector", lambda e, w=w: e.tensor_single_scalar(
                b_row[:, 0:w], ps_a[:, 0:w], 1.0 / D_MODEL, ALU.mult))
            sq.step("vector", lambda e, w=w: e.tensor_single_scalar(
                a_row[:, 0:w], ps_b[:, 0:w], 1.0 / D_MODEL, ALU.mult))
            sq.step("scalar", lambda e, w=w: e.activation(
                t_row[:, 0:w], b_row[:, 0:w], AF.Square))
            sq.step("vector", lambda e, w=w: e.tensor_sub(
                a_row[:, 0:w], a_row[:, 0:w], t_row[:, 0:w]))
            sq.step("scalar", lambda e, w=w: e.activation(
                a_row[:, 0:w], a_row[:, 0:w], AF.Ln, bias=eps_row[:]))
            sq.step("scalar", lambda e, w=w: e.activation(
                a_row[:, 0:w], a_row[:, 0:w], AF.Exp, scale=-0.5))
            sq.step("tensor", lambda e, w=w: e.matmul(
                ps_bc1[:, 0:w], ones_row[:], b_row[:, 0:w]))
            sq.step("tensor", lambda e, w=w: e.matmul(
                ps_bc2[:, 0:w], ones_row[:], a_row[:, 0:w]))
            for c in range(NCH):
                sq.step("vector", lambda e, c=c, w=w: e.tensor_sub(
                    sq32[:, 0:w], Xt[:, c * 512:c * 512 + w], ps_bc1[:, 0:w]))
                sq.step("vector", lambda e, c=c, c0=c0, w=w: e.tensor_mul(
                    xn[:, c * TE + c0:c * TE + c0 + w], sq32[:, 0:w], ps_bc2[:, 0:w]))

        for et in range(NEIN):
            sq.dma(wsl[:].rearrange("p (c d) -> p c d", c=NCH),
                   ipv[:, :, et * 128:(et + 1) * 128], engine="gpsimd")
            for (c0, w) in COLT:
                sq.step("tensor", lambda e, et=et, c0=c0, w=w: [
                    e.matmul(ps_mm[:, 0:w], wsl[:, c * 128:(c + 1) * 128],
                             xn[:, c * TE + c0:c * TE + c0 + w],
                             start=(c == 0), stop=(c == NCH - 1)) for c in range(NCH)])
                sq.step("vector", lambda e, c0=c0, w=w: e.tensor_copy(
                    xraw[:, c0:c0 + w], ps_mm[:, 0:w]))
            for (t0, w) in TT4:
                sq.step("vector", lambda e, et=et, t0=t0, w=w: e.tensor_scalar(
                    sq32[:, 0:w], xraw[:, t0:t0 + w], convw_s[:, et * 4:et * 4 + 1],
                    None, ALU.mult))
                for j in range(1, 4):
                    sq.step("vector", lambda e, et=et, t0=t0, w=w, j=j:
                        e.scalar_tensor_tensor(
                            sq32[:, 0:w], xraw[:, t0 + j:t0 + j + w],
                            convw_s[:, et * 4 + j:et * 4 + j + 1],
                            sq32[:, 0:w], ALU.mult, ALU.add))
                sq.step("scalar", lambda e, et=et, t0=t0, w=w: e.activation(
                    xconv[:, et * T_CORE + t0:et * T_CORE + t0 + w], sq32[:, 0:w],
                    AF.Silu, bias=convb_s[:, et:et + 1]))
                sq.step("vector", lambda e, et=et, t0=t0, w=w: e.tensor_copy(
                    scr2[:, 0:w], xconv[:, et * T_CORE + t0:et * T_CORE + t0 + w]))
                sq.dma(xin_v[:, et, t0:t0 + w], scr2[:, 0:w], engine="gpsimd")

        for et in range(NEIN):
            sq.dma(wsl[:].rearrange("p (c d) -> p c d", c=NCH),
                   ipv[:, :, D_INNER + et * 128:D_INNER + (et + 1) * 128],
                   engine="gpsimd")
            for (t0, w) in TT4:
                sq.step("tensor", lambda e, t0=t0, w=w: [
                    e.matmul(ps_mm[:, 0:w], wsl[:, c * 128:(c + 1) * 128],
                             xn[:, c * TE + 3 + t0:c * TE + 3 + t0 + w],
                             start=(c == 0), stop=(c == NCH - 1)) for c in range(NCH)])
                sq.step("scalar", lambda e, w=w: e.activation(
                    scr2[:, 0:w], ps_mm[:, 0:w], AF.Silu))
                sq.dma(zs_v[:, et, t0:t0 + w], scr2[:, 0:w])

        for et in range(NEIN):
            sq.dma(wsl2[:].rearrange("p (c d) -> p c d", c=NEIN),
                   xpv[:, :, et * 128:(et + 1) * 128], engine="gpsimd")
            for (t0, w) in TT4:
                sq.step("tensor", lambda e, t0=t0, w=w: [
                    e.matmul(ps_mm[:, 0:w], wsl2[:, c * 128:(c + 1) * 128],
                             xconv[:, c * T_CORE + t0:c * T_CORE + t0 + w],
                             start=(c == 0), stop=(c == NEIN - 1)) for c in range(NEIN)])
                sq.step("vector", lambda e, et=et, t0=t0, w=w: e.tensor_copy(
                    din[:, et * T_CORE + t0:et * T_CORE + t0 + w], ps_mm[:, 0:w]))
        for bi, out_t in ((0, B_o), (1, C_o)):
            sq.dma(wsl2[:, 0:NEIN * 16].rearrange("p (c d) -> p c d", c=NEIN),
                   xpv[:, :, D_INNER + bi * 16:D_INNER + (bi + 1) * 16],
                   engine="gpsimd")
            for (t0, w) in TT4:
                sq.step("tensor", lambda e, t0=t0, w=w: [
                    e.matmul(ps_sm[:, 0:w], wsl2[:, c * 16:(c + 1) * 16],
                             xconv[:, c * T_CORE + t0:c * T_CORE + t0 + w],
                             start=(c == 0), stop=(c == NEIN - 1)) for c in range(NEIN)])
                sq.step("vector", lambda e, w=w: e.tensor_copy(
                    bc_scr[:, 0:w], ps_sm[:, 0:w]))
                sq.dma(out_t[:, t0:t0 + w], bc_scr[:, 0:w], engine="gpsimd")

        for et in range(NEIN):
            sq.dma(wsl2[:].rearrange("p (c d) -> p c d", c=NEIN),
                   dpv[:, :, et * 128:(et + 1) * 128], engine="gpsimd")
            for (t0, w) in TT4:
                sq.step("tensor", lambda e, t0=t0, w=w: [
                    e.matmul(ps_mm[:, 0:w], wsl2[:, c * 128:(c + 1) * 128],
                             din[:, c * T_CORE + t0:c * T_CORE + t0 + w],
                             start=(c == 0), stop=(c == NEIN - 1)) for c in range(NEIN)])
                sq.step("vector", lambda e, et=et, w=w: e.tensor_scalar(
                    sq32[:, 0:w], ps_mm[:, 0:w], dtb_s[:, et:et + 1], None, ALU.add))
                sq.step("scalar", lambda e, w=w: e.activation(
                    scr2[:, 0:w], sq32[:, 0:w], AF.Abs))
                sq.step("scalar", lambda e, w=w: e.activation(
                    scr2[:, 0:w], scr2[:, 0:w], AF.Exp, scale=-1.0))
                sq.step("scalar", lambda e, w=w: e.activation(
                    scr2[:, 0:w], scr2[:, 0:w], AF.Ln, bias=1.0))
                sq.step("vector", lambda e, w=w: e.tensor_single_scalar(
                    sq32[:, 0:w], sq32[:, 0:w], 0.0, ALU.max))
                sq.step("vector", lambda e, w=w: e.tensor_add(
                    sq32[:, 0:w], sq32[:, 0:w], scr2[:, 0:w]))
                sq.dma(dt_v[:, et, t0:t0 + w], sq32[:, 0:w], engine="gpsimd")

    if emit_only:
        return prog
    return seq_program(prog)


def build_mamba_scan(io=None, emit_only=False):
    L = 4096
    NDT = 8
    NTC = 8

    def prog(nc, sq, es):
        mexg = _dram(nc, io, "mexg", [2 * 4128, T_CORE], BF16, "ExternalInput")
        mA_i = _dram(nc, io, "mA_i", [128, 1], F32, "ExternalInput")
        mB_i = _dram(nc, io, "mB_i", [128, 1], F32, "ExternalInput")
        Alog_i = _dram(nc, io, "Alog_i", [128, NDT * D_STATE], F32, "ExternalInput")
        D_i = _dram(nc, io, "D_i", [128, NDT], F32, "ExternalInput")
        ymy = _dram(nc, io, "ymy", [2048, T_CORE], BF16, "ExternalOutput")

        def mex_view(rank, part, hh):
            base = rank * 4128 + part * 2048 + hh * 1024
            return mexg[base:base + 1024, :].rearrange("(c p) t -> p c t", p=128)

        ymy_v = ymy.rearrange("(hh c p) t -> hh p c t", p=128, hh=2)

        sb = lambda n, s, d=F32: es.enter_context(nc.sbuf_tensor(n, s, d))
        pt = lambda n, s, d=F32: es.enter_context(nc.psum_tensor(n, s, d))

        ones_row = sb("ones_row", [1, 128])
        dt_s = sb("dt_s", [128, NDT * 512])
        xin_s = sb("xin_s", [128, NDT * 512])
        dtx = sb("dtx", [128, NDT * 512])
        y_s = sb("y_s", [128, NDT * 512])
        A_s = sb("A_s", [128, NDT * D_STATE])
        D_s = sb("D_s", [128, NDT])
        h_st = sb("h_st", [128, NDT * D_STATE])
        B_row = sb("B_row", [1, 512])
        C_row = sb("C_row", [1, 512])
        Brep = sb("Brep", [128, 512])
        Crep = sb("Crep", [128, 512])
        dA = sb("dA", [128, 512])
        u_s = sb("u_s", [128, 512])
        hs = sb("hs", [128, 512])
        t1 = sb("t1", [128, 512])

        ps_b1 = pt("ps_b1", [128, 512])
        ps_b2 = pt("ps_b2", [128, 512])

        sq.step("gpsimd", lambda e: e.memset(ones_row[:], 1.0))
        sq.step("gpsimd", lambda e: e.memset(h_st[:], 0.0))
        sq.dma(A_s[:], Alog_i)
        sq.dma(D_s[:], D_i)
        sq.step("scalar", lambda e: e.activation(A_s[:], A_s[:], AF.Exp))
        sq.step("vector", lambda e: e.tensor_single_scalar(A_s[:], A_s[:], -1.0, ALU.mult))

        cnd0 = es.enter_context(nc.sbuf_tensor("cnd0", [128, NDT * 512], BF16))
        cnd1 = es.enter_context(nc.sbuf_tensor("cnd1", [128, NDT * 512], BF16))
        mA_s = es.enter_context(nc.sbuf_tensor("mA_s", [128, 1], F32))
        mB_s = es.enter_context(nc.sbuf_tensor("mB_s", [128, 1], F32))
        blsc = es.enter_context(nc.sbuf_tensor("blsc", [128, 512], F32))
        sq.dma(mA_s[:], mA_i)
        sq.dma(mB_s[:], mB_i)
        for tc in range(NTC):
            t0 = tc * 512
            rank, t0l = tc // 4, (tc % 4) * 512
            for part, dst in ((0, dt_s), (1, xin_s)):
                sq.dma(cnd0[:].rearrange("p (c t) -> p c t", c=NDT),
                       mex_view(rank, part, 0)[:, :, t0l:t0l + 512])
                sq.dma(cnd1[:].rearrange("p (c t) -> p c t", c=NDT),
                       mex_view(rank, part, 1)[:, :, t0l:t0l + 512])
                for d in range(NDT):
                    sl = slice(d * 512, (d + 1) * 512)
                    sq.step("vector", lambda e, sl=sl: e.tensor_scalar(
                        blsc[:], cnd0[:, sl], mA_s[:], None, ALU.mult))
                    sq.step("vector", lambda e, sl=sl, dst=dst: e.scalar_tensor_tensor(
                        dst[:, sl], cnd1[:, sl], mB_s[:], blsc[:],
                        ALU.mult, ALU.add))
            for d in range(NDT):
                sq.step("vector", lambda e, d=d: e.tensor_mul(
                    dtx[:, d * 512:(d + 1) * 512], dt_s[:, d * 512:(d + 1) * 512],
                    xin_s[:, d * 512:(d + 1) * 512]))
            sq.step("gpsimd", lambda e: e.memset(y_s[:], 0.0))
            for s in range(D_STATE):
                rb = rank * 4128 + 4096
                sq.dma(B_row[:], mexg[rb + s:rb + s + 1, t0l:t0l + 512],
                       engine="gpsimd")
                sq.dma(C_row[:], mexg[rb + 16 + s:rb + 16 + s + 1, t0l:t0l + 512],
                       engine="gpsimd")
                sq.step("tensor", lambda e: e.matmul(ps_b1[:], ones_row[:], B_row[:]))
                sq.step("tensor", lambda e: e.matmul(ps_b2[:], ones_row[:], C_row[:]))
                sq.step("vector", lambda e: e.tensor_copy(Brep[:], ps_b1[:]))
                sq.step("vector", lambda e: e.tensor_copy(Crep[:], ps_b2[:]))
                for d in range(NDT):
                    col = d * D_STATE + s
                    sq.step("scalar", lambda e, d=d, col=col: e.activation(
                        dA[:], dt_s[:, d * 512:(d + 1) * 512], AF.Exp,
                        scale=A_s[:, col:col + 1]))
                    sq.step("vector", lambda e, d=d: e.tensor_mul(
                        u_s[:], dtx[:, d * 512:(d + 1) * 512], Brep[:]))
                    sq.step("vector", lambda e, col=col: e.tensor_tensor_scan(
                        hs[:], dA[:], u_s[:], h_st[:, col:col + 1],
                        ALU.mult, ALU.add))
                    sq.step("vector", lambda e, col=col: e.tensor_copy(
                        h_st[:, col:col + 1], hs[:, 511:512]))
                    sq.step("vector", lambda e: e.tensor_mul(t1[:], hs[:], Crep[:]))
                    sq.step("vector", lambda e, d=d: e.tensor_add(
                        y_s[:, d * 512:(d + 1) * 512],
                        y_s[:, d * 512:(d + 1) * 512], t1[:]))
            for d in range(NDT):
                sq.step("vector", lambda e, d=d: e.scalar_tensor_tensor(
                    y_s[:, d * 512:(d + 1) * 512], xin_s[:, d * 512:(d + 1) * 512],
                    D_s[:, d:d + 1], y_s[:, d * 512:(d + 1) * 512],
                    ALU.mult, ALU.add))
                sq.dma(ymy_v[tc // 4, :, d, t0l:t0l + 512],
                       y_s[:, d * 512:(d + 1) * 512], engine="gpsimd")

    if emit_only:
        return prog
    return seq_program(prog)


def build_mamba_post(io=None, emit_only=False):
    NT, T = 4, 512

    def prog(nc, sq, es):
        x_fm = _dram(nc, io, "x_fm", [D_MODEL, T_CORE], F32, "ExternalInput")
        yag = _dram(nc, io, "yag", [2 * 2048, T_CORE], BF16, "ExternalInput")
        mA_i = _dram(nc, io, "mA_i", [128, 1], F32, "ExternalInput")
        mB_i = _dram(nc, io, "mB_i", [128, 1], F32, "ExternalInput")
        zs_fm = _dram(nc, io, "zs_fm", [D_INNER, T_CORE], F32, "ExternalInput")
        opT = _dram(nc, io, "opT", [D_INNER, D_MODEL], F32, "ExternalInput")
        x_out = _dram(nc, io, "x_out", [D_MODEL, T_CORE], F32, "ExternalOutput")

        xv = x_fm.rearrange("(c p) t -> p c t", p=128)
        yagv = yag.rearrange("(r hh c p) t -> r hh p c t", r=2, hh=2, p=128)
        zv = zs_fm.rearrange("(c p) t -> p c t", p=128)
        ov = opT.rearrange("(c p) d -> p c d", p=128)
        xo = x_out.rearrange("(c p) t -> p c t", p=128)

        sb = lambda n, s, d=F32: es.enter_context(nc.sbuf_tensor(n, s, d))
        pt = lambda n, s, d=F32: es.enter_context(nc.psum_tensor(n, s, d))

        w_s = sb("w_s", [128, NEIN * D_MODEL], BF16)
        y_t = sb("y_t", [128, NEIN * T])
        ycn0 = sb("ycn0", [128, NEIN * T], BF16)
        ycn1 = sb("ycn1", [128, NEIN * T], BF16)
        mA_s = sb("mA_s", [128, 1])
        mB_s = sb("mB_s", [128, 1])
        ybl = sb("ybl", [128, T])
        z_t = sb("z_t", [128, NEIN * T])
        g_t = sb("g_t", [128, NEIN * T], BF16)
        Xt = sb("Xt", [128, NCH * T])
        oscr = sb("oscr", [128, T])
        ps_mm = pt("ps_mm", [128, 512])

        sq.dma(w_s[:].rearrange("p (c d) -> p c d", c=NEIN), ov, engine="gpsimd")
        sq.dma(mA_s[:], mA_i)
        sq.dma(mB_s[:], mB_i)
        for tt in range(NT):
            t0 = tt * T
            for r in range(2):
                sq.dma(ycn0[:].rearrange("p (c t) -> p c t", c=NEIN)[:, r * 8:(r + 1) * 8],
                       yagv[r, 0, :, :, t0:t0 + T])
                sq.dma(ycn1[:].rearrange("p (c t) -> p c t", c=NEIN)[:, r * 8:(r + 1) * 8],
                       yagv[r, 1, :, :, t0:t0 + T])
            for c in range(NEIN):
                sl = slice(c * T, (c + 1) * T)
                sq.step("vector", lambda e, sl=sl: e.tensor_scalar(
                    ybl[:], ycn0[:, sl], mA_s[:], None, ALU.mult))
                sq.step("vector", lambda e, sl=sl: e.scalar_tensor_tensor(
                    y_t[:, sl], ycn1[:, sl], mB_s[:], ybl[:], ALU.mult, ALU.add))
            sq.dma(z_t[:].rearrange("p (c t) -> p c t", c=NEIN), zv[:, :, t0:t0 + T])
            sq.dma(Xt[:].rearrange("p (c t) -> p c t", c=NCH), xv[:, :, t0:t0 + T])
            for c in range(NEIN):
                sq.step("vector", lambda e, c=c: e.tensor_mul(
                    g_t[:, c * T:(c + 1) * T], y_t[:, c * T:(c + 1) * T],
                    z_t[:, c * T:(c + 1) * T]))
            for d in range(NCH):
                sq.step("tensor", lambda e, d=d: [
                    e.matmul(ps_mm[:], w_s[:, c * D_MODEL + d * 128:
                                           c * D_MODEL + (d + 1) * 128],
                             g_t[:, c * T:(c + 1) * T],
                             start=(c == 0), stop=(c == NEIN - 1))
                    for c in range(NEIN)])
                sq.step("vector", lambda e, d=d: e.tensor_add(
                    oscr[:], ps_mm[:], Xt[:, d * T:(d + 1) * T]))
                sq.dma(xo[:, d, t0:t0 + T], oscr[:])

    if emit_only:
        return prog
    return seq_program(prog)


def host_mamba_pre_inputs(x, W, core):
    b, h = core // 2, core % 2
    sl = core_slice_tokens(x, core)
    if h == 0:
        halo = np.zeros((3, D_MODEL), np.float32)
    else:
        halo = np.asarray(x[b, h * 2048 - 3:h * 2048])
    x_ext = fmT(np.concatenate([halo, sl], axis=0))
    cw = np.asarray(W["conv_w"], np.float32).reshape(D_INNER, 4)
    cw_p = np.ascontiguousarray(cw.reshape(NEIN, 128, 4).transpose(1, 0, 2).reshape(128, 64))
    return {
        "x_ext": x_ext,
        "in_projT": fmT(W["in_proj_w"]),
        "x_projT": fmT(W["x_proj_w"]),
        "dt_projT": fmT(W["dt_proj_w"]),
        "conv_w": cw_p,
        "conv_b": pack_vec(W["conv_b"]),
        "dt_proj_b": pack_vec(W["dt_proj_b"]),
    }


def host_mamba_scan_inputs(pre_outs, W, core):
    b, j = core // 2, core % 2
    def full(key, rows):
        return np.concatenate([pre_outs[2 * b][key], pre_outs[2 * b + 1][key]], axis=1)
    dt_f = full("dt_o", D_INNER)[j * 1024:(j + 1) * 1024]
    xin_f = full("xin_o", D_INNER)[j * 1024:(j + 1) * 1024]
    B_f = full("B_o", D_STATE)
    C_f = full("C_o", D_STATE)
    Alog = np.asarray(W["A_log"], np.float32)[j * 1024:(j + 1) * 1024]
    Alog_p = np.ascontiguousarray(
        Alog.reshape(8, 128, D_STATE).transpose(1, 0, 2).reshape(128, 8 * D_STATE))
    Dv = np.asarray(W["D"], np.float32)[j * 1024:(j + 1) * 1024]
    return {
        "dt_i": np.ascontiguousarray(dt_f), "xin_i": np.ascontiguousarray(xin_f),
        "B_i": np.ascontiguousarray(B_f), "C_i": np.ascontiguousarray(C_f),
        "Alog_i": Alog_p, "D_i": pack_vec(Dv),
    }


def host_mamba_post_inputs(x, scan_outs, pre_outs, W, core):
    b, h = core // 2, core % 2
    y_full = np.concatenate([scan_outs[2 * b]["y_o"], scan_outs[2 * b + 1]["y_o"]],
                            axis=0)
    y_slice = y_full[:, h * 2048:(h + 1) * 2048]
    zs = pre_outs[core]["zs_o"]
    return {
        "x_fm": fmT(core_slice_tokens(x, core)),
        "y_fm": np.ascontiguousarray(y_slice),
        "zs_fm": np.ascontiguousarray(zs),
        "opT": fmT(W["out_proj_w"]),
    }



def build_wkv_kv(io=None, emit_only=False):
    NT, T = 4, 512

    def prog(nc, sq, es):
        x_fm = _dram(nc, io, "x_fm", [D_MODEL, T_CORE], F32, "ExternalInput")
        xn_my = _dram(nc, io, "xn_my", [D_MODEL, T_CORE], BF16, "ExternalOutput")
        xnag = _dram(nc, io, "xnag", [8 * D_MODEL, T_CORE], BF16, "ExternalInput")
        idxp_i = _dram(nc, io, "idxp_i", [128, NCH], U32, "ExternalInput")
        mz_i = _dram(nc, io, "mz_i", [128, 1], F32, "ExternalInput")
        wkT = _dram(nc, io, "wkT", [D_MODEL, D_MODEL], F32, "ExternalInput")
        wvT = _dram(nc, io, "wvT", [D_MODEL, D_MODEL], F32, "ExternalInput")
        wrT = _dram(nc, io, "wrT", [D_MODEL, D_MODEL], F32, "ExternalInput")
        mu_k = _dram(nc, io, "mu_k", [128, NCH], F32, "ExternalInput")
        mu_v = _dram(nc, io, "mu_v", [128, NCH], F32, "ExternalInput")
        mu_r = _dram(nc, io, "mu_r", [128, NCH], F32, "ExternalInput")
        ek_o = _dram(nc, io, "ek_o", [D_MODEL, T_CORE], F32, "ExternalOutput")
        ekv_o = _dram(nc, io, "ekv_o", [D_MODEL, T_CORE], F32, "ExternalOutput")
        ekekv_my = _dram(nc, io, "ekekv_my", [2 * D_MODEL, T_CORE], BF16,
                         "ExternalOutput")
        r_o = _dram(nc, io, "r_o", [D_MODEL, T_CORE], F32, "ExternalOutput")

        xv_ = x_fm.rearrange("(c p) t -> p c t", p=128)
        eko = ek_o.rearrange("(c p) t -> p c t", p=128)
        ekvo = ekv_o.rearrange("(c p) t -> p c t", p=128)
        ekmy = ekekv_my[0:D_MODEL, :].rearrange("(c p) t -> p c t", p=128)
        ekvmy = ekekv_my[D_MODEL:2 * D_MODEL, :].rearrange("(c p) t -> p c t", p=128)
        ro = r_o.rearrange("(c p) t -> p c t", p=128)

        sb = lambda n, s, d=F32: es.enter_context(nc.sbuf_tensor(n, s, d))
        pt = lambda n, s, d=F32: es.enter_context(nc.psum_tensor(n, s, d))

        ones_col = sb("ones_col", [128, 1])
        ones_row = sb("ones_row", [1, 128])
        eps_row = sb("eps_row", [1, 1])
        Xt = sb("Xt", [128, NCH * 512])
        sq32 = sb("sq32", [128, 512])
        a_row = sb("a_row", [1, 512])
        b_row = sb("b_row", [1, 512])
        t_row = sb("t_row", [1, 512])
        xn = sb("xn", [128, NCH * T_CORE], BF16)
        xnp = sb("xnp", [128, NCH * T_CORE], BF16)
        w_s = sb("w_s", [128, NCH * D_MODEL], BF16)
        mus = {nm: sb(f"mus_{nm}", [128, NCH]) for nm in ("k", "v", "r")}
        xm = sb("xm", [128, NCH * T], BF16)
        ek_t = sb("ek_t", [128, 512])
        v_t = sb("v_t", [128, 512])
        oscr = sb("oscr", [128, 512])
        ps_a = pt("ps_a", [1, 512])
        ps_b = pt("ps_b", [1, 512])
        ps_bc1 = pt("ps_bc1", [128, 512])
        ps_bc2 = pt("ps_bc2", [128, 512])
        ps_mm = pt("ps_mm", [128, 512])
        ps_mm2 = pt("ps_mm2", [128, 512])

        sq.step("gpsimd", lambda e: e.memset(ones_col[:], 1.0))
        sq.step("gpsimd", lambda e: e.memset(ones_row[:], 1.0))
        sq.step("gpsimd", lambda e: e.memset(eps_row[:], LN_EPS))
        for nm, t in mus.items():
            sq.dma(t[:], {"k": mu_k, "v": mu_v, "r": mu_r}[nm])

        def ln_to(xsrc, dst):
            for it in range(4):
                c0, w = it * 512, 512
                sq.dma(Xt[:].rearrange("p (c t) -> p c t", c=NCH), xsrc[:, :, c0:c0 + w])
                sq.step("tensor", lambda e, w=w: [
                    e.matmul(ps_a[:, 0:w], ones_col[:], Xt[:, c * 512:c * 512 + w],
                             start=(c == 0), stop=(c == NCH - 1)) for c in range(NCH)])
                for c in range(NCH):
                    sq.step("scalar", lambda e, c=c, w=w: e.activation(
                        sq32[:, 0:w], Xt[:, c * 512:c * 512 + w], AF.Square))
                    sq.step("tensor", lambda e, c=c, w=w: e.matmul(
                        ps_b[:, 0:w], ones_col[:], sq32[:, 0:w],
                        start=(c == 0), stop=(c == NCH - 1)))
                sq.step("vector", lambda e, w=w: e.tensor_single_scalar(
                    b_row[:, 0:w], ps_a[:, 0:w], 1.0 / D_MODEL, ALU.mult))
                sq.step("vector", lambda e, w=w: e.tensor_single_scalar(
                    a_row[:, 0:w], ps_b[:, 0:w], 1.0 / D_MODEL, ALU.mult))
                sq.step("scalar", lambda e, w=w: e.activation(
                    t_row[:, 0:w], b_row[:, 0:w], AF.Square))
                sq.step("vector", lambda e, w=w: e.tensor_sub(
                    a_row[:, 0:w], a_row[:, 0:w], t_row[:, 0:w]))
                sq.step("scalar", lambda e, w=w: e.activation(
                    a_row[:, 0:w], a_row[:, 0:w], AF.Ln, bias=eps_row[:]))
                sq.step("scalar", lambda e, w=w: e.activation(
                    a_row[:, 0:w], a_row[:, 0:w], AF.Exp, scale=-0.5))
                sq.step("tensor", lambda e, w=w: e.matmul(
                    ps_bc1[:, 0:w], ones_row[:], b_row[:, 0:w]))
                sq.step("tensor", lambda e, w=w: e.matmul(
                    ps_bc2[:, 0:w], ones_row[:], a_row[:, 0:w]))
                for c in range(NCH):
                    sq.step("vector", lambda e, c=c, w=w: e.tensor_sub(
                        sq32[:, 0:w], Xt[:, c * 512:c * 512 + w], ps_bc1[:, 0:w]))
                    sq.step("vector", lambda e, c=c, c0=c0, w=w: e.tensor_mul(
                        dst[:, c * T_CORE + c0:c * T_CORE + c0 + w], sq32[:, 0:w],
                        ps_bc2[:, 0:w]))

        ln_to(xv_, xn)
        sq.dma(xn_my.rearrange("(c p) t -> p c t", p=128),
               xn[:].rearrange("p (c t) -> p c t", c=NCH))
        if io is not None:
            sq.step("gpsimd", lambda e: e.collective_compute(
                "AllGather", ALU.bypass, replica_groups=[list(range(8))],
                ins=[io["xn_my_t"].ap().opt()], outs=[io["xnag_t"].ap().opt()]),
                inc=1, sclass="cc0")
        idxp_s = es.enter_context(nc.sbuf_tensor("idxp_s", [128, NCH], U32))
        mz_s = es.enter_context(nc.sbuf_tensor("mz_s", [128, 1], F32))
        sq.dma(idxp_s[:], idxp_i)
        sq.dma(mz_s[:], mz_i)
        for c in range(NCH):
            sq.step("gpsimd", lambda e, c=c: e.indirect_dma_start(
                out=xnp[:, c * T_CORE:(c + 1) * T_CORE],
                out_offset=None, in_=xnag,
                in_offset=IndirectOffsetOnAxis(ap=idxp_s[:, c:c + 1], axis=0),
                oob_is_err=False), inc=16, dma=True)
            for tt4 in range(4):
                t0g = tt4 * 512
                sq.step("vector", lambda e, c=c, t0g=t0g: e.tensor_scalar(
                    xnp[:, c * T_CORE + t0g:c * T_CORE + t0g + 512],
                    xnp[:, c * T_CORE + t0g:c * T_CORE + t0g + 512],
                    mz_s[:], None, ALU.mult))

        for kind in ("k", "v", "r"):
            wsrc = {"k": wkT, "v": wvT, "r": wrT}[kind]
            sq.dma(w_s[:].rearrange("p (c d) -> p c d", c=NCH),
                   wsrc.rearrange("(c p) d -> p c d", p=128), engine="gpsimd")
            mu = mus[kind]
            for tt in range(NT):
                t0 = tt * T
                for c in range(NCH):
                    sq.step("vector", lambda e, c=c, t0=t0: e.tensor_sub(
                        sq32[:], xn[:, c * T_CORE + t0:c * T_CORE + t0 + T],
                        xnp[:, c * T_CORE + t0:c * T_CORE + t0 + T]))
                    sq.step("vector", lambda e, c=c, t0=t0, mu=mu: e.scalar_tensor_tensor(
                        xm[:, c * T:(c + 1) * T], sq32[:], mu[:, c:c + 1],
                        xnp[:, c * T_CORE + t0:c * T_CORE + t0 + T],
                        ALU.mult, ALU.add))
                for d in range(NCH):
                    sq.step("tensor", lambda e, d=d: [
                        e.matmul(ps_mm[:], w_s[:, c * D_MODEL + d * 128:
                                               c * D_MODEL + (d + 1) * 128],
                                 xm[:, c * T:(c + 1) * T],
                                 start=(c == 0), stop=(c == NCH - 1))
                        for c in range(NCH)])
                    if kind == "k":
                        sq.step("scalar", lambda e: e.activation(
                            ek_t[:], ps_mm[:], AF.Exp))
                        sq.dma(eko[:, d, t0:t0 + T], ek_t[:])
                        sq.dma(ekmy[:, d, t0:t0 + T], ek_t[:], engine="gpsimd")
                    elif kind == "v":
                        sq.dma(ek_t[:], eko[:, d, t0:t0 + T])
                        sq.step("vector", lambda e: e.tensor_copy(v_t[:], ps_mm[:]))
                        sq.step("vector", lambda e: e.tensor_mul(
                            oscr[:], ek_t[:], v_t[:]))
                        sq.dma(ekvo[:, d, t0:t0 + T], oscr[:])
                        sq.dma(ekvmy[:, d, t0:t0 + T], oscr[:], engine="gpsimd")
                    else:
                        sq.step("vector", lambda e: e.tensor_copy(oscr[:], ps_mm[:]))
                        sq.dma(ro[:, d, t0:t0 + T], oscr[:])
        if io is not None:
            sq.step("gpsimd", lambda e: e.collective_compute(
                "AllGather", ALU.bypass, replica_groups=[list(range(8))],
                ins=[io["ekekv_my_t"].ap().opt()],
                outs=[io["ekekvg_t"].ap().opt()]), inc=1, sclass="cc1")

    if emit_only:
        return prog
    return seq_program(prog)


def build_wkv_out(io=None, emit_only=False):
    NT, T = 4, 512

    def prog(nc, sq, es):
        x_fm = _dram(nc, io, "x_fm", [D_MODEL, T_CORE], F32, "ExternalInput")
        ekekvg = _dram(nc, io, "ekekvg", [8 * 2 * D_MODEL, T_CORE], BF16,
                       "ExternalInput")
        idxo1_i = _dram(nc, io, "idxo1_i", [128, NCH], U32, "ExternalInput")
        idxo2_i = _dram(nc, io, "idxo2_i", [128, NCH], U32, "ExternalInput")
        r_fm = _dram(nc, io, "r_fm", [D_MODEL, T_CORE], F32, "ExternalInput")
        woT = _dram(nc, io, "woT", [D_MODEL, D_MODEL], F32, "ExternalInput")
        wv_ = _dram(nc, io, "wv_", [128, NCH], F32, "ExternalInput")
        uv_ = _dram(nc, io, "uv_", [128, NCH], F32, "ExternalInput")
        steps = _dram(nc, io, "steps", [128, 8], F32, "ExternalInput")
        mask = _dram(nc, io, "mask", [128, 8], F32, "ExternalInput")
        x_out = _dram(nc, io, "x_out", [D_MODEL, T_CORE], F32, "ExternalOutput")

        xv_ = x_fm.rearrange("(c p) t -> p c t", p=128)
        ekg = ekekvg.rearrange("(r hh c p) t -> r hh p c t", r=8, hh=2, p=128)
        rv_ = r_fm.rearrange("(c p) t -> p c t", p=128)
        wov = woT.rearrange("(c p) d -> p c d", p=128)
        xo = x_out.rearrange("(c p) t -> p c t", p=128)

        sb = lambda n, s, d=F32: es.enter_context(nc.sbuf_tensor(n, s, d))
        pt = lambda n, s, d=F32: es.enter_context(nc.psum_tensor(n, s, d))

        w_s = sb("w_s", [128, NCH * D_MODEL], BF16)
        wneg = sb("wneg", [128, NCH])
        iew = sb("iew", [128, NCH])
        eu = sb("eu", [128, NCH])
        dec = sb("dec", [128, NCH * 8])
        st_s = sb("st_s", [128, 8])
        mk_s = sb("mk_s", [128, 8])
        ek_own = sb("ek_own", [128, NCH * T_CORE], BF16)
        ekv_own = sb("ekv_own", [128, NCH * T_CORE], BF16)
        bufb = sb("bufb", [128, NCH * T], BF16)
        idxo1_s = sb("idxo1_s", [128, NCH], U32)
        idxo2_s = sb("idxo2_s", [128, NCH], U32)
        cek = sb("cek", [128, NCH * T])
        cekv = sb("cekv", [128, NCH * T])
        buf = sb("buf", [128, NCH * T])
        num = sb("num", [128, NCH * T])
        wkb = sb("wkb", [128, NCH * T], BF16)
        sq32 = sb("sq32", [128, 512])
        scr2 = sb("scr2", [128, 512])
        Xt = sb("Xt", [128, NCH * T])
        oscr = sb("oscr", [128, 512])
        ps_mm = pt("ps_mm", [128, 512])

        sq.dma(wneg[:], wv_)
        sq.dma(eu[:], uv_)
        sq.dma(st_s[:], steps)
        sq.dma(mk_s[:], mask)
        sq.dma(w_s[:].rearrange("p (c d) -> p c d", c=NCH),
               wov, engine="gpsimd")
        sq.step("scalar", lambda e: e.activation(wneg[:], wneg[:], AF.Exp))
        sq.step("vector", lambda e: e.tensor_single_scalar(
            wneg[:], wneg[:], -1.0, ALU.mult))
        sq.step("scalar", lambda e: e.activation(iew[:], wneg[:], AF.Exp, scale=-1.0))
        sq.step("scalar", lambda e: e.activation(eu[:], eu[:], AF.Exp))
        sq.dma(idxo1_s[:], idxo1_i)
        sq.dma(idxo2_s[:], idxo2_i)
        for c in range(NCH):
            for bp in range(8):
                sq.step("vector", lambda e, c=c, bp=bp: e.tensor_mul(
                    dec[:, c * 8 + bp:c * 8 + bp + 1], wneg[:, c:c + 1],
                    st_s[:, bp:bp + 1]))
        sq.step("scalar", lambda e: e.activation(dec[:], dec[:], AF.Exp))
        for c in range(NCH):
            sq.step("vector", lambda e, c=c: e.tensor_mul(
                dec[:, c * 8:(c + 1) * 8], dec[:, c * 8:(c + 1) * 8], mk_s[:]))

        for tt in range(NT):
            t0 = tt * T
            sq.step("gpsimd", lambda e: e.memset(cek[:], 0.0))
            sq.step("gpsimd", lambda e: e.memset(cekv[:], 0.0))
            for bp in range(8):
                sq.dma(bufb[:].rearrange("p (c t) -> p c t", c=NCH),
                       ekg[bp, 0, :, :, t0:t0 + T])
                for c in range(NCH):
                    sq.step("vector", lambda e, c=c, bp=bp: e.scalar_tensor_tensor(
                        cek[:, c * T:(c + 1) * T], bufb[:, c * T:(c + 1) * T],
                        dec[:, c * 8 + bp:c * 8 + bp + 1],
                        cek[:, c * T:(c + 1) * T], ALU.mult, ALU.add))
                sq.dma(bufb[:].rearrange("p (c t) -> p c t", c=NCH),
                       ekg[bp, 1, :, :, t0:t0 + T])
                for c in range(NCH):
                    sq.step("vector", lambda e, c=c, bp=bp: e.scalar_tensor_tensor(
                        cekv[:, c * T:(c + 1) * T], bufb[:, c * T:(c + 1) * T],
                        dec[:, c * 8 + bp:c * 8 + bp + 1],
                        cekv[:, c * T:(c + 1) * T], ALU.mult, ALU.add))
            if tt == 0:
                for c in range(NCH):
                    sq.step("gpsimd", lambda e, c=c: e.indirect_dma_start(
                        out=ek_own[:, c * T_CORE:(c + 1) * T_CORE], out_offset=None,
                        in_=ekekvg,
                        in_offset=IndirectOffsetOnAxis(ap=idxo1_s[:, c:c + 1], axis=0),
                        oob_is_err=False), inc=16, dma=True)
                    sq.step("gpsimd", lambda e, c=c: e.indirect_dma_start(
                        out=ekv_own[:, c * T_CORE:(c + 1) * T_CORE], out_offset=None,
                        in_=ekekvg,
                        in_offset=IndirectOffsetOnAxis(ap=idxo2_s[:, c:c + 1], axis=0),
                        oob_is_err=False), inc=16, dma=True)
            for c in range(NCH):
                sq.step("vector", lambda e, c=c, t0=t0: e.tensor_sub(
                    sq32[:], cek[:, c * T:(c + 1) * T],
                    ek_own[:, c * T_CORE + t0:c * T_CORE + t0 + T]))
                sq.step("vector", lambda e, c=c: e.tensor_scalar(
                    sq32[:], sq32[:], iew[:, c:c + 1], None, ALU.mult))
                sq.step("vector", lambda e, c=c, t0=t0: e.scalar_tensor_tensor(
                    buf[:, c * T:(c + 1) * T],
                    ek_own[:, c * T_CORE + t0:c * T_CORE + t0 + T],
                    eu[:, c:c + 1], sq32[:], ALU.mult, ALU.add))
                sq.step("vector", lambda e, c=c, t0=t0: e.tensor_sub(
                    sq32[:], cekv[:, c * T:(c + 1) * T],
                    ekv_own[:, c * T_CORE + t0:c * T_CORE + t0 + T]))
                sq.step("vector", lambda e, c=c: e.tensor_scalar(
                    sq32[:], sq32[:], iew[:, c:c + 1], None, ALU.mult))
                sq.step("vector", lambda e, c=c, t0=t0: e.scalar_tensor_tensor(
                    num[:, c * T:(c + 1) * T],
                    ekv_own[:, c * T_CORE + t0:c * T_CORE + t0 + T],
                    eu[:, c:c + 1], sq32[:], ALU.mult, ALU.add))
                sq.step("vector", lambda e, c=c: e.reciprocal(
                    scr2[:], buf[:, c * T:(c + 1) * T]))
                sq.step("vector", lambda e, c=c: e.tensor_mul(
                    sq32[:], num[:, c * T:(c + 1) * T], scr2[:]))
                sq.dma(Xt[:, 0:T], rv_[:, c, t0:t0 + T])
                sq.step("scalar", lambda e: e.activation(scr2[:], Xt[:, 0:T], AF.Sigmoid))
                sq.step("vector", lambda e, c=c: e.tensor_mul(
                    wkb[:, c * T:(c + 1) * T], sq32[:], scr2[:]))
            sq.dma(Xt[:].rearrange("p (c t) -> p c t", c=NCH), xv_[:, :, t0:t0 + T])
            for d in range(NCH):
                sq.step("tensor", lambda e, d=d: [
                    e.matmul(ps_mm[:], w_s[:, c * D_MODEL + d * 128:
                                           c * D_MODEL + (d + 1) * 128],
                             wkb[:, c * T:(c + 1) * T],
                             start=(c == 0), stop=(c == NCH - 1))
                    for c in range(NCH)])
                sq.step("vector", lambda e, d=d: e.tensor_add(
                    oscr[:], ps_mm[:], Xt[:, d * T:(d + 1) * T]))
                sq.dma(xo[:, d, t0:t0 + T], oscr[:])

    if emit_only:
        return prog
    return seq_program(prog)


def host_wkv_kv_inputs(x, W, core):
    b, h = core // 2, core % 2
    if b == 0:
        xp = np.zeros((T_CORE, D_MODEL), np.float32)
    else:
        xp = np.asarray(x[b - 1, h * 2048:(h + 1) * 2048])
    return {
        "x_fm": fmT(core_slice_tokens(x, core)),
        "xp_fm": fmT(xp),
        "wkT": fmT(W["Wk"]), "wvT": fmT(W["Wv"]), "wrT": fmT(W["Wr"]),
        "mu_k": pack_vec(W["mu_k"]), "mu_v": pack_vec(W["mu_v"]),
        "mu_r": pack_vec(W["mu_r"]),
    }


def host_wkv_out_inputs(x, kv_outs, W, core):
    b, h = core // 2, core % 2
    order = [(b - j) % 4 for j in range(4)]
    ek = np.stack([kv_outs[2 * bp + h]["ek_o"] for bp in order])
    ekv = np.stack([kv_outs[2 * bp + h]["ekv_o"] for bp in order])
    steps = np.zeros((128, 4), np.float32)
    maskv = np.zeros((128, 4), np.float32)
    for j, bp in enumerate(order):
        steps[:, j] = b - bp
        maskv[:, j] = 1.0 if bp <= b else 0.0
    return {
        "x_fm": fmT(core_slice_tokens(x, core)),
        "ek_all": np.ascontiguousarray(ek),
        "ekv_all": np.ascontiguousarray(ekv),
        "r_fm": kv_outs[core]["r_o"],
        "woT": fmT(W["W_out"]),
        "wv_": pack_vec(W["w"]), "uv_": pack_vec(W["u"]),
        "steps": steps, "mask": maskv,
    }



ATT_SCALE = float(1.0 / np.sqrt(16 + 512 + 512))


def build_mla_qkv(io=None, emit_only=False):
    DQ = 512
    NQ = 4

    def prog(nc, sq, es):
        x_fm = _dram(nc, io, "x_fm", [D_MODEL, T_CORE], F32, "ExternalInput")
        wdq = _dram(nc, io, "wdq", [D_MODEL, DQ], F32, "ExternalInput")
        wdkv = _dram(nc, io, "wdkv", [D_MODEL, DQ], F32, "ExternalInput")
        wuq = _dram(nc, io, "wuq", [DQ, D_MODEL], F32, "ExternalInput")
        wqr = _dram(nc, io, "wqr", [DQ, D_MODEL], F32, "ExternalInput")
        wuk = _dram(nc, io, "wuk", [DQ, D_MODEL], F32, "ExternalInput")
        wuv = _dram(nc, io, "wuv", [DQ, D_MODEL], F32, "ExternalInput")
        wkr = _dram(nc, io, "wkr", [D_MODEL, D_MODEL], F32, "ExternalInput")
        b_dq = _dram(nc, io, "b_dq", [128, NQ], F32, "ExternalInput")
        b_dkv = _dram(nc, io, "b_dkv", [128, NQ], F32, "ExternalInput")
        brows = {nm: _dram(nc, io, f"brow_{nm}", [1, D_MODEL], F32, "ExternalInput")
                 for nm in ("uq", "qr", "kr", "uk", "uv")}
        cos_t = _dram(nc, io, "cos_t", [T_CORE, DQ], F32, "ExternalInput")
        sin_t = _dram(nc, io, "sin_t", [T_CORE, DQ], F32, "ExternalInput")
        q_o = _dram(nc, io, "q_o", [D_MODEL, T_CORE], BF16, "ExternalOutput")
        k_o = _dram(nc, io, "k_o", [D_MODEL, T_CORE], BF16, "ExternalOutput")
        v_o = _dram(nc, io, "v_o", [T_CORE, 16 * 65], BF16, "ExternalOutput")

        xv_ = x_fm.rearrange("(c p) t -> p c t", p=128)
        qo = q_o.rearrange("(c p) t -> p c t", p=128)
        ko = k_o.rearrange("(c p) t -> p c t", p=128)

        sb = lambda n, s, d=F32: es.enter_context(nc.sbuf_tensor(n, s, d))
        pt = lambda n, s, d=F32: es.enter_context(nc.psum_tensor(n, s, d))

        ones_col = sb("ones_col", [128, 1])
        ones_row = sb("ones_row", [1, 128])
        eps_row = sb("eps_row", [1, 1])
        identw = sb("identw", [128, 128])
        identb = sb("identb", [128, 128], BF16)
        Xt = sb("Xt", [128, NCH * 512])
        sq32 = sb("sq32", [128, 512])
        scr_a = sb("scr_a", [128, 512])
        scr_b = sb("scr_b", [128, 512])
        a_row = sb("a_row", [1, 512])
        b_row = sb("b_row", [1, 512])
        t_row = sb("t_row", [1, 512])
        xn = sb("xn", [128, NCH * T_CORE], BF16)
        ctq = sb("ctq", [128, NQ * T_CORE], BF16)
        ctkv = sb("ctkv", [128, NQ * T_CORE], BF16)
        w_dq = sb("w_dq", [128, NCH * DQ], BF16)
        w_dkv = sb("w_dkv", [128, NCH * DQ], BF16)
        w_uq = sb("w_uq", [128, NQ * D_MODEL], BF16)
        w_qr = sb("w_qr", [128, NQ * D_MODEL], BF16)
        w_uk = sb("w_uk", [128, NQ * D_MODEL], BF16)
        w_uv = sb("w_uv", [128, NQ * D_MODEL], BF16)
        w_kr = sb("w_kr", [128, NCH * D_MODEL], BF16)
        bdq_s = sb("bdq_s", [128, NQ])
        bdkv_s = sb("bdkv_s", [128, NQ])
        brep = {nm: sb(f"brep_{nm}", [128, D_MODEL]) for nm in
                ("uq", "qr", "kr", "uk", "uv")}
        brow_s = sb("brow_s", [1, D_MODEL])
        cs_t = sb("cs_t", [128, DQ])
        sn_t = sb("sn_t", [128, DQ])
        qtm = sb("qtm", [128, D_MODEL])
        rtm = sb("rtm", [128, D_MODEL])
        qtb = sb("qtb", [128, D_MODEL], BF16)
        vaug_tb = sb("vaug_tb", [128, 16 * 65], BF16)
        tfm = sb("tfm", [128, 128], BF16)

        ps_a = pt("ps_a", [1, 512])
        ps_b = pt("ps_b", [1, 512])
        ps_bc1 = pt("ps_bc1", [128, 512])
        ps_bc2 = pt("ps_bc2", [128, 512])
        ps_m1 = pt("ps_m1", [128, 512])
        ps_m2 = pt("ps_m2", [128, 512])
        ps_tr = pt("ps_tr", [128, 128], BF16)

        sq.step("gpsimd", lambda e: e.memset(ones_col[:], 1.0))
        sq.step("gpsimd", lambda e: e.memset(ones_row[:], 1.0))
        sq.step("gpsimd", lambda e: e.memset(eps_row[:], LN_EPS))
        sq.step("gpsimd", lambda e: e.memset(identw[:], 1.0))
        sq.step("gpsimd", lambda e: e.affine_select(
            identw[:], identw[:], pattern=[[1, 128]], compare_op=ALU.is_equal,
            fill=0.0, base=0, channel_multiplier=-1))
        sq.step("vector", lambda e: e.tensor_copy(identb[:], identw[:]))
        sq.dma(bdq_s[:], b_dq)
        sq.dma(bdkv_s[:], b_dkv)
        sq.dma(w_dq[:].rearrange("p (c d) -> p c d", c=NCH),
               wdq.rearrange("(c p) d -> p c d", p=128), engine="gpsimd")
        sq.dma(w_dkv[:].rearrange("p (c d) -> p c d", c=NCH),
               wdkv.rearrange("(c p) d -> p c d", p=128), engine="gpsimd")
        sq.dma(w_uq[:].rearrange("p (c d) -> p c d", c=NQ),
               wuq.rearrange("(c p) d -> p c d", p=128), engine="gpsimd")
        sq.dma(w_qr[:].rearrange("p (c d) -> p c d", c=NQ),
               wqr.rearrange("(c p) d -> p c d", p=128), engine="gpsimd")
        sq.dma(w_uk[:].rearrange("p (c d) -> p c d", c=NQ),
               wuk.rearrange("(c p) d -> p c d", p=128), engine="gpsimd")
        sq.dma(w_uv[:].rearrange("p (c d) -> p c d", c=NQ),
               wuv.rearrange("(c p) d -> p c d", p=128), engine="gpsimd")
        sq.dma(w_kr[:].rearrange("p (c d) -> p c d", c=NCH),
               wkr.rearrange("(c p) d -> p c d", p=128), engine="gpsimd")
        for nm in brep:
            sq.dma(brow_s[:], brows[nm])
            for n in range(2):
                sq.step("tensor", lambda e, n=n, nm=nm: e.matmul(
                    ps_bc1[:], ones_row[:], brow_s[:, n * 512:(n + 1) * 512]))
                sq.step("vector", lambda e, n=n, nm=nm: e.tensor_copy(
                    brep[nm][:, n * 512:(n + 1) * 512], ps_bc1[:]))

        for it in range(4):
            c0, w = it * 512, 512
            sq.dma(Xt[:].rearrange("p (c t) -> p c t", c=NCH), xv_[:, :, c0:c0 + w])
            sq.step("tensor", lambda e, w=w: [
                e.matmul(ps_a[:, 0:w], ones_col[:], Xt[:, c * 512:c * 512 + w],
                         start=(c == 0), stop=(c == NCH - 1)) for c in range(NCH)])
            for c in range(NCH):
                sq.step("scalar", lambda e, c=c, w=w: e.activation(
                    sq32[:, 0:w], Xt[:, c * 512:c * 512 + w], AF.Square))
                sq.step("tensor", lambda e, c=c, w=w: e.matmul(
                    ps_b[:, 0:w], ones_col[:], sq32[:, 0:w],
                    start=(c == 0), stop=(c == NCH - 1)))
            sq.step("vector", lambda e, w=w: e.tensor_single_scalar(
                b_row[:, 0:w], ps_a[:, 0:w], 1.0 / D_MODEL, ALU.mult))
            sq.step("vector", lambda e, w=w: e.tensor_single_scalar(
                a_row[:, 0:w], ps_b[:, 0:w], 1.0 / D_MODEL, ALU.mult))
            sq.step("scalar", lambda e, w=w: e.activation(
                t_row[:, 0:w], b_row[:, 0:w], AF.Square))
            sq.step("vector", lambda e, w=w: e.tensor_sub(
                a_row[:, 0:w], a_row[:, 0:w], t_row[:, 0:w]))
            sq.step("scalar", lambda e, w=w: e.activation(
                a_row[:, 0:w], a_row[:, 0:w], AF.Ln, bias=eps_row[:]))
            sq.step("scalar", lambda e, w=w: e.activation(
                a_row[:, 0:w], a_row[:, 0:w], AF.Exp, scale=-0.5))
            sq.step("tensor", lambda e, w=w: e.matmul(
                ps_bc1[:, 0:w], ones_row[:], b_row[:, 0:w]))
            sq.step("tensor", lambda e, w=w: e.matmul(
                ps_bc2[:, 0:w], ones_row[:], a_row[:, 0:w]))
            for c in range(NCH):
                sq.step("vector", lambda e, c=c, w=w: e.tensor_sub(
                    sq32[:, 0:w], Xt[:, c * 512:c * 512 + w], ps_bc1[:, 0:w]))
                sq.step("vector", lambda e, c=c, c0=c0, w=w: e.tensor_mul(
                    xn[:, c * T_CORE + c0:c * T_CORE + c0 + w], sq32[:, 0:w],
                    ps_bc2[:, 0:w]))

        for nm, wsl, bias, dst in (("q", w_dq, bdq_s, ctq), ("kv", w_dkv, bdkv_s, ctkv)):
            for dqt in range(NQ):
                for tt in range(4):
                    t0 = tt * 512
                    sq.step("tensor", lambda e, wsl=wsl, dqt=dqt, t0=t0: [
                        e.matmul(ps_m1[:], wsl[:, c * DQ + dqt * 128:
                                               c * DQ + (dqt + 1) * 128],
                                 xn[:, c * T_CORE + t0:c * T_CORE + t0 + 512],
                                 start=(c == 0), stop=(c == NCH - 1))
                        for c in range(NCH)])
                    sq.step("vector", lambda e, dst=dst, bias=bias, dqt=dqt, t0=t0:
                        e.tensor_scalar(
                            dst[:, dqt * T_CORE + t0:dqt * T_CORE + t0 + 512],
                            ps_m1[:], bias[:, dqt:dqt + 1], None, ALU.add))

        for qt in range(T_CORE // 128):
            t0 = qt * 128
            sq.dma(cs_t[:], cos_t[t0:t0 + 128, :])
            sq.dma(sn_t[:], sin_t[t0:t0 + 128, :])
            for which in ("q", "k"):
                ct = ctq if which == "q" else ctkv
                wc = w_uq if which == "q" else w_uk
                bc = brep["uq"] if which == "q" else brep["uk"]
                for n in range(2):
                    sq.step("tensor", lambda e, ct=ct, wc=wc, n=n, t0=t0: [
                        e.matmul(ps_m1[:], ct[:, c * T_CORE + t0:c * T_CORE + t0 + 128],
                                 wc[:, c * D_MODEL + n * 512:c * D_MODEL + n * 512 + 512],
                                 start=(c == 0), stop=(c == NQ - 1))
                        for c in range(NQ)])
                    sq.step("vector", lambda e, bc=bc, n=n: e.tensor_add(
                        qtm[:, n * 512:(n + 1) * 512], ps_m1[:],
                        bc[:, n * 512:(n + 1) * 512]))
                if which == "q":
                    for n in range(2):
                        sq.step("tensor", lambda e, n=n, t0=t0: [
                            e.matmul(ps_m2[:],
                                     ctq[:, c * T_CORE + t0:c * T_CORE + t0 + 128],
                                     w_qr[:, c * D_MODEL + n * 512:
                                          c * D_MODEL + n * 512 + 512],
                                     start=(c == 0), stop=(c == NQ - 1))
                            for c in range(NQ)])
                        sq.step("vector", lambda e, n=n: e.tensor_add(
                            rtm[:, n * 512:(n + 1) * 512], ps_m2[:],
                            brep["qr"][:, n * 512:(n + 1) * 512]))
                else:
                    for n in range(2):
                        sq.step("tensor", lambda e, n=n, t0=t0: [
                            e.matmul(ps_m2[:],
                                     xn[:, c * T_CORE + t0:c * T_CORE + t0 + 128],
                                     w_kr[:, c * D_MODEL + n * 512:
                                          c * D_MODEL + n * 512 + 512],
                                     start=(c == 0), stop=(c == NCH - 1))
                            for c in range(NCH)])
                        sq.step("vector", lambda e, n=n: e.tensor_add(
                            rtm[:, n * 512:(n + 1) * 512], ps_m2[:],
                            brep["kr"][:, n * 512:(n + 1) * 512]))
                re = rtm[:].rearrange("p (i two) -> p two i", two=2)[:, 0, :]
                ro = rtm[:].rearrange("p (i two) -> p two i", two=2)[:, 1, :]
                qe = qtm[:].rearrange("p (i two) -> p two i", two=2)[:, 0, :]
                qo_ = qtm[:].rearrange("p (i two) -> p two i", two=2)[:, 1, :]
                sq.step("vector", lambda e, re=re: e.tensor_mul(scr_a[:], re, cs_t[:]))
                sq.step("vector", lambda e, ro=ro: e.tensor_mul(scr_b[:], ro, sn_t[:]))
                sq.step("vector", lambda e: e.tensor_sub(scr_a[:], scr_a[:], scr_b[:]))
                sq.step("vector", lambda e, qe=qe: e.tensor_add(qe, qe, scr_a[:]))
                sq.step("vector", lambda e, ro=ro: e.tensor_mul(scr_a[:], ro, cs_t[:]))
                sq.step("vector", lambda e, re=re: e.tensor_mul(scr_b[:], re, sn_t[:]))
                sq.step("vector", lambda e: e.tensor_add(scr_a[:], scr_a[:], scr_b[:]))
                sq.step("vector", lambda e, qo_=qo_: e.tensor_add(qo_, qo_, scr_a[:]))
                sq.step("vector", lambda e: e.tensor_copy(qtb[:], qtm[:]))
                dst = qo if which == "q" else ko
                for c in range(NCH):
                    sq.step("tensor", lambda e, c=c: e.transpose(
                        ps_tr[:], qtb[:, c * 128:(c + 1) * 128], identb[:]))
                    sq.step("vector", lambda e: e.tensor_copy(tfm[:], ps_tr[:]))
                    sq.dma(dst[:, c, t0:t0 + 128], tfm[:])
            if qt == 0:
                sq.step("gpsimd", lambda e: e.memset(vaug_tb[:], 1.0))
            for n in range(2):
                sq.step("tensor", lambda e, n=n, t0=t0: [
                    e.matmul(ps_m1[:], ctkv[:, c * T_CORE + t0:c * T_CORE + t0 + 128],
                             w_uv[:, c * D_MODEL + n * 512:c * D_MODEL + n * 512 + 512],
                             start=(c == 0), stop=(c == NQ - 1))
                    for c in range(NQ)])
                for hh in range(8):
                    g = n * 8 + hh
                    sq.step("vector", lambda e, n=n, hh=hh, g=g: e.tensor_add(
                        vaug_tb[:, g * 65:g * 65 + 64], ps_m1[:, hh * 64:(hh + 1) * 64],
                        brep["uv"][:, g * 64:(g + 1) * 64]))
            sq.dma(v_o[t0:t0 + 128, :], vaug_tb[:])

    if emit_only:
        return prog
    return seq_program(prog)


def build_mla_attn(io=None, emit_only=False):
    S = 4096
    NKT = S // 128
    NH = 16

    def prog(nc, sq, es):
        q_i = _dram(nc, io, "q_i", [D_MODEL, T_CORE], BF16, "ExternalInput")
        k_i = _dram(nc, io, "k_i", [D_MODEL, S], BF16, "ExternalInput")
        v_i = _dram(nc, io, "v_i", [S, D_MODEL], BF16, "ExternalInput")
        x_fm = _dram(nc, io, "x_fm", [D_MODEL, T_CORE], F32, "ExternalInput")
        wo = _dram(nc, io, "wo", [D_MODEL, D_MODEL], F32, "ExternalInput")
        b_o = _dram(nc, io, "b_o", [128, NCH], F32, "ExternalInput")
        x_out = _dram(nc, io, "x_out", [D_MODEL, T_CORE], F32, "ExternalOutput")

        qv = q_i.rearrange("(c p) t -> p c t", p=128)
        kv_ = k_i.rearrange("(c p) t -> p c t", p=128)
        vv = v_i.rearrange("(kt p) d -> p kt d", p=128)
        xv_ = x_fm.rearrange("(c p) t -> p c t", p=128)
        xo = x_out.rearrange("(c p) t -> p c t", p=128)

        sb = lambda n, s, d=F32: es.enter_context(nc.sbuf_tensor(n, s, d))
        pt = lambda n, s, d=F32: es.enter_context(nc.psum_tensor(n, s, d))

        ones_col = sb("ones_col", [128, 1])
        onesb_col = sb("onesb_col", [128, 1], BF16)
        ones_row = sb("ones_row", [1, 128])
        identw = sb("identw", [128, 128])
        identb = sb("identb", [128, 128], BF16)
        q_sb = sb("q_sb", [128, NCH * T_CORE], BF16)
        k_sb = sb("k_sb", [128, NCH * S], BF16)
        v_sb = sb("v_sb", [128, NKT * D_MODEL], BF16)
        wo_sb = sb("wo_sb", [128, NCH * D_MODEL], BF16)
        bo_sb = sb("bo_sb", [128, NCH])
        Ktil = sb("Ktil", [128, S], BF16)
        Qtil = sb("Qtil", [128, 128], BF16)
        mm8 = sb("mm8", [128, 8])
        m_c = sb("m_c", [128, 1])
        P_T = sb("P_T", [128, 128], BF16)
        rs_row = sb("rs_row", [1, 128])
        rc_col = sb("rc_col", [128, 1])
        out_tm = sb("out_tm", [128, NH * 64])
        otb = sb("otb", [128, NH * 64], BF16)
        ofm = sb("ofm", [128, NCH * 128], BF16)
        Xt = sb("Xt", [128, NCH * 128])
        oscr = sb("oscr", [128, 128])
        tfm = sb("tfm", [128, 128], BF16)

        ps_sc = pt("ps_sc", [128, 512])
        ps_st = pt("ps_st", [128, 128])
        ps_o = pt("ps_o", [128, 64])
        ps_rs = pt("ps_rs", [1, 128])
        ps_tr = pt("ps_tr", [128, 128], BF16)
        ps_r1 = pt("ps_r1", [1, 128])
        ps_out = pt("ps_out", [128, 128])

        sq.step("gpsimd", lambda e: e.memset(ones_col[:], 1.0))
        sq.step("gpsimd", lambda e: e.memset(onesb_col[:], 1.0))
        sq.step("gpsimd", lambda e: e.memset(ones_row[:], 1.0))
        sq.step("gpsimd", lambda e: e.memset(identw[:], 1.0))
        sq.step("gpsimd", lambda e: e.affine_select(
            identw[:], identw[:], pattern=[[1, 128]], compare_op=ALU.is_equal,
            fill=0.0, base=0, channel_multiplier=-1))
        sq.step("vector", lambda e: e.tensor_copy(identb[:], identw[:]))
        sq.dma(q_sb[:].rearrange("p (c t) -> p c t", c=NCH), qv)
        sq.dma(k_sb[:].rearrange("p (c t) -> p c t", c=NCH), kv_)
        sq.dma(v_sb[:].rearrange("p (kt d) -> p kt d", kt=NKT), vv)
        sq.dma(wo_sb[:].rearrange("p (c d) -> p c d", c=NCH),
               wo.rearrange("(c p) d -> p c d", p=128), engine="gpsimd")
        sq.dma(bo_sb[:], b_o)

        for qt in range(T_CORE // 128):
            t0 = qt * 128
            for h in range(NH):
                hb = 64 * (h % 2)
                hc = h // 2
                sq.dma(Ktil[0:64, :], k_sb[hb:hb + 64, hc * S:(hc + 1) * S],
                       engine="gpsimd")
                sq.step("gpsimd", lambda e: e.memset(Ktil[64:65, 0:S], 1.0))
                for kt8 in range(8):
                    sq.step("tensor", lambda e, h=h, hb=hb, hc=hc, t0=t0, kt8=kt8:
                        e.matmul(ps_sc[:],
                                 q_sb[hb:hb + 64, hc * T_CORE + t0:hc * T_CORE + t0 + 128],
                                 k_sb[hb:hb + 64, hc * S + kt8 * 512:hc * S + (kt8 + 1) * 512],
                                 start=True, stop=True))
                    sq.step("vector", lambda e, kt8=kt8: e.reduce_max(
                        mm8[:, kt8:kt8 + 1], ps_sc[:], axis=AX.X))
                sq.step("vector", lambda e: e.reduce_max(m_c[:], mm8[:], axis=AX.X))
                sq.step("vector", lambda e: e.tensor_single_scalar(
                    m_c[:], m_c[:], -1.0, ALU.mult))
                sq.dma(Qtil[0:64, :],
                       q_sb[hb:hb + 64, hc * T_CORE + t0:hc * T_CORE + t0 + 128],
                       engine="gpsimd")
                sq.step("tensor", lambda e: e.transpose(
                    ps_r1[:], m_c[:], identw[:]))
                sq.step("vector", lambda e: e.tensor_copy(
                    Qtil[64:65, :], ps_r1[:]))
                for kt in range(NKT):
                    sq.step("tensor", lambda e, kt=kt: e.matmul(
                        ps_st[:], Ktil[0:65, kt * 128:(kt + 1) * 128],
                        Qtil[0:65, :], start=True, stop=True))
                    sq.step("scalar", lambda e: e.activation(
                        P_T[:], ps_st[:], AF.Exp, scale=ATT_SCALE))
                    sq.step("tensor", lambda e, kt=kt: e.matmul(
                        ps_rs[:], onesb_col[:], P_T[:],
                        start=(kt == 0), stop=(kt == NKT - 1)))
                    sq.step("tensor", lambda e, h=h, kt=kt: e.matmul(
                        ps_o[:], P_T[:], v_sb[:, kt * D_MODEL + h * 64:
                                              kt * D_MODEL + (h + 1) * 64],
                        start=(kt == 0), stop=(kt == NKT - 1)))
                sq.step("vector", lambda e: e.reciprocal(rs_row[:], ps_rs[:]))
                sq.step("tensor", lambda e: e.transpose(
                    ps_st[0:128, 0:1], rs_row[:], identw[0:1, 0:1]))
                sq.step("vector", lambda e: e.tensor_copy(
                    rc_col[:], ps_st[0:128, 0:1]))
                sq.step("vector", lambda e, h=h: e.tensor_scalar(
                    out_tm[:, h * 64:(h + 1) * 64], ps_o[:], rc_col[:],
                    None, ALU.mult))
                if h == NH - 1:
                    sq.step("vector", lambda e: e.tensor_copy(otb[:], out_tm[:]))
                    for c in range(NCH):
                        sq.step("tensor", lambda e, c=c: e.transpose(
                            ps_tr[:], otb[:, c * 128:(c + 1) * 128], identb[:]))
                        sq.step("vector", lambda e, c=c: e.tensor_copy(
                            ofm[:, c * 128:(c + 1) * 128], ps_tr[:]))
                    sq.dma(Xt[:].rearrange("p (c t) -> p c t", c=NCH),
                           xv_[:, :, t0:t0 + 128])
                    for d in range(NCH):
                        sq.step("tensor", lambda e, d=d: [
                            e.matmul(ps_out[:], wo_sb[:, c * D_MODEL + d * 128:
                                                      c * D_MODEL + (d + 1) * 128],
                                     ofm[:, c * 128:(c + 1) * 128],
                                     start=(c == 0), stop=(c == NCH - 1))
                            for c in range(NCH)])
                        sq.step("vector", lambda e, d=d: e.tensor_scalar(
                            oscr[:], ps_out[:], bo_sb[:, d:d + 1], None, ALU.add))
                        sq.step("vector", lambda e, d=d: e.tensor_add(
                            oscr[:], oscr[:], Xt[:, d * 128:(d + 1) * 128]))
                        sq.dma(xo[:, d, t0:t0 + 128], oscr[:])

    if emit_only:
        return prog
    return seq_program(prog)


def build_mla_attn2(io=None, emit_only=False):
    S = 4096
    NKT = 32
    NH = 16
    NH65 = NH * 65

    def prog(nc, sq, es):
        q_i = _dram(nc, io, "q_i", [D_MODEL, T_CORE], BF16, "ExternalInput")
        k_i = _dram(nc, io, "k_i", [2 * D_MODEL, T_CORE], BF16, "ExternalInput")
        vaug_i = _dram(nc, io, "vaug_i", [S, NH65], BF16, "ExternalInput")
        x_fm = _dram(nc, io, "x_fm", [D_MODEL, T_CORE], F32, "ExternalInput")
        wo = _dram(nc, io, "wo", [D_MODEL, D_MODEL], F32, "ExternalInput")
        b_o = _dram(nc, io, "b_o", [128, NCH], F32, "ExternalInput")
        x_out = _dram(nc, io, "x_out", [D_MODEL, T_CORE], F32, "ExternalOutput")

        qv = q_i.rearrange("(c p) t -> p c t", p=128)
        kagv = k_i.rearrange("(r c p) t -> r p c t", r=2, p=128)
        vv = vaug_i.rearrange("(kt p) d -> p kt d", p=128)
        xv_ = x_fm.rearrange("(c p) t -> p c t", p=128)
        xo = x_out.rearrange("(c p) t -> p c t", p=128)

        sb = lambda n, s, d=F32: es.enter_context(nc.sbuf_tensor(n, s, d))
        pt = lambda n, s, d=F32: es.enter_context(nc.psum_tensor(n, s, d))

        ones_col = sb("ones_col", [128, 1])
        ones_r64 = sb("ones_r64", [1, 64])
        vaug = sb("vaug", [128, NKT * NH65], BF16)
        wo_sb = sb("wo_sb", [128, NCH * D_MODEL], BF16)
        bo_sb = sb("bo_sb", [128, NCH])
        Ktil = sb("Ktil", [65, 2 * S], BF16)
        Qtil = sb("Qtil", [65, 2 * T_CORE], BF16)
        P_T = sb("P_T", [128, 2 * 512], BF16)
        ksq = sb("ksq", [128, 512])
        ktl_b = sb("ktl_b", [128, 512], BF16)
        kt8 = sb("kt8", [1, 16])
        negkm = sb("negkm", [1, NH])
        qn_row = sb("qn_row", [1, 512])
        rrow = sb("rrow", [1, 512])
        rbc = sb("rbc", [64, 512])
        odd_t = sb("odd_t", [64, 1024], BF16)
        ofm = sb("ofm", [128, NCH * T_CORE], BF16)
        Xt = sb("Xt", [128, NCH * 512])
        oscr = sb("oscr", [128, 512])

        ps_sc = pt("ps_sc", [128, 1024])
        ps_av = pt("ps_av", [65, 1024])
        ps_bc = pt("ps_bc", [64, 512])
        ps_q1 = pt("ps_q1", [1, 512])
        ps_out = pt("ps_out", [128, 1024])

        sq.step("gpsimd", lambda e: e.memset(ones_col[:], 1.0))
        sq.step("gpsimd", lambda e: e.memset(ones_r64[:], 1.0))
        t_init = sq.step("gpsimd", lambda e: e.memset(Ktil[64:65, :], 1.0))
        t_v = sq.dma(vaug[:].rearrange("p (kt d) -> p kt d", kt=NKT), vv,
                     engine="gpsimd")
        t_wo = sq.dma(wo_sb[:].rearrange("p (c d) -> p c d", c=NCH),
                      wo.rearrange("(c p) d -> p c d", p=128), engine="gpsimd")
        t_bo = sq.dma(bo_sb[:], b_o)

        t_p = t_init
        for hc in range(NCH):
            for tile in range(8):
                t_p = sq.dma(ktl_b[:, 0:512],
                             kagv[tile // 4, :, hc, (tile % 4) * 512:(tile % 4) * 512 + 512],
                             wait=t_p)
                t_p = sq.step("scalar", lambda e: e.activation(
                    ksq[:], ktl_b[:], AF.Square), wait=t_p)
                t_p = sq.step("tensor", lambda e: e.matmul(
                    ps_q1[:], ones_col[0:64, :], ksq[0:64, :],
                    start=True, stop=True), wait=t_p)
                t_p = sq.step("vector", lambda e, tile=tile: e.reduce_max(
                    kt8[:, tile:tile + 1], ps_q1[:], axis=AX.X), wait=t_p)
                t_p = sq.step("tensor", lambda e: e.matmul(
                    ps_q1[:], ones_col[64:128, :], ksq[64:128, :],
                    start=True, stop=True), wait=t_p)
                t_p = sq.step("vector", lambda e, tile=tile: e.reduce_max(
                    kt8[:, 8 + tile:8 + tile + 1], ps_q1[:], axis=AX.X), wait=t_p)
            for par in range(2):
                h = 2 * hc + par
                t_p = sq.step("vector", lambda e, par=par, h=h: e.reduce_max(
                    negkm[:, h:h + 1], kt8[:, par * 8:(par + 1) * 8], axis=AX.X),
                    wait=t_p)
        t_p = sq.step("scalar", lambda e: e.activation(
            negkm[:], negkm[:], AF.Sqrt), wait=t_p)
        t_km = sq.step("vector", lambda e: e.tensor_single_scalar(
            negkm[:], negkm[:], -1.0, ALU.mult), wait=t_p)

        tok_kload = [None] * (NH + 2)
        tok_headS = [None] * (NH + 2)
        tok_exp = {}
        tok_norm = [None, None]
        tok_bcmul = None
        t_odd_dma = [None, None]

        lh_chain = [t_km]

        def load_head(h):
            par = h % 2
            hc, hb = h // 2, 64 * (h % 2)
            w1 = tmax(tok_headS[h - 2] if h >= 2 else None, t_km, t_init)
            tk0 = sq.dma(Ktil[0:64, par * S:par * S + 2048],
                         kagv[0, hb:hb + 64, hc, :], engine="gpsimd", wait=w1)
            tk = sq.dma(Ktil[0:64, par * S + 2048:par * S + 4096],
                        kagv[1, hb:hb + 64, hc, :], engine="gpsimd")
            tq = sq.dma(Qtil[0:64, par * T_CORE:(par + 1) * T_CORE],
                        qv[hb:hb + 64, hc, :], engine="gpsimd", wait=w1)
            t_p = tmax(tq, lh_chain[0])
            for tile in range(4):
                qs = par * T_CORE + tile * 512
                t_p = sq.step("scalar", lambda e, qs=qs: e.activation(
                    ksq[0:64, 0:512], Qtil[0:64, qs:qs + 512], AF.Square), wait=t_p)
                t_p = sq.step("tensor", lambda e: e.matmul(
                    ps_q1[:], ones_col[0:64, :], ksq[0:64, :],
                    start=True, stop=True), wait=t_p)
                t_p = sq.step("scalar", lambda e: e.activation(
                    qn_row[:], ps_q1[:], AF.Sqrt), wait=t_p)
                t_p = sq.step("vector", lambda e, qs=qs, h=h: e.tensor_scalar(
                    Qtil[64:65, qs:qs + 512], qn_row[:], negkm[:, h:h + 1],
                    None, ALU.mult), wait=t_p)
            lh_chain[0] = t_p
            tok_kload[h] = tmax(tk, t_p)

        load_head(0)
        load_head(1)
        for h in range(NH):
            par = h % 2
            hc, hb = h // 2, 64 * (h % 2)
            for qt in range(4):
                pv = (h * 4 + qt) % 2
                qs = par * T_CORE + qt * 512
                t_S = None
                for kt in range(NKT):
                    sp = kt % 2
                    w = tmax(tok_kload[h], tok_exp.get((h, qt, kt - 2), None),
                             t_v if kt == 0 else None,
                             tok_norm[pv] if kt == 0 else None)
                    t_S = sq.step("tensor", lambda e, par=par, kt=kt, sp=sp, qs=qs:
                        e.matmul(ps_sc[:, sp * 512:sp * 512 + 512],
                                 Ktil[0:65, par * S + kt * 128:par * S + kt * 128 + 128],
                                 Qtil[0:65, qs:qs + 512], start=True, stop=True),
                        wait=w)
                    t_E = sq.step("scalar", lambda e, sp=sp: e.activation(
                        P_T[:, sp * 512:sp * 512 + 512],
                        ps_sc[:, sp * 512:sp * 512 + 512], AF.Exp,
                        scale=ATT_SCALE), wait=t_S)
                    tok_exp[(h, qt, kt)] = t_E
                    t_AV = sq.step("tensor", lambda e, h=h, kt=kt, sp=sp, pv=pv:
                        e.matmul(ps_av[0:65, pv * 512:pv * 512 + 512],
                                 vaug[:, kt * NH65 + h * 65:kt * NH65 + h * 65 + 65],
                                 P_T[:, sp * 512:sp * 512 + 512],
                                 start=(kt == 0), stop=(kt == NKT - 1)),
                        wait=t_E)
                tok_headS[h] = t_S
                t_r = sq.step("vector", lambda e, pv=pv: e.reciprocal(
                    rrow[:], ps_av[64:65, pv * 512:pv * 512 + 512]), wait=t_AV)
                t_b = sq.step("tensor", lambda e: e.matmul(
                    ps_bc[:], ones_r64[:], rrow[:], start=True, stop=True),
                    wait=tmax(t_r, tok_bcmul))
                t_c = sq.step("vector", lambda e: e.tensor_copy(rbc[:], ps_bc[:]),
                              wait=t_b)
                if hb == 0:
                    tok_bcmul = sq.step("vector",
                        lambda e, hc=hc, qt=qt, pv=pv: e.tensor_mul(
                            ofm[0:64, hc * T_CORE + qt * 512:hc * T_CORE + qt * 512 + 512],
                            ps_av[0:64, pv * 512:pv * 512 + 512], rbc[:]), wait=t_c)
                else:
                    tok_bcmul = sq.step("vector",
                        lambda e, pv=pv: e.tensor_mul(
                            odd_t[:, pv * 512:pv * 512 + 512],
                            ps_av[0:64, pv * 512:pv * 512 + 512], rbc[:]),
                        wait=tmax(t_c, t_odd_dma[pv]))
                    t_odd_dma[pv] = sq.dma(
                        ofm[64:128, hc * T_CORE + qt * 512:hc * T_CORE + qt * 512 + 512],
                        odd_t[:, pv * 512:pv * 512 + 512], wait=tok_bcmul)
                tok_norm[pv] = sq.barrier()
            if h + 2 < NH:
                load_head(h + 2)

        t_all = sq.barrier()
        t_evac = [None, None]
        t_lastadd = None
        t_store = None
        for qt in range(4):
            t0 = qt * 512
            tx = sq.dma(Xt[:].rearrange("p (c t) -> p c t", c=NCH),
                        xv_[:, :, t0:t0 + 512], wait=tmax(t_all, t_lastadd))
            for d in range(NCH):
                po = (qt * NCH + d) % 2
                t_mm = sq.step("tensor", lambda e, d=d, t0=t0, po=po: [
                    e.matmul(ps_out[:, po * 512:po * 512 + 512],
                             wo_sb[:, c * D_MODEL + d * 128:c * D_MODEL + (d + 1) * 128],
                             ofm[:, c * T_CORE + t0:c * T_CORE + t0 + 512],
                             start=(c == 0), stop=(c == NCH - 1))
                    for c in range(NCH)], wait=tmax(t_all, t_wo, t_evac[po]))
                t_evac[po] = sq.step("vector", lambda e, d=d, po=po: e.tensor_scalar(
                    oscr[:], ps_out[:, po * 512:po * 512 + 512],
                    bo_sb[:, d:d + 1], None, ALU.add),
                    wait=tmax(t_mm, tx, t_bo, t_store))
                t_lastadd = sq.step("vector", lambda e, d=d: e.tensor_add(
                    oscr[:], oscr[:], Xt[:, d * 512:(d + 1) * 512]))
                t_store = sq.dma(xo[:, d, t0:t0 + 512], oscr[:], wait=t_lastadd)

    if emit_only:
        return prog
    return seq_program(prog)


def host_mla_qkv_inputs(x, W, core):
    ids = np.arange(512, dtype=np.float64)
    theta = (10000.0 ** (-2.0 * ids)).astype(np.float32) / 512.0
    b, h = core // 2, core % 2
    pos = np.arange(h * 2048, (h + 1) * 2048, dtype=np.float32)
    ang = pos[:, None] * theta[None, :]
    return {
        "x_fm": fmT(core_slice_tokens(x, core)),
        "wdq": np.ascontiguousarray(W["Wd_q_w"]),
        "wdkv": np.ascontiguousarray(W["Wd_kv_w"]),
        "wuq": np.ascontiguousarray(W["Wu_q_w"]),
        "wqr": np.ascontiguousarray(W["Wq_r_w"]),
        "wuk": np.ascontiguousarray(W["Wu_k_w"]),
        "wuv": np.ascontiguousarray(W["Wu_v_w"]),
        "wkr": np.ascontiguousarray(W["Wk_r_w"]),
        "b_dq": pack_vec(W["Wd_q_b"]),
        "b_dkv": pack_vec(W["Wd_kv_b"]),
        "brow_uq": np.asarray(W["Wu_q_b"], np.float32)[None, :],
        "brow_qr": np.asarray(W["Wq_r_b"], np.float32)[None, :],
        "brow_kr": np.asarray(W["Wk_r_b"], np.float32)[None, :],
        "brow_uk": np.asarray(W["Wu_k_b"], np.float32)[None, :],
        "brow_uv": np.asarray(W["Wu_v_b"], np.float32)[None, :],
        "cos_t": np.cos(ang).astype(np.float32),
        "sin_t": np.sin(ang).astype(np.float32),
    }


def host_mla_attn_inputs(x, qkv_outs, W, core):
    b, h = core // 2, core % 2
    k_full = np.concatenate([qkv_outs[2 * b]["k_o"], qkv_outs[2 * b + 1]["k_o"]],
                            axis=1)
    v_full = np.concatenate([qkv_outs[2 * b]["v_o"], qkv_outs[2 * b + 1]["v_o"]],
                            axis=0)
    return {
        "q_i": qkv_outs[core]["q_o"],
        "k_i": np.ascontiguousarray(k_full),
        "vaug_i": np.ascontiguousarray(v_full),
        "x_fm": fmT(core_slice_tokens(x, core)),
        "wo": np.ascontiguousarray(W["Wo_w"]),
        "b_o": pack_vec(W["Wo_b"]),
    }



D_CKV = 256
D_CQ = 512
TOPK_NH = 8
D_HEAD = 64
D_ROPE = 32
K_TS = 96
LW = 64
ATT2_SCALE = float((D_HEAD + D_ROPE) ** -0.5)
PADK = 63


def build_topk_proj(io=None, emit_only=False):
    def prog(nc, sq, es):
        x_fm = _dram(nc, io, "x_fm", [D_MODEL, T_CORE], F32, "ExternalInput")
        wdkv = _dram(nc, io, "wdkv", [D_MODEL, D_CKV], F32, "ExternalInput")
        wuk = _dram(nc, io, "wuk", [D_CKV, 512], F32, "ExternalInput")
        wuv = _dram(nc, io, "wuv", [D_CKV, 512], F32, "ExternalInput")
        wdq = _dram(nc, io, "wdq", [D_MODEL, D_CQ], F32, "ExternalInput")
        wuq = _dram(nc, io, "wuq", [D_CQ, 512], F32, "ExternalInput")
        wqr = _dram(nc, io, "wqr", [D_CQ, 256], F32, "ExternalInput")
        wkr = _dram(nc, io, "wkr", [D_MODEL, 32], F32, "ExternalInput")
        wiq = _dram(nc, io, "wiq", [D_MODEL, 256], F32, "ExternalInput")
        wik = _dram(nc, io, "wik", [D_MODEL, 64], F32, "ExternalInput")
        q_c_tm = _dram(nc, io, "q_c_tm", [T_CORE, 512], BF16, "ExternalOutput")
        qr_tm = _dram(nc, io, "qr_tm", [T_CORE, 256], F32, "ExternalOutput")
        kv_tm = _dram(nc, io, "kv_tm", [T_CORE, 512], BF16, "ExternalOutput")
        v_tm = _dram(nc, io, "v_tm", [T_CORE, 512], BF16, "ExternalOutput")
        mu_tm = _dram(nc, io, "mu_tm", [T_CORE, 32], BF16, "ExternalOutput")
        q_c_fm = _dram(nc, io, "q_c_fm", [512, T_CORE], BF16, "ExternalOutput")
        q_i_fm = _dram(nc, io, "q_i_fm", [256, T_CORE], BF16, "ExternalOutput")
        ki_fm = _dram(nc, io, "ki_fm", [64, T_CORE], BF16, "ExternalOutput")
        kv_fm = _dram(nc, io, "kv_fm", [512, T_CORE], BF16, "ExternalOutput")
        mu_fm = _dram(nc, io, "mu_fm", [32, T_CORE], BF16, "ExternalOutput")

        xv_ = x_fm.rearrange("(c p) t -> p c t", p=128)

        sb = lambda n, s, d=F32: es.enter_context(nc.sbuf_tensor(n, s, d))
        pt = lambda n, s, d=F32: es.enter_context(nc.psum_tensor(n, s, d))

        ones_col = sb("ones_col", [128, 1])
        ones_row = sb("ones_row", [1, 128])
        eps_row = sb("eps_row", [1, 1])
        Xt = sb("Xt", [128, NCH * 512])
        sq32 = sb("sq32", [128, 512])
        a_row = sb("a_row", [1, 512])
        b_row = sb("b_row", [1, 512])
        t_row = sb("t_row", [1, 512])
        xn = sb("xn", [128, NCH * T_CORE], BF16)
        ctq = sb("ctq", [128, 4 * T_CORE], BF16)
        ctkv = sb("ctkv", [128, 2 * T_CORE], BF16)
        w_a = sb("w_a", [128, NCH * 512], BF16)
        w_b = sb("w_b", [128, NCH * 512], BF16)
        otm = sb("otm", [128, 512])
        otb = sb("otb", [128, 512], BF16)

        ps_a = pt("ps_a", [1, 512])
        ps_b = pt("ps_b", [1, 512])
        ps_bc1 = pt("ps_bc1", [128, 512])
        ps_bc2 = pt("ps_bc2", [128, 512])
        ps_m1 = pt("ps_m1", [128, 512])

        sq.step("gpsimd", lambda e: e.memset(ones_col[:], 1.0))
        sq.step("gpsimd", lambda e: e.memset(ones_row[:], 1.0))
        sq.step("gpsimd", lambda e: e.memset(eps_row[:], LN_EPS))

        for it in range(4):
            c0, w = it * 512, 512
            sq.dma(Xt[:].rearrange("p (c t) -> p c t", c=NCH), xv_[:, :, c0:c0 + w])
            sq.step("tensor", lambda e, w=w: [
                e.matmul(ps_a[:, 0:w], ones_col[:], Xt[:, c * 512:c * 512 + w],
                         start=(c == 0), stop=(c == NCH - 1)) for c in range(NCH)])
            for c in range(NCH):
                sq.step("scalar", lambda e, c=c, w=w: e.activation(
                    sq32[:, 0:w], Xt[:, c * 512:c * 512 + w], AF.Square))
                sq.step("tensor", lambda e, c=c, w=w: e.matmul(
                    ps_b[:, 0:w], ones_col[:], sq32[:, 0:w],
                    start=(c == 0), stop=(c == NCH - 1)))
            sq.step("vector", lambda e, w=w: e.tensor_single_scalar(
                b_row[:, 0:w], ps_a[:, 0:w], 1.0 / D_MODEL, ALU.mult))
            sq.step("vector", lambda e, w=w: e.tensor_single_scalar(
                a_row[:, 0:w], ps_b[:, 0:w], 1.0 / D_MODEL, ALU.mult))
            sq.step("scalar", lambda e, w=w: e.activation(
                t_row[:, 0:w], b_row[:, 0:w], AF.Square))
            sq.step("vector", lambda e, w=w: e.tensor_sub(
                a_row[:, 0:w], a_row[:, 0:w], t_row[:, 0:w]))
            sq.step("scalar", lambda e, w=w: e.activation(
                a_row[:, 0:w], a_row[:, 0:w], AF.Ln, bias=eps_row[:]))
            sq.step("scalar", lambda e, w=w: e.activation(
                a_row[:, 0:w], a_row[:, 0:w], AF.Exp, scale=-0.5))
            sq.step("tensor", lambda e, w=w: e.matmul(
                ps_bc1[:, 0:w], ones_row[:], b_row[:, 0:w]))
            sq.step("tensor", lambda e, w=w: e.matmul(
                ps_bc2[:, 0:w], ones_row[:], a_row[:, 0:w]))
            for c in range(NCH):
                sq.step("vector", lambda e, c=c, w=w: e.tensor_sub(
                    sq32[:, 0:w], Xt[:, c * 512:c * 512 + w], ps_bc1[:, 0:w]))
                sq.step("vector", lambda e, c=c, c0=c0, w=w: e.tensor_mul(
                    xn[:, c * T_CORE + c0:c * T_CORE + c0 + w], sq32[:, 0:w],
                    ps_bc2[:, 0:w]))

        for nm, wsrc, nd, dst in (("q", wdq, 4, ctq), ("kv", wdkv, 2, ctkv)):
            DOUT = nd * 128
            sq.dma(w_a[:, 0:NCH * DOUT].rearrange("p (c d) -> p c d", c=NCH),
                   wsrc.rearrange("(c p) d -> p c d", p=128), engine="gpsimd")
            for dt in range(nd):
                for tt in range(4):
                    t0 = tt * 512
                    sq.step("tensor", lambda e, dt=dt, t0=t0, DOUT=DOUT: [
                        e.matmul(ps_m1[:], w_a[:, c * DOUT + dt * 128:
                                               c * DOUT + (dt + 1) * 128],
                                 xn[:, c * T_CORE + t0:c * T_CORE + t0 + 512],
                                 start=(c == 0), stop=(c == NCH - 1))
                        for c in range(NCH)])
                    sq.step("vector", lambda e, dst=dst, dt=dt, t0=t0: e.tensor_copy(
                        dst[:, dt * T_CORE + t0:dt * T_CORE + t0 + 512], ps_m1[:]))

        def tm_proj(src, src_nc, wsrc, wid, out_t, post, wslot):
            sq.dma(wslot[:, 0:src_nc * wid].rearrange("p (c d) -> p c d", c=src_nc),
                   wsrc.rearrange("(c p) d -> p c d", p=128), engine="gpsimd")
            for qt in range(T_CORE // 128):
                t0 = qt * 128
                nn = (wid + 511) // 512
                for n in range(nn):
                    w = min(512, wid - n * 512)
                    sq.step("tensor", lambda e, src=src, wslot=wslot, src_nc=src_nc,
                            wid=wid, t0=t0, n=n, w=w: [
                        e.matmul(ps_m1[:, 0:w],
                                 src[:, c * T_CORE + t0:c * T_CORE + t0 + 128],
                                 wslot[:, c * wid + n * 512:c * wid + n * 512 + w],
                                 start=(c == 0), stop=(c == src_nc - 1))
                        for c in range(src_nc)])
                    post(t0, n, w)

        def mk_post_bf16(out_t):
            def post(t0, n, w):
                sq.step("vector", lambda e, t0=t0, n=n, w=w: e.tensor_copy(
                    otb[:, 0:w], ps_m1[:, 0:w]))
                sq.dma(out_t[t0:t0 + 128, n * 512:n * 512 + w], otb[:, 0:w])
            return post

        tm_proj(ctq, 4, wuq, 512, q_c_tm, mk_post_bf16(q_c_tm), w_b)

        def post_qr(t0, n, w):
            sq.step("vector", lambda e, w=w: e.tensor_copy(otm[:, 0:w], ps_m1[:, 0:w]))
            sq.dma(qr_tm[t0:t0 + 128, 0:w], otm[:, 0:w])
        tm_proj(ctq, 4, wqr, 256, qr_tm, post_qr, w_b)

        tm_proj(ctkv, 2, wuk, 512, kv_tm, mk_post_bf16(kv_tm), w_b)
        tm_proj(ctkv, 2, wuv, 512, v_tm, mk_post_bf16(v_tm), w_b)

        def post_mu(t0, n, w):
            sq.step("scalar", lambda e, w=w: e.activation(
                otm[:, 0:w], ps_m1[:, 0:w], AF.Abs))
            sq.step("scalar", lambda e, w=w: e.activation(
                otm[:, 0:w], otm[:, 0:w], AF.Exp, scale=-1.0))
            sq.step("scalar", lambda e, w=w: e.activation(
                otm[:, 0:w], otm[:, 0:w], AF.Ln, bias=1.0))
            sq.step("vector", lambda e, w=w: e.tensor_single_scalar(
                sq32[:, 0:w], ps_m1[:, 0:w], 0.0, ALU.max))
            sq.step("vector", lambda e, w=w: e.tensor_add(
                otm[:, 0:w], otm[:, 0:w], sq32[:, 0:w]))
            sq.dma(mu_tm[t0:t0 + 128, 0:w], otm[:, 0:w], engine="gpsimd")
        tm_proj(xn, NCH, wkr, 32, mu_tm, post_mu, w_b)

        def fm_proj(src, src_nc, wsrc, dout, out_t, softplus=False):
            sq.dma(w_b[:, 0:src_nc * dout].rearrange("p (c d) -> p c d", c=src_nc),
                   wsrc.rearrange("(c p) d -> p c d", p=128), engine="gpsimd")
            nrow = min(dout, 128)
            for dt in range(max(1, dout // 128)):
                for tt in range(4):
                    t0 = tt * 512
                    sq.step("tensor", lambda e, dt=dt, t0=t0, src=src,
                            src_nc=src_nc, dout=dout, nrow=nrow: [
                        e.matmul(ps_m1[0:nrow, :],
                                 w_b[:, c * dout + dt * 128:c * dout + dt * 128 + nrow],
                                 src[:, c * T_CORE + t0:c * T_CORE + t0 + 512],
                                 start=(c == 0), stop=(c == src_nc - 1))
                        for c in range(src_nc)])
                    if not softplus:
                        sq.step("vector", lambda e, nrow=nrow: e.tensor_copy(
                            otb[0:nrow, :], ps_m1[0:nrow, :]))
                    else:
                        sq.step("scalar", lambda e, nrow=nrow: e.activation(
                            otm[0:nrow, 0:512], ps_m1[0:nrow, :], AF.Abs))
                        sq.step("scalar", lambda e, nrow=nrow: e.activation(
                            otm[0:nrow, 0:512], otm[0:nrow, 0:512], AF.Exp, scale=-1.0))
                        sq.step("scalar", lambda e, nrow=nrow: e.activation(
                            otm[0:nrow, 0:512], otm[0:nrow, 0:512], AF.Ln, bias=1.0))
                        sq.step("vector", lambda e, nrow=nrow: e.tensor_single_scalar(
                            sq32[0:nrow, 0:512], ps_m1[0:nrow, :], 0.0, ALU.max))
                        sq.step("vector", lambda e, nrow=nrow: e.tensor_add(
                            otb[0:nrow, :], otm[0:nrow, 0:512], sq32[0:nrow, 0:512]))
                    sq.dma(out_t[dt * 128:dt * 128 + nrow, t0:t0 + 512],
                           otb[0:nrow, :])

        fm_proj(ctq, 4, wuq, 512, q_c_fm)
        fm_proj(xn, NCH, wiq, 256, q_i_fm)
        fm_proj(xn, NCH, wik, 64, ki_fm)
        fm_proj(ctkv, 2, wuk, 512, kv_fm)
        fm_proj(xn, NCH, wkr, 32, mu_fm, softplus=True)

    if emit_only:
        return prog
    return seq_program(prog)


def host_topk_proj_inputs(x, W, core):
    return {
        "x_fm": fmT(core_slice_tokens(x, core)),
        "wdkv": np.ascontiguousarray(W["W_dkv"]),
        "wuk": np.ascontiguousarray(W["W_uk"]),
        "wuv": np.ascontiguousarray(W["W_uv"]),
        "wdq": np.ascontiguousarray(W["W_dq"]),
        "wuq": np.ascontiguousarray(W["W_uq"]),
        "wqr": np.ascontiguousarray(W["W_qr"]),
        "wkr": np.ascontiguousarray(W["W_kr"]),
        "wiq": np.ascontiguousarray(W["idx_wq"]),
        "wik": np.ascontiguousarray(W["idx_wk"]),
    }


def build_topk_attn(debug=False, io=None, emit_only=False):
    S = 4096
    SPB = 2112
    NQT = T_CORE // 128

    def prog(nc, sq, es):
        dt_ = lambda nm, shp, dty, kind: _dram(nc, io, nm, shp, dty, kind)
        x_fm = dt_("x_fm", [D_MODEL, T_CORE], F32, "ExternalInput")
        q_c_tm = dt_("q_c_tm", [T_CORE, 512], BF16, "ExternalInput")
        q_c_fmi = dt_("q_c_fmi", [512, T_CORE], BF16, "ExternalInput")
        qr_tmi = dt_("qr_tmi", [T_CORE, 256], F32, "ExternalInput")
        q_i_fmi = dt_("q_i_fmi", [256, T_CORE], BF16, "ExternalInput")
        k_i_fmi = dt_("k_i_fmi", [128, S], BF16, "ExternalInput")
        kv_fm_pad = dt_("kv_fm_pad", [512, SPB], BF16, "ExternalInput")
        tkt_full = dt_("tkt_full", [S, 1056], BF16, "ExternalInput")
        kv_tm_full = dt_("kv_tm_full", [S, 512], BF16, "ExternalInput")
        v_tm_full = dt_("v_tm_full", [S, 512], BF16, "ExternalInput")
        v_pad = dt_("v_pad", [SPB, 512], BF16, "ExternalInput")
        mu1_pad = dt_("mu1_pad", [16, SPB], BF16, "ExternalInput")
        mu2_pad = dt_("mu2_pad", [16, SPB], BF16, "ExternalInput")
        mu_tm_full = dt_("mu_tm_full", [S, 32], BF16, "ExternalInput")
        w_out = dt_("w_out", [512, D_MODEL], F32, "ExternalInput")
        idxw_row = dt_("idxw_row", [1, 4], F32, "ExternalInput")
        rdelta = dt_("rdelta", [1, 16], F32, "ExternalInput")
        thr_i = dt_("thr_i", [128, NQT], F32, "ExternalInput")
        ramp0 = dt_("ramp0", [128, 96], F32, "ExternalInput")
        mask0 = dt_("mask0", [128, 512], U8, "ExternalInput")
        cth_q = dt_("cth_q", [T_CORE, 16], F32, "ExternalInput")
        sth_q = dt_("sth_q", [T_CORE, 16], F32, "ExternalInput")
        cth_q2 = dt_("cth_q2", [T_CORE, 16], F32, "ExternalInput")
        sth_q2 = dt_("sth_q2", [T_CORE, 16], F32, "ExternalInput")
        cjb_i = dt_("cjb_i", [16, 192], F32, "ExternalInput")
        sjb_i = dt_("sjb_i", [16, 192], F32, "ExternalInput")
        cabs96 = dt_("cabs96", [16, 96], F32, "ExternalInput")
        sabs96 = dt_("sabs96", [16, 96], F32, "ExternalInput")
        crk_i = dt_("crk_i", [128, 512], F32, "ExternalInput")
        srk_i = dt_("srk_i", [128, 512], F32, "ExternalInput")
        x_out = dt_("x_out", [D_MODEL, T_CORE], F32, "ExternalOutput")
        if debug:
            dbg = {nm: dt_(f"dbg_{nm}", shp, dty, kind="ExternalOutput").ap()
                   for nm, shp, dty in (
                       ("I", [128, 4096], F32), ("idx", [128, 32], U32),
                       ("sctop", [128, 32], F32), ("band", [128, 192], F32),
                       ("qh1", [128, 128], F32), ("qh2", [128, 128], F32),
                       ("mh1", [128, 512], F32), ("mh2", [128, 512], F32),
                       ("outtm", [128, 512], F32), ("rs", [128, 1], F32),
                       ("ka", [16, 192], F32), ("qpa", [16, 1024], F32),
                       ("mug", [128, 1024], F32))}

        xv_ = x_fm.rearrange("(c p) t -> p c t", p=128)
        xo = x_out.rearrange("(c p) t -> p c t", p=128)

        sb = lambda n, s, d=F32: es.enter_context(nc.sbuf_tensor(n, s, d))
        pt = lambda n, s, d=F32: es.enter_context(nc.psum_tensor(n, s, d))

        ones_row = sb("ones_row", [1, 128])
        identw = sb("identw", [128, 128])
        identb = sb("identb", [128, 128], BF16)
        kvf_sb = sb("kvf_sb", [128, 4 * SPB], BF16)
        qcf_sb = sb("qcf_sb", [128, 4 * T_CORE], BF16)
        qif_sb = sb("qif_sb", [128, 2 * T_CORE], BF16)
        kif_sb = sb("kif_sb", [128, S], BF16)
        mu1_sb = sb("mu1_sb", [16, SPB], BF16)
        mu2_sb = sb("mu2_sb", [16, SPB], BF16)
        wo_sb = sb("wo_sb", [128, 4 * D_MODEL], BF16)
        iw_bc = sb("iw_bc", [128, 4])
        thr_sb = sb("thr_sb", [128, NQT])
        ramp_sb = sb("ramp_sb", [128, 96])
        mask0_sb = sb("mask0_sb", [128, 512], U8)
        crkd = sb("crkd", [128, 512])
        srkd = sb("srkd", [128, 512])
        drow = sb("drow", [1, 16])
        cd_row = sb("cd_row", [1, 16])
        sd_row = sb("sd_row", [1, 16])
        cd_bc = sb("cd_bc", [128, 16])
        sd_bc = sb("sd_bc", [128, 16])
        cd_pp = sb("cd_pp", [16, 1])
        sd_pp = sb("sd_pp", [16, 1])
        tr16 = sb("tr16", [1, 16])
        cjb_sb = sb("cjb_sb", [16, 192])
        sjb_sb = sb("sjb_sb", [16, 192])
        Cjb = sb("Cjb", [16, 192], BF16)
        Sjb = sb("Sjb", [16, 192], BF16)
        c96 = sb("c96", [16, 96])
        s96 = sb("s96", [16, 96])
        C96 = sb("C96", [16, 96], BF16)
        S96 = sb("S96", [16, 96], BF16)
        ka_band = sb("ka_band", [16, 192], BF16)
        kb_band = sb("kb_band", [16, 192], BF16)
        ka96 = sb("ka96", [16, 96], BF16)
        kb96 = sb("kb96", [16, 96], BF16)
        I_sb = sb("I_sb", [128, S])
        scrB = sb("scrB", [128, 2048])
        scrC = sb("scrC", [128, 512])
        m8 = sb("m8", [128, 8])
        idx32 = sb("idx32", [128, 32], U32)
        gall = sb("gall", [128, 32 * 1056], BF16)
        qr_t = sb("qr_t", [128, 256])
        qm_t = sb("qm_t", [128, 256])
        cth_t = sb("cth_t", [128, 16])
        sth_t = sb("sth_t", [128, 16])
        ct2_t = sb("ct2_t", [128, 16])
        st2_t = sb("st2_t", [128, 16])
        ctd = sb("ctd", [128, 16])
        std_ = sb("std_", [128, 16])
        ct2d = sb("ct2d", [128, 16])
        st2d = sb("st2d", [128, 16])
        qh1 = sb("qh1", [128, 128])
        qh2 = sb("qh2", [128, 128])
        qp1 = sb("qp1", [128, 128])
        qp2 = sb("qp2", [128, 128])
        qp1b = sb("qp1b", [128, 128], BF16)
        qp2b = sb("qp2b", [128, 128], BF16)
        qpa_fm = sb("qpa_fm", [16, 8 * 128], BF16)
        qpb_fm = sb("qpb_fm", [16, 8 * 128], BF16)
        qsa_fm = sb("qsa_fm", [16, 8 * 128], BF16)
        qsb_fm = sb("qsb_fm", [16, 8 * 128], BF16)
        mh1 = sb("mh1", [128, 512])
        mh2 = sb("mh2", [128, 512])
        sb_band = sb("sb_band", [128, 192])
        sc_top = sb("sc_top", [128, 32])
        mb_c = sb("mb_c", [128, 1])
        mt_c = sb("mt_c", [128, 1])
        bias_c = sb("bias_c", [128, 1])
        rs1 = sb("rs1", [128, 1])
        rs2 = sb("rs2", [128, 1])
        rr_c = sb("rr_c", [128, 1])
        P_band = sb("P_band", [128, 192], BF16)
        P_top = sb("P_top", [128, 32])
        PbT1 = sb("PbT1", [128, 128], BF16)
        PbT2 = sb("PbT2", [64, 128], BF16)
        Vb1 = sb("Vb1", [128, 512], BF16)
        Vb2 = sb("Vb2", [64, 512], BF16)
        v96_sb = sb("v96_sb", [96, 512], BF16)
        ot_top = sb("ot_top", [128, 64])
        out_tm = sb("out_tm", [128, 512])
        dense_tm = sb("dense_tm", [128, 512])
        otb = sb("otb", [128, 512], BF16)
        ofm = sb("ofm", [128, 4 * 128], BF16)
        Xt = sb("Xt", [128, NCH * 128])
        oscr = sb("oscr", [128, 128])
        qct_t = sb("qct_t", [128, 512], BF16)

        ps_i = pt("ps_i", [128, 512])
        ps_band = pt("ps_band", [128, 192])
        ps_96 = pt("ps_96", [128, 96])
        ps_av = pt("ps_av", [128, 64])
        ps_tr = pt("ps_tr", [128, 128], BF16)
        ps_trf = pt("ps_trf", [128, 128])
        ps_out = pt("ps_out", [128, 128])

        sq.step("gpsimd", lambda e: e.memset(ones_row[:], 1.0))
        sq.step("gpsimd", lambda e: e.memset(identw[:], 1.0))
        sq.step("gpsimd", lambda e: e.affine_select(
            identw[:], identw[:], pattern=[[1, 128]], compare_op=ALU.is_equal,
            fill=0.0, base=0, channel_multiplier=-1))
        sq.step("vector", lambda e: e.tensor_copy(identb[:], identw[:]))
        sq.dma(kvf_sb[:].rearrange("p (c t) -> p c t", c=4),
               kv_fm_pad.rearrange("(c p) t -> p c t", p=128))
        sq.dma(qcf_sb[:].rearrange("p (c t) -> p c t", c=4),
               q_c_fmi.rearrange("(c p) t -> p c t", p=128))
        sq.dma(qif_sb[:].rearrange("p (c t) -> p c t", c=2),
               q_i_fmi.rearrange("(c p) t -> p c t", p=128))
        sq.dma(kif_sb[:], k_i_fmi)
        sq.dma(mu1_sb[:], mu1_pad)
        sq.dma(mu2_sb[:], mu2_pad)
        sq.dma(wo_sb[:].rearrange("p (c d) -> p c d", c=4),
               w_out.rearrange("(c p) d -> p c d", p=128), engine="gpsimd")
        sq.dma(thr_sb[:], thr_i)
        sq.dma(ramp_sb[:], ramp0)
        sq.dma(mask0_sb[:], mask0)
        sq.dma(crkd[:], crk_i)
        sq.dma(srkd[:], srk_i)
        sq.dma(cjb_sb[:], cjb_i)
        sq.dma(sjb_sb[:], sjb_i)
        sq.dma(c96[:], cabs96)
        sq.dma(s96[:], sabs96)
        sq.dma(drow[:], rdelta)
        sq.dma(tr16[:, 0:4], idxw_row)
        sq.step("tensor", lambda e: e.matmul(ps_i[:, 0:4], ones_row[:], tr16[:, 0:4]))
        sq.step("vector", lambda e: e.tensor_copy(iw_bc[:], ps_i[:, 0:4]))
        sq.step("scalar", lambda e: e.activation(drow[:], drow[:], AF.Sigmoid))
        sq.step("vector", lambda e: e.tensor_single_scalar(
            drow[:], drow[:], -2.0 * np.pi, ALU.mult))
        sq.step("vector", lambda e: e.tensor_single_scalar(
            tr16[:], drow[:], -np.pi, ALU.is_lt))
        sq.step("vector", lambda e: e.scalar_tensor_tensor(
            drow[:], tr16[:], 2.0 * np.pi, drow[:], ALU.mult, ALU.add))
        sq.step("scalar", lambda e: e.activation(sd_row[:], drow[:], AF.Sin))
        sq.step("vector", lambda e: e.tensor_single_scalar(
            cd_row[:], drow[:], np.pi / 2, ALU.add))
        sq.step("vector", lambda e: e.tensor_single_scalar(
            tr16[:], cd_row[:], np.pi, ALU.is_gt))
        sq.step("vector", lambda e: e.scalar_tensor_tensor(
            cd_row[:], tr16[:], -2.0 * np.pi, cd_row[:], ALU.mult, ALU.add))
        sq.step("scalar", lambda e: e.activation(cd_row[:], cd_row[:], AF.Sin))
        for src, dstb, dstp in ((cd_row, cd_bc, cd_pp), (sd_row, sd_bc, sd_pp)):
            sq.step("tensor", lambda e, src=src: e.matmul(
                ps_i[:, 0:16], ones_row[:], src[:]))
            sq.step("vector", lambda e, dstb=dstb: e.tensor_copy(dstb[:], ps_i[:, 0:16]))
            sq.step("tensor", lambda e, src=src: e.transpose(
                ps_trf[0:16, 0:1], src[:], identw[0:1, 0:1]))
            sq.step("vector", lambda e, dstp=dstp: e.tensor_copy(
                dstp[:], ps_trf[0:16, 0:1]))
        cdrep = cd_bc[:].rearrange("p (o i) -> p o i", o=1).to_broadcast([128, 32, 16])
        sdrep = sd_bc[:].rearrange("p (o i) -> p o i", o=1).to_broadcast([128, 32, 16])
        crkv = crkd[:].rearrange("p (r i) -> p r i", i=16)
        srkv = srkd[:].rearrange("p (r i) -> p r i", i=16)
        sq.step("vector", lambda e: e.tensor_mul(
            scrB[:, 0:512].rearrange("p (r i) -> p r i", i=16), crkv, cdrep))
        sq.step("vector", lambda e: e.tensor_mul(
            scrB[:, 512:1024].rearrange("p (r i) -> p r i", i=16), srkv, sdrep))
        sq.step("vector", lambda e: e.tensor_mul(
            scrB[:, 1024:1536].rearrange("p (r i) -> p r i", i=16), srkv, cdrep))
        sq.step("vector", lambda e: e.tensor_mul(
            scrB[:, 1536:2048].rearrange("p (r i) -> p r i", i=16), crkv, sdrep))
        sq.step("vector", lambda e: e.tensor_sub(
            crkd[:], scrB[:, 0:512], scrB[:, 512:1024]))
        sq.step("vector", lambda e: e.tensor_add(
            srkd[:], scrB[:, 1024:1536], scrB[:, 1536:2048]))
        sq.step("vector", lambda e: e.tensor_scalar(
            scrB[0:16, 0:192], cjb_sb[:], cd_pp[:], None, ALU.mult))
        sq.step("vector", lambda e: e.tensor_scalar(
            scrC[0:16, 0:192], sjb_sb[:], sd_pp[:], None, ALU.mult))
        sq.step("vector", lambda e: e.tensor_sub(
            Cjb[:], scrB[0:16, 0:192], scrC[0:16, 0:192]))
        sq.step("vector", lambda e: e.tensor_scalar(
            scrB[0:16, 0:192], sjb_sb[:], cd_pp[:], None, ALU.mult))
        sq.step("vector", lambda e: e.tensor_scalar(
            scrC[0:16, 0:192], cjb_sb[:], sd_pp[:], None, ALU.mult))
        sq.step("vector", lambda e: e.tensor_add(
            Sjb[:], scrB[0:16, 0:192], scrC[0:16, 0:192]))
        sq.step("vector", lambda e: e.tensor_scalar(
            scrB[0:16, 0:96], c96[:], cd_pp[:], None, ALU.mult))
        sq.step("vector", lambda e: e.tensor_scalar(
            scrC[0:16, 0:96], s96[:], sd_pp[:], None, ALU.mult))
        sq.step("vector", lambda e: e.tensor_sub(
            C96[:], scrB[0:16, 0:96], scrC[0:16, 0:96]))
        sq.step("vector", lambda e: e.tensor_scalar(
            scrB[0:16, 0:96], s96[:], cd_pp[:], None, ALU.mult))
        sq.step("vector", lambda e: e.tensor_scalar(
            scrC[0:16, 0:96], c96[:], sd_pp[:], None, ALU.mult))
        sq.step("vector", lambda e: e.tensor_add(
            S96[:], scrB[0:16, 0:96], scrC[0:16, 0:96]))
        sq.step("vector", lambda e: e.tensor_mul(
            scrB[0:16, 0:96], mu1_sb[0:16, PADK:PADK + 96], C96[:]))
        sq.step("vector", lambda e: e.tensor_mul(
            scrC[0:16, 0:96], mu2_sb[0:16, PADK:PADK + 96], S96[:]))
        sq.step("vector", lambda e: e.tensor_sub(
            ka96[:], scrB[0:16, 0:96], scrC[0:16, 0:96]))
        sq.step("vector", lambda e: e.tensor_mul(
            scrB[0:16, 0:96], mu1_sb[0:16, PADK:PADK + 96], S96[:]))
        sq.step("vector", lambda e: e.tensor_mul(
            scrC[0:16, 0:96], mu2_sb[0:16, PADK:PADK + 96], C96[:]))
        sq.step("vector", lambda e: e.tensor_add(
            kb96[:], scrB[0:16, 0:96], scrC[0:16, 0:96]))
        sq.dma(v96_sb[:], v_pad[PADK:PADK + 96, :])
        iota_c = sb("iota_c", [128, 512])
        bandmask = sb("bandmask", [128, 192])
        thr_adj = sb("thr_adj", [128, 1])
        sq.step("gpsimd", lambda e: e.iota(
            iota_c[:], pattern=[[1, 512]], base=0, channel_multiplier=0,
            allow_small_or_imprecise_dtypes=True))
        sq.step("gpsimd", lambda e: e.memset(bandmask[:], 0.0))
        sq.step("gpsimd", lambda e: e.affine_select(
            bandmask[:], bandmask[:], pattern=[[1, 192]], compare_op=ALU.is_ge,
            fill=-3e38, base=0, channel_multiplier=-1))
        sq.step("gpsimd", lambda e: e.affine_select(
            bandmask[:], bandmask[:], pattern=[[-1, 192]], compare_op=ALU.is_ge,
            fill=-3e38, base=63, channel_multiplier=1))

        for qt in range(NQT):
            t0 = qt * 128
            for kt in range(8):
                for hh in range(4):
                    hb = 64 * (hh % 2)
                    hc = hh // 2
                    sq.step("tensor", lambda e, hb=hb, hc=hc, t0=t0, kt=kt: e.matmul(
                        ps_i[:], qif_sb[hb:hb + 64,
                                        hc * T_CORE + t0:hc * T_CORE + t0 + 128],
                        kif_sb[hb:hb + 64, kt * 512:(kt + 1) * 512],
                        start=True, stop=True))
                    if hh == 0:
                        sq.step("vector", lambda e, kt=kt, hh=hh: e.tensor_scalar(
                            I_sb[:, kt * 512:(kt + 1) * 512], ps_i[:], 0.0,
                            iw_bc[:, hh:hh + 1], ALU.max, ALU.mult))
                    else:
                        sq.step("vector", lambda e, kt=kt, hh=hh: e.tensor_scalar(
                            scrC[:], ps_i[:], 0.0, iw_bc[:, hh:hh + 1],
                            ALU.max, ALU.mult))
                        sq.step("vector", lambda e, kt=kt: e.tensor_add(
                            I_sb[:, kt * 512:(kt + 1) * 512],
                            I_sb[:, kt * 512:(kt + 1) * 512], scrC[:]))
            for kt in range(8):
                sq.step("vector", lambda e, qt=qt, kt=kt: e.tensor_single_scalar(
                    thr_adj[:], thr_sb[:, qt:qt + 1], float(-512 * kt), ALU.add))
                sq.step("vector", lambda e: e.tensor_scalar(
                    scrC[:], iota_c[:], thr_adj[:], None, ALU.is_le))
                sq.step("vector", lambda e: e.tensor_scalar(
                    scrC[:], scrC[:], 2e38, -1e38, ALU.mult, ALU.add))
                sq.step("vector", lambda e, kt=kt: e.tensor_tensor(
                    I_sb[:, kt * 512:(kt + 1) * 512],
                    I_sb[:, kt * 512:(kt + 1) * 512], scrC[:], ALU.min))
            if qt == 0:
                sq.step("vector", lambda e: e.tensor_max(
                    I_sb[:, 0:96], I_sb[:, 0:96], ramp_sb[:]))
            for rd in range(4):
                sq.step("vector", lambda e: e.max(out=m8[:], in_=I_sb[:]))
                sq.step("vector", lambda e, rd=rd: e.max_index(
                    out=idx32[:, rd * 8:(rd + 1) * 8], in_max=m8[:], in_values=I_sb[:]))
                sq.step("vector", lambda e: e.match_replace(
                    out=I_sb[:], in_to_replace=m8[:], in_values=I_sb[:],
                    imm_value=-3e38))
            for r in range(32):
                sq.step("gpsimd", lambda e, r=r: e.indirect_dma_start(
                    out=gall[:, r * 1056:(r + 1) * 1056], out_offset=None,
                    in_=tkt_full,
                    in_offset=IndirectOffsetOnAxis(ap=idx32[:, r:r + 1], axis=0),
                    oob_is_err=False), inc=16, dma=True)
            if debug and qt == 8:
                sq.dma(dbg["I"], I_sb[:])
                sq.dma(dbg["idx"], idx32[:])
            sq.dma(qr_t[:], qr_tmi[t0:t0 + 128, :])
            sq.dma(qct_t[:], q_c_tm[t0:t0 + 128, :])
            sq.dma(cth_t[:], cth_q[t0:t0 + 128, :])
            sq.dma(sth_t[:], sth_q[t0:t0 + 128, :])
            sq.dma(ct2_t[:], cth_q2[t0:t0 + 128, :])
            sq.dma(st2_t[:], sth_q2[t0:t0 + 128, :])
            sq.step("scalar", lambda e: e.activation(qm_t[:], qr_t[:], AF.Abs))
            sq.step("scalar", lambda e: e.activation(qm_t[:], qm_t[:], AF.Exp, scale=-1.0))
            sq.step("scalar", lambda e: e.activation(qm_t[:], qm_t[:], AF.Ln, bias=1.0))
            sq.step("vector", lambda e: e.tensor_single_scalar(
                scrB[:, 0:256], qr_t[:], 0.0, ALU.max))
            sq.step("vector", lambda e: e.tensor_add(qm_t[:], qm_t[:], scrB[:, 0:256]))
            for (csrc, ssrc, cdst, sdst) in ((cth_t, sth_t, ctd, std_),
                                             (ct2_t, st2_t, ct2d, st2d)):
                sq.step("vector", lambda e, csrc=csrc, cdst=cdst: e.tensor_mul(
                    cdst[:], csrc[:], cd_bc[:]))
                sq.step("vector", lambda e, ssrc=ssrc: e.tensor_mul(
                    scrC[:, 0:16], ssrc[:], sd_bc[:]))
                sq.step("vector", lambda e, cdst=cdst: e.tensor_sub(
                    cdst[:], cdst[:], scrC[:, 0:16]))
                sq.step("vector", lambda e, ssrc=ssrc, sdst=sdst: e.tensor_mul(
                    sdst[:], ssrc[:], cd_bc[:]))
                sq.step("vector", lambda e, csrc=csrc: e.tensor_mul(
                    scrC[:, 0:16], csrc[:], sd_bc[:]))
                sq.step("vector", lambda e, sdst=sdst: e.tensor_add(
                    sdst[:], sdst[:], scrC[:, 0:16]))
            qm1 = qm_t[:].rearrange("p (h two i) -> p h two i", two=2, i=16)[:, :, 0, :]
            qm2 = qm_t[:].rearrange("p (h two i) -> p h two i", two=2, i=16)[:, :, 1, :]
            for (cc, ss, d1, d2) in ((ctd, std_, qh1, qh2), (ct2d, st2d, qp1, qp2)):
                ccb = cc[:].rearrange("p (o i) -> p o i", o=1).to_broadcast([128, 8, 16])
                ssb = ss[:].rearrange("p (o i) -> p o i", o=1).to_broadcast([128, 8, 16])
                d1v = d1[:].rearrange("p (h i) -> p h i", i=16)
                d2v = d2[:].rearrange("p (h i) -> p h i", i=16)
                sB = scrB[:, 0:128].rearrange("p (h i) -> p h i", i=16)
                sq.step("vector", lambda e, d1v=d1v, ccb=ccb: e.tensor_mul(d1v, qm1, ccb))
                sq.step("vector", lambda e, sB=sB, ssb=ssb: e.tensor_mul(sB, qm2, ssb))
                sq.step("vector", lambda e, d1=d1: e.tensor_sub(
                    d1[:], d1[:], scrB[:, 0:128]))
                sq.step("vector", lambda e, d2v=d2v, ssb=ssb: e.tensor_mul(d2v, qm1, ssb))
                sq.step("vector", lambda e, sB=sB, ccb=ccb: e.tensor_mul(sB, qm2, ccb))
                sq.step("vector", lambda e, d2=d2: e.tensor_add(
                    d2[:], d2[:], scrB[:, 0:128]))
            sq.step("vector", lambda e: e.tensor_copy(qp1b[:], qp1[:]))
            sq.step("vector", lambda e: e.tensor_copy(qp2b[:], qp2[:]))
            for h in range(8):
                sq.step("tensor", lambda e, h=h: e.transpose(
                    ps_tr[0:16, 0:128], qp1b[:, h * 16:(h + 1) * 16], identb[:]))
                sq.step("vector", lambda e, h=h: e.tensor_copy(
                    qpa_fm[:, h * 128:(h + 1) * 128], ps_tr[0:16, 0:128]))
                sq.step("tensor", lambda e, h=h: e.transpose(
                    ps_tr[0:16, 0:128], qp2b[:, h * 16:(h + 1) * 16], identb[:]))
                sq.step("vector", lambda e, h=h: e.tensor_copy(
                    qpb_fm[:, h * 128:(h + 1) * 128], ps_tr[0:16, 0:128]))
            if qt == 0:
                sq.step("vector", lambda e: e.tensor_copy(qp1b[:], qh1[:]))
                sq.step("vector", lambda e: e.tensor_copy(qp2b[:], qh2[:]))
                for h in range(8):
                    sq.step("tensor", lambda e, h=h: e.transpose(
                        ps_tr[0:16, 0:128], qp1b[:, h * 16:(h + 1) * 16], identb[:]))
                    sq.step("vector", lambda e, h=h: e.tensor_copy(
                        qsa_fm[:, h * 128:(h + 1) * 128], ps_tr[0:16, 0:128]))
                    sq.step("tensor", lambda e, h=h: e.transpose(
                        ps_tr[0:16, 0:128], qp2b[:, h * 16:(h + 1) * 16], identb[:]))
                    sq.step("vector", lambda e, h=h: e.tensor_copy(
                        qsb_fm[:, h * 128:(h + 1) * 128], ps_tr[0:16, 0:128]))
            mg1 = gall[:].rearrange("p (r v) -> p r v", v=1056)[:, :, 1024:1040]
            mg2 = gall[:].rearrange("p (r v) -> p r v", v=1056)[:, :, 1040:1056]
            crv = crkd[:].rearrange("p (r i) -> p r i", i=16)
            srv = srkd[:].rearrange("p (r i) -> p r i", i=16)
            m1v = mh1[:].rearrange("p (r i) -> p r i", i=16)
            m2v = mh2[:].rearrange("p (r i) -> p r i", i=16)
            sBv = scrB[:, 0:512].rearrange("p (r i) -> p r i", i=16)
            sq.step("vector", lambda e: e.tensor_mul(m1v, mg1, crv))
            sq.step("vector", lambda e: e.tensor_mul(sBv, mg2, srv))
            sq.step("vector", lambda e: e.tensor_sub(mh1[:], mh1[:], scrB[:, 0:512]))
            sq.step("vector", lambda e: e.tensor_mul(m2v, mg1, srv))
            sq.step("vector", lambda e: e.tensor_mul(sBv, mg2, crv))
            sq.step("vector", lambda e: e.tensor_add(mh2[:], mh2[:], scrB[:, 0:512]))
            for (dst, A, Bt) in ((ka_band, Cjb, Sjb), (kb_band, Sjb, Cjb)):
                sq.step("vector", lambda e, t0=t0, A=A: e.tensor_mul(
                    scrB[0:16, 0:192], mu1_sb[0:16, t0:t0 + 192], A[:]))
                sq.step("vector", lambda e, t0=t0, Bt=Bt: e.tensor_mul(
                    scrC[0:16, 0:192], mu2_sb[0:16, t0:t0 + 192], Bt[:]))
                if dst is ka_band:
                    sq.step("vector", lambda e, dst=dst: e.tensor_sub(
                        dst[:], scrB[0:16, 0:192], scrC[0:16, 0:192]))
                else:
                    sq.step("vector", lambda e, dst=dst: e.tensor_add(
                        dst[:], scrB[0:16, 0:192], scrC[0:16, 0:192]))
            sq.dma(Vb1[:], v_pad[t0:t0 + 128, :])
            sq.dma(Vb2[:], v_pad[t0 + 128:t0 + 192, :])

            for h in range(8):
                hb = 64 * (h % 2)
                hc = h // 2
                sq.step("tensor", lambda e, hb=hb, hc=hc, t0=t0: e.matmul(
                    ps_band[:], qcf_sb[hb:hb + 64, hc * T_CORE + t0:hc * T_CORE + t0 + 128],
                    kvf_sb[hb:hb + 64, hc * SPB + t0:hc * SPB + t0 + 192],
                    start=True, stop=False))
                sq.step("tensor", lambda e, h=h: e.matmul(
                    ps_band[:], qpa_fm[:, h * 128:(h + 1) * 128], ka_band[:],
                    start=False, stop=False))
                sq.step("tensor", lambda e, h=h: e.matmul(
                    ps_band[:], qpb_fm[:, h * 128:(h + 1) * 128], kb_band[:],
                    start=False, stop=True))
                sq.step("vector", lambda e: e.tensor_add(
                    sb_band[:], ps_band[:], bandmask[:]))
                sq.step("vector", lambda e: e.reduce_max(mb_c[:], sb_band[:], axis=AX.X))
                qcb = qct_t[:].rearrange("p (h i) -> p h i", i=64)[
                    :, h:h + 1, :].to_broadcast([128, 32, 64])
                kgv = gall[:].rearrange("p (r v) -> p r v", v=1056)[:, :, h * 64:(h + 1) * 64]
                sq.step("vector", lambda e, qcb=qcb, kgv=kgv: e.tensor_mul(
                    scrB[:, 0:2048].rearrange("p (r i) -> p r i", i=64), kgv, qcb))
                sq.step("vector", lambda e: e.reduce_sum(
                    sc_top[:], scrB[:, 0:2048].rearrange("p (r i) -> p r i", i=64),
                    axis=AX.X))
                q1b = qh1[:].rearrange("p (h i) -> p h i", i=16)[
                    :, h:h + 1, :].to_broadcast([128, 32, 16])
                q2b = qh2[:].rearrange("p (h i) -> p h i", i=16)[
                    :, h:h + 1, :].to_broadcast([128, 32, 16])
                sq.step("vector", lambda e, q1b=q1b: e.tensor_mul(
                    scrB[:, 0:512].rearrange("p (r i) -> p r i", i=16), m1v, q1b))
                sq.step("vector", lambda e, q2b=q2b: e.tensor_mul(
                    scrB[:, 512:1024].rearrange("p (r i) -> p r i", i=16), m2v, q2b))
                sq.step("vector", lambda e: e.tensor_add(
                    scrB[:, 0:512], scrB[:, 0:512], scrB[:, 512:1024]))
                sq.step("vector", lambda e: e.reduce_sum(
                    scrC[:, 0:32], scrB[:, 0:512].rearrange("p (r i) -> p r i", i=16),
                    axis=AX.X))
                sq.step("vector", lambda e: e.tensor_add(
                    sc_top[:], sc_top[:], scrC[:, 0:32]))
                if debug and qt == 8 and h == 0:
                    sq.dma(dbg["sctop"], sc_top[:])
                    sq.dma(dbg["band"], sb_band[:])
                    sq.dma(dbg["qh1"], qh1[:])
                    sq.dma(dbg["qh2"], qh2[:])
                    sq.dma(dbg["mh1"], mh1[:])
                    sq.dma(dbg["mh2"], mh2[:])
                sq.step("vector", lambda e: e.reduce_max(mt_c[:], sc_top[:], axis=AX.X))
                sq.step("vector", lambda e: e.tensor_max(mb_c[:], mb_c[:], mt_c[:]))
                sq.step("vector", lambda e: e.tensor_single_scalar(
                    bias_c[:], mb_c[:], -ATT2_SCALE, ALU.mult))
                sq.step("scalar", lambda e: e.activation(
                    P_band[:], sb_band[:], AF.Exp, bias=bias_c[:],
                    scale=ATT2_SCALE, accum_out=rs1[:]))
                sq.step("scalar", lambda e: e.activation(
                    P_top[:], sc_top[:], AF.Exp, bias=bias_c[:],
                    scale=ATT2_SCALE, accum_out=rs2[:]))
                sq.step("vector", lambda e: e.tensor_add(rs1[:], rs1[:], rs2[:]))
                sq.step("vector", lambda e: e.reciprocal(rr_c[:], rs1[:]))
                sq.step("tensor", lambda e: e.transpose(
                    ps_tr[:], P_band[:, 0:128], identb[:]))
                sq.step("vector", lambda e: e.tensor_copy(PbT1[:], ps_tr[:]))
                sq.step("tensor", lambda e: e.transpose(
                    ps_tr[0:64, 0:128], P_band[:, 128:192], identb[:]))
                sq.step("vector", lambda e: e.tensor_copy(PbT2[:], ps_tr[0:64, 0:128]))
                sq.step("tensor", lambda e, h=h: e.matmul(
                    ps_av[:], PbT1[:], Vb1[:, h * 64:(h + 1) * 64],
                    start=True, stop=False))
                sq.step("tensor", lambda e, h=h: e.matmul(
                    ps_av[:], PbT2[:], Vb2[0:64, h * 64:(h + 1) * 64],
                    start=False, stop=True))
                vgv = gall[:].rearrange("p (r v) -> p v r", v=1056)[:, 512 + h * 64:512 + (h + 1) * 64, :]
                ptb = P_top[:].rearrange("p (o r) -> p o r", o=1).to_broadcast([128, 64, 32])
                sq.step("vector", lambda e, vgv=vgv, ptb=ptb: e.tensor_mul(
                    scrB[:, 0:2048].rearrange("p (v r) -> p v r", r=32), vgv, ptb))
                sq.step("vector", lambda e: e.reduce_sum(
                    ot_top[:], scrB[:, 0:2048].rearrange("p (v r) -> p v r", r=32),
                    axis=AX.X))
                sq.step("vector", lambda e: e.tensor_add(ot_top[:], ot_top[:], ps_av[:]))
                sq.step("vector", lambda e, h=h: e.tensor_scalar(
                    out_tm[:, h * 64:(h + 1) * 64], ot_top[:], rr_c[:], None, ALU.mult))
                if qt == 0:
                    sq.step("tensor", lambda e, hb=hb, hc=hc: e.matmul(
                        ps_96[:], qcf_sb[hb:hb + 64, hc * T_CORE:hc * T_CORE + 128],
                        kvf_sb[hb:hb + 64, hc * SPB + PADK:hc * SPB + PADK + 96],
                        start=True, stop=False))
                    sq.step("tensor", lambda e, h=h: e.matmul(
                        ps_96[:], qsa_fm[:, h * 128:(h + 1) * 128], ka96[:],
                        start=False, stop=False))
                    sq.step("tensor", lambda e, h=h: e.matmul(
                        ps_96[:], qsb_fm[:, h * 128:(h + 1) * 128], kb96[:],
                        start=False, stop=True))
                    sq.step("vector", lambda e: e.tensor_copy(
                        scrB[:, 0:96], ps_96[:]))
                    sq.step("vector", lambda e: e.reduce_max(
                        mb_c[:], scrB[:, 0:96], axis=AX.X))
                    sq.step("vector", lambda e: e.tensor_single_scalar(
                        bias_c[:], mb_c[:], -ATT2_SCALE, ALU.mult))
                    sq.step("scalar", lambda e: e.activation(
                        P_band[:, 0:96], scrB[:, 0:96], AF.Exp, bias=bias_c[:],
                        scale=ATT2_SCALE, accum_out=rs1[:]))
                    sq.step("vector", lambda e: e.reciprocal(rr_c[:], rs1[:]))
                    sq.step("tensor", lambda e: e.transpose(
                        ps_tr[0:96, 0:128], P_band[:, 0:96], identb[:]))
                    sq.step("vector", lambda e: e.tensor_copy(
                        PbT1[0:96, :], ps_tr[0:96, 0:128]))
                    sq.step("tensor", lambda e, h=h: e.matmul(
                        ps_av[:], PbT1[0:96, :], v96_sb[0:96, h * 64:(h + 1) * 64],
                        start=True, stop=True))
                    sq.step("vector", lambda e, h=h: e.tensor_scalar(
                        dense_tm[:, h * 64:(h + 1) * 64], ps_av[:], rr_c[:],
                        None, ALU.mult))
            if qt == 0:
                sq.step("vector", lambda e: e.tensor_copy(scrB[:, 0:512], out_tm[:]))
                sq.step("vector", lambda e: e.select(
                    out_tm[:], mask0_sb[:], dense_tm[:], scrB[:, 0:512]))
            if debug and qt == 8:
                sq.dma(dbg["outtm"], out_tm[:])
                sq.dma(dbg["rs"], rs1[:])
                sq.step("vector", lambda e: e.tensor_copy(scrC[0:16, 0:192], ka_band[:]))
                sq.dma(dbg["ka"], scrC[0:16, 0:192])
                sq.step("vector", lambda e: e.tensor_copy(scrB[0:16, 0:1024], qpa_fm[:]))
                sq.dma(dbg["qpa"], scrB[0:16, 0:1024])
                sq.dma(dbg["mug"], gall[:, 0:1024])
            sq.step("vector", lambda e: e.tensor_copy(otb[:], out_tm[:]))
            for c in range(4):
                sq.step("tensor", lambda e, c=c: e.transpose(
                    ps_tr[:], otb[:, c * 128:(c + 1) * 128], identb[:]))
                sq.step("vector", lambda e, c=c: e.tensor_copy(
                    ofm[:, c * 128:(c + 1) * 128], ps_tr[:]))
            sq.dma(Xt[:].rearrange("p (c t) -> p c t", c=NCH), xv_[:, :, t0:t0 + 128])
            for d in range(NCH):
                sq.step("tensor", lambda e, d=d: [
                    e.matmul(ps_out[:], wo_sb[:, c * D_MODEL + d * 128:
                                              c * D_MODEL + (d + 1) * 128],
                             ofm[:, c * 128:(c + 1) * 128],
                             start=(c == 0), stop=(c == 3))
                    for c in range(4)])
                sq.step("vector", lambda e, d=d: e.tensor_add(
                    oscr[:], ps_out[:], Xt[:, d * 128:(d + 1) * 128]))
                sq.dma(xo[:, d, t0:t0 + 128], oscr[:])

    if emit_only:
        return prog
    return seq_program(prog)


THETA16 = (10000.0 ** (-2.0 * np.arange(16) / 32.0)).astype(np.float64)


def host_topk_attn_inputs(x, proj_outs, W, core):
    b, h = core // 2, core % 2
    t0g = h * 2048
    full = lambda key: np.concatenate(
        [proj_outs[2 * b][key], proj_outs[2 * b + 1][key]], axis=0)
    kv_full = full("kv_tm")
    v_full = full("v_tm")
    mu_full = full("mu_tm")
    ki_full = full("k_i_tm")
    SPB = 2112
    zc = np.zeros((512, 1), kv_full.dtype)
    kv_fm_pad = np.concatenate(
        [np.zeros((512, PADK), kv_full.dtype), np.ascontiguousarray(kv_full.T),
         zc], axis=1)[:, t0g:t0g + SPB]
    v_pad = np.concatenate([np.zeros((PADK, 512), v_full.dtype), v_full,
                            np.zeros((1, 512), v_full.dtype)],
                           axis=0)[t0g:t0g + SPB]
    mu1_pad = np.concatenate(
        [np.zeros((16, PADK), kv_full.dtype),
         np.ascontiguousarray(mu_full[:, 0:16].T).astype(kv_full.dtype),
         zc[0:16]], axis=1)[:, t0g:t0g + SPB]
    mu2_pad = np.concatenate(
        [np.zeros((16, PADK), kv_full.dtype),
         np.ascontiguousarray(mu_full[:, 16:32].T).astype(kv_full.dtype),
         zc[0:16]], axis=1)[:, t0g:t0g + SPB]

    tg = np.arange(t0g, t0g + T_CORE, dtype=np.float64)
    t0tile = (np.arange(T_CORE) // 128 * 128) + t0g
    ang_q = tg[:, None] * THETA16[None, :]
    ang_q2 = (2 * tg - t0tile)[:, None] * THETA16[None, :]
    jj = np.arange(192, dtype=np.float64)
    ang_j = jj[None, :] * THETA16[:, None]
    ss = np.arange(96, dtype=np.float64)
    ang_s = ss[None, :] * THETA16[:, None]
    rr = np.arange(64, 96, dtype=np.float64)
    ang_r = rr[:, None] * THETA16[None, :]
    crk = np.tile(np.cos(ang_r).astype(np.float32).reshape(1, 512), (128, 1))
    srk = np.tile(np.sin(ang_r).astype(np.float32).reshape(1, 512), (128, 1))

    thr = np.zeros((128, 16), np.float32)
    for qt in range(16):
        thr[:, qt] = t0g + qt * 128 + np.arange(128) - 64.0
    ramp0 = np.full((128, 96), -3e38, np.float32)
    if h == 0:
        for p in range(64, 95):
            t = p
            for s in range(t + 1, 96):
                ramp0[p, s] = -1e30 - s * 1e26
    mask0 = np.zeros((128, 512), np.uint8)
    if h == 0:
        mask0[0:64, :] = 1

    return {
        "x_fm": fmT(core_slice_tokens(x, core)),
        "q_c_tm": proj_outs[core]["q_c_tm"],
        "q_c_fmi": np.ascontiguousarray(proj_outs[core]["q_c_tm"].T),
        "qr_tmi": proj_outs[core]["qr_tm"],
        "q_i_fmi": np.ascontiguousarray(proj_outs[core]["q_i_tm"].T),
        "k_i_fmi": np.ascontiguousarray(
            np.concatenate([ki_full.T, ki_full.T], axis=0)),
        "kv_fm_pad": np.ascontiguousarray(kv_fm_pad),
        "kv_tm_full": np.ascontiguousarray(kv_full),
        "v_tm_full": np.ascontiguousarray(v_full),
        "v_pad": np.ascontiguousarray(v_pad),
        "mu1_pad": np.ascontiguousarray(mu1_pad),
        "mu2_pad": np.ascontiguousarray(mu2_pad),
        "mu_tm_full": np.ascontiguousarray(mu_full),
        "w_out": np.ascontiguousarray(W["W_out"]),
        "idxw_row": np.asarray(W["idx_w"], np.float32)[None, :],
        "rdelta": np.asarray(W["raw_delta"], np.float32)[None, :],
        "thr_i": thr,
        "ramp0": ramp0,
        "mask0": mask0,
        "cth_q": np.cos(ang_q).astype(np.float32),
        "sth_q": np.sin(ang_q).astype(np.float32),
        "cth_q2": np.cos(ang_q2).astype(np.float32),
        "sth_q2": np.sin(ang_q2).astype(np.float32),
        "cjb_i": np.cos(ang_j).astype(np.float32),
        "sjb_i": np.sin(ang_j).astype(np.float32),
        "cabs96": np.cos(ang_s).astype(np.float32),
        "sabs96": np.sin(ang_s).astype(np.float32),
        "crk_i": crk,
        "srk_i": srk,
    }



def _run(nc, in_maps):
    res = run_bass_kernel_spmd(nc, in_maps, core_ids=list(range(N_CORES)))
    return res.results


_PROG_CACHE = {}


def _prog(name, builder):
    if name not in _PROG_CACHE:
        _PROG_CACHE[name] = builder()
    return _PROG_CACHE[name]



PAIRS = [[0, 1], [2, 3], [4, 5], [6, 7]]
ALL8 = [list(range(8))]


def build_fused(tap=None):
    nc = bass.Bass("TRN2", target_bir_lowering=False, debug=False)
    sq = Seq(nc)
    with ExitStack() as es:
        T = nc.dram_tensor
        x_ext_m = T("x_ext_m", [D_MODEL, T_CORE + 3], F32, kind="ExternalInput")
        x_outF = T("x_outF", [D_MODEL, T_CORE], F32, kind="ExternalOutput")
        r1 = T("r1", [D_MODEL, T_CORE], F32, kind="Internal")
        r2 = T("r2", [D_MODEL, T_CORE], F32, kind="Internal")
        r3 = T("r3", [D_MODEL, T_CORE], F32, kind="Internal")
        r4 = T("r4", [D_MODEL, T_CORE], F32, kind="Internal")
        xe5 = T("xe5", [D_MODEL, T_CORE + 1], F32, kind="Internal")
        r6 = T("r6", [D_MODEL, T_CORE], F32, kind="Internal")
        xe7 = T("xe7", [D_MODEL, T_CORE + 1], F32, kind="Internal")
        mex = T("mex", [4128, T_CORE], BF16, kind="Internal")
        mexg = T("mexg", [8256, T_CORE], BF16, kind="Internal")
        zs = T("zs", [D_INNER, T_CORE], F32, kind="Internal")
        ymy = T("ymy", [2048, T_CORE], BF16, kind="Internal")
        yag = T("yag", [4096, T_CORE], BF16, kind="Internal")
        q_ot = T("q_ot", [D_MODEL, T_CORE], BF16, kind="Internal")
        k_ot = T("k_ot", [D_MODEL, T_CORE], BF16, kind="Internal")
        kag = T("kag", [2 * D_MODEL, T_CORE], BF16, kind="Internal")
        vaug_my = T("vaug_my", [T_CORE, 1040], BF16, kind="Internal")
        vag = T("vag", [2 * T_CORE, 1040], BF16, kind="Internal")
        cmb1 = T("cmb1", [8, T_CORE], F32, kind="Internal")
        cmb2 = T("cmb2", [8, T_CORE], F32, kind="Internal")
        q_c_tmt = T("q_c_tmt", [T_CORE, 512], BF16, kind="Internal")
        qr_tmt = T("qr_tmt", [T_CORE, 256], F32, kind="Internal")
        q_c_fmt = T("q_c_fmt", [512, T_CORE], BF16, kind="Internal")
        q_i_fmt = T("q_i_fmt", [256, T_CORE], BF16, kind="Internal")
        tkt = T("tkt", [T_CORE, 1056], BF16, kind="Internal")
        tktg = T("tktg", [2 * T_CORE, 1056], BF16, kind="Internal")
        tkf = T("tkf", [608, T_CORE], BF16, kind="Internal")
        tkfg = T("tkfg", [1216, T_CORE], BF16, kind="Internal")
        kv_fm_pad = T("kv_fm_padI", [512, 2112], BF16, kind="Internal")
        mu1_pad = T("mu1_padI", [16, 2112], BF16, kind="Internal")
        mu2_pad = T("mu2_padI", [16, 2112], BF16, kind="Internal")
        v_padI = T("v_padI", [2112, 512], BF16, kind="Internal")
        va_dI = T("va_dI", [2112, 512], BF16, kind="Internal")
        vb_dI = T("vb_dI", [2112, 512], BF16, kind="Internal")
        kifmi = T("kifmiI", [128, 4096], BF16, kind="Internal")
        tgmA = T("tg_mA", [128, 1], F32, kind="ExternalInput")
        tgmB = T("tg_mB", [128, 1], F32, kind="ExternalInput")
        hcol1 = T("hcol1", [D_MODEL, 1], F32, kind="Internal")
        hag1 = T("hag1", [2 * D_MODEL, 1], F32, kind="Internal")
        hcol2 = T("hcol2", [D_MODEL, 1], F32, kind="Internal")
        hag2 = T("hag2", [2 * D_MODEL, 1], F32, kind="Internal")
        hm = T("hm", [128, 1], F32, kind="ExternalInput")
        xn_my = T("xn_myI", [D_MODEL, T_CORE], BF16, kind="Internal")
        xnag = T("xnagI", [8 * D_MODEL, T_CORE], BF16, kind="Internal")
        ek_ot = T("ek_oI", [D_MODEL, T_CORE], F32, kind="Internal")
        ekv_ot = T("ekv_oI", [D_MODEL, T_CORE], F32, kind="Internal")
        ekekv_my = T("ekekv_myI", [2 * D_MODEL, T_CORE], BF16, kind="Internal")
        ekekvg = T("ekekvgI", [16 * D_MODEL, T_CORE], BF16, kind="Internal")
        r_ot = T("r_oI", [D_MODEL, T_CORE], F32, kind="Internal")

        class NcProxy:
            def __init__(self, base, prefix):
                self._b, self._p = base, prefix

            def sbuf_tensor(self, name, *a, **k):
                return self._b.sbuf_tensor(self._p + name, *a, **k)

            def psum_tensor(self, name, *a, **k):
                return self._b.psum_tensor(self._p + name, *a, **k)

            def __getattr__(self, a):
                return getattr(self._b, a)

        def run(build, io, **kw):
            prog = build(io=io, emit_only=True, **kw)
            with ExitStack() as ses:
                prog(NcProxy(nc, io.nprefix), sq, ses)

        def ag(cls, src, dst, groups=PAIRS):
            sq.step("gpsimd", lambda e: e.collective_compute(
                "AllGather", ALU.bypass, replica_groups=groups,
                ins=[src.ap().opt()], outs=[dst.ap().opt()]),
                inc=1, sclass=cls)

        run(build_mamba_pre, FusedIO("mp_", {
            "x_ext": x_ext_m.ap(), "mex": mex.ap(), "zs_o": zs.ap()}))
        ag("ccA", mex, mexg)
        run(build_mamba_scan, FusedIO("ms_", {
            "mexg": mexg.ap(), "ymy": ymy.ap()}))
        ag("ccB", ymy, yag)
        run(build_mamba_post, FusedIO("mo_", {
            "x_fm": x_ext_m.ap()[:, 3:T_CORE + 3], "yag": yag.ap(),
            "zs_fm": zs.ap(), "x_out": r1.ap()}))

        io_ff = FusedIO("ff_", {"x_fm": r1.ap(), "x_out": r2.ap(),
                                "cmb_dram": cmb1.ap()})
        run(build_l0_ffn, io_ff)

        run(build_mla_qkv, FusedIO("mq_", {
            "x_fm": r2.ap(), "q_o": q_ot.ap(), "k_o": k_ot.ap(),
            "v_o": vaug_my.ap()}))
        ag("ccC", k_ot, kag)
        ag("ccD", vaug_my, vag)
        run(build_mla_attn2, FusedIO("ma_", {
            "q_i": q_ot.ap(), "k_i": kag.ap(), "vaug_i": vag.ap(),
            "x_fm": r2.ap(), "x_out": r3.ap()}))

        io_ff2 = FusedIO("ff_", dict(io_ff), nprefix="ff2_")
        io_ff2.update({"x_fm": r3.ap(), "x_out": r4.ap(), "cmb_dram": cmb2.ap()})
        run(build_l0_ffn, io_ff2)

        run(build_topk_proj, FusedIO("tp_", {
            "x_fm": r4.ap(), "q_c_tm": q_c_tmt.ap(), "qr_tm": qr_tmt.ap(),
            "kv_tm": tkt.ap()[:, 0:512], "v_tm": tkt.ap()[:, 512:1024],
            "mu_tm": tkt.ap()[:, 1024:1056],
            "q_c_fm": q_c_fmt.ap(), "q_i_fm": q_i_fmt.ap(),
            "kv_fm": tkf.ap()[0:512, :], "mu_fm": tkf.ap()[512:544, :],
            "ki_fm": tkf.ap()[544:608, :]}))
        ag("ccE", tkt, tktg)
        ag("ccF", tkf, tkfg)
        emit_topk_glue(nc, sq, tktg.ap(), tkfg.ap(), kv_fm_pad.ap(),
                       mu1_pad.ap(), mu2_pad.ap(), v_padI.ap(),
                       va_dI.ap(), vb_dI.ap(), kifmi.ap(),
                       tgmA.ap(), tgmB.ap())
        run(build_topk_attn, FusedIO("ta_", {
            "x_fm": r4.ap(), "tkt_full": tktg.ap(),
            "q_c_tm": q_c_tmt.ap(), "q_c_fmi": q_c_fmt.ap(),
            "qr_tmi": qr_tmt.ap(), "q_i_fmi": q_i_fmt.ap(),
            "k_i_fmi": kifmi.ap(), "kv_fm_pad": kv_fm_pad.ap(),
            "kv_tm_full": tktg.ap()[:, 0:512], "v_tm_full": tktg.ap()[:, 512:1024],
            "mu_tm_full": tktg.ap()[:, 1024:1056], "v_pad": v_padI.ap(),
            "mu1_pad": mu1_pad.ap(), "mu2_pad": mu2_pad.ap(),
            "x_out": xe5.ap()[:, 1:T_CORE + 1]}))

        emit_halo(nc, sq, "ccG", xe5.ap(), hcol1, hag1, hm.ap())
        io_lf = FusedIO("lf_", {"x_ext": xe5.ap(), "x_out": r6.ap()})
        run(build_l1_ffn, io_lf)

        run(build_wkv_kv, FusedIO("wk_", {
            "x_fm": r6.ap(), "xn_my": xn_my.ap(), "xnag": xnag.ap(),
            "xn_my_t": xn_my, "xnag_t": xnag,
            "ek_o": ek_ot.ap(), "ekv_o": ekv_ot.ap(),
            "ekekv_my": ekekv_my.ap(), "ekekv_my_t": ekekv_my,
            "ekekvg_t": ekekvg, "r_o": r_ot.ap()}))
        run(build_wkv_out, FusedIO("wo_", {
            "x_fm": r6.ap(), "ekekvg": ekekvg.ap(), "r_fm": r_ot.ap(),
            "x_out": xe7.ap()[:, 1:T_CORE + 1]}))

        emit_halo(nc, sq, "ccH", xe7.ap(), hcol2, hag2, hm.ap())
        io_lf2 = FusedIO("lf_", dict(io_lf), nprefix="lf2_")
        io_lf2.update({"x_ext": xe7.ap(), "x_out": x_outF.ap()})
        run(build_l1_ffn, io_lf2)

        if tap is not None:
            taps = {"r1": r1, "r2": r2, "r3": r3, "r4": r4, "r6": r6}
            if tap in taps:
                sq.dma(x_outF.ap(), taps[tap].ap())
            elif tap == "xe5":
                sq.dma(x_outF.ap(), xe5.ap()[:, 1:T_CORE + 1])
            elif tap == "xe7":
                sq.dma(x_outF.ap(), xe7.ap()[:, 1:T_CORE + 1])
            elif tap == "ek2":
                sq.dma(x_outF.ap(), ekekvg.ap()[4096:5120, :], engine="gpsimd")
            elif tap == "ek1":
                sq.dma(x_outF.ap(), ekekvg.ap()[2048:3072, :], engine="gpsimd")
            elif tap == "ekv1":
                sq.dma(x_outF.ap(), ekekvg.ap()[3072:4096, :], engine="gpsimd")
            elif tap == "ro":
                sq.dma(x_outF.ap(), r_ot.ap())
            elif tap == "xn1":
                sq.dma(x_outF.ap(), xnag.ap()[1024:2048, :], engine="gpsimd")

        sems = {}
        for st in sq.counts:
            sems[st] = es.enter_context(nc.semaphore(f"s_{st[0]}_{st[1]}"))
        block = es.enter_context(nc.Block())
        sq.build_into(block, sems)
        sq.finish(block, sems)
    return nc


def emit_topk_glue(nc, sq, tktg, tkfg, kv_fm_pad, mu1_pad, mu2_pad, v_pad_d,
                   va_d, vb_d, kifmi, mA_ap, mB_ap):
    with ExitStack() as es:
        sb = lambda n, s, d: es.enter_context(nc.sbuf_tensor(n, s, d))
        za = sb("tg_za", [128, 2112], BF16)
        zb = sb("tg_zb", [128, 2112], BF16)
        zo = sb("tg_zo", [128, 2112], BF16)
        zf = sb("tg_zf", [128, 2112], F32)
        mA = sb("tg_mAs", [128, 1], F32)
        mB = sb("tg_mBs", [128, 1], F32)
        sq.dma(mA[:], mA_ap)
        sq.dma(mB[:], mB_ap)

        def blend_pad(rows, base_a, base_b, dst):
            sq.step("gpsimd", lambda e: e.memset(za[0:rows, :], 0.0))
            sq.step("gpsimd", lambda e: e.memset(zb[0:rows, :], 0.0))
            sq.dma(za[0:rows, 63:2111], tkfg[base_a:base_a + rows, 0:2048])
            sq.dma(za[0:rows, 2111:2112], tkfg[base_b:base_b + rows, 0:1],
                   allow_slow_non_contiguous=True)
            sq.dma(zb[0:rows, 0:63], tkfg[base_a:base_a + rows, 1985:2048])
            sq.dma(zb[0:rows, 63:2111], tkfg[base_b:base_b + rows, 0:2048])
            sq.step("vector", lambda e: e.tensor_scalar(
                zf[0:rows, :], za[0:rows, :], mA[0:rows, :], None, ALU.mult))
            sq.step("vector", lambda e: e.scalar_tensor_tensor(
                zo[0:rows, :], zb[0:rows, :], mB[0:rows, :], zf[0:rows, :],
                ALU.mult, ALU.add))
            sq.dma(dst, zo[0:rows, :])

        for c in range(4):
            blend_pad(128, c * 128, 608 + c * 128,
                      kv_fm_pad[c * 128:(c + 1) * 128, :])
        blend_pad(16, 512, 1120, mu1_pad[0:16, :])
        blend_pad(16, 528, 1136, mu2_pad[0:16, :])
        zz = sb("tg_zz", [63, 512], BF16)
        sq.step("gpsimd", lambda e: e.memset(zz[:], 0.0))
        sq.dma(va_d[0:63, :], zz[0:63, :])
        sq.dma(va_d[63:2111, :], tktg[0:2048, 512:1024])
        sq.dma(va_d[2111:2112, :], tktg[2048:2049, 512:1024])
        sq.dma(vb_d[0:63, :], tktg[1985:2048, 512:1024])
        sq.dma(vb_d[63:2111, :], tktg[2048:4096, 512:1024])
        sq.dma(vb_d[2111:2112, :], zz[0:1, :])
        for i in range(17):
            r0 = i * 128
            nr = min(128, 2112 - r0)
            sq.dma(za[0:nr, 0:512], va_d[r0:r0 + nr, :])
            sq.dma(zb[0:nr, 0:512], vb_d[r0:r0 + nr, :])
            sq.step("vector", lambda e, nr=nr: e.tensor_scalar(
                zf[0:nr, 0:512], za[0:nr, 0:512], mA[0:nr, :], None, ALU.mult))
            sq.step("vector", lambda e, nr=nr: e.scalar_tensor_tensor(
                zo[0:nr, 0:512], zb[0:nr, 0:512], mB[0:nr, :], zf[0:nr, 0:512],
                ALU.mult, ALU.add))
            sq.dma(v_pad_d[r0:r0 + nr, :], zo[0:nr, 0:512])
        sq.dma(kifmi[0:64, 0:2048], tkfg[544:608, :])
        sq.dma(kifmi[0:64, 2048:4096], tkfg[1152:1216, :])
        sq.dma(kifmi[64:128, 0:2048], tkfg[544:608, :])
        sq.dma(kifmi[64:128, 2048:4096], tkfg[1152:1216, :])


def emit_halo(nc, sq, cls, xe, hcol_t, hag_t, mH_ap):
    with ExitStack() as es:
        sb = lambda n, s, d: es.enter_context(nc.sbuf_tensor(n, s, d))
        hb = sb(f"hb_{cls}", [128, NCH], F32)
        mH = sb(f"mH_{cls}", [128, 1], F32)
        sq.dma(hcol_t.ap(), xe[:, T_CORE:T_CORE + 1],
               allow_slow_non_contiguous=True)
        sq.step("gpsimd", lambda e: e.collective_compute(
            "AllGather", ALU.bypass, replica_groups=PAIRS,
            ins=[hcol_t.ap().opt()], outs=[hag_t.ap().opt()]),
            inc=1, sclass=cls)
        sq.dma(hb[:], hag_t.ap()[0:D_MODEL, :].rearrange(
            "(c p) one -> p (c one)", p=128), allow_slow_non_contiguous=True)
        sq.dma(mH[:], mH_ap)
        sq.step("vector", lambda e: e.tensor_scalar(
            hb[:], hb[:], mH[:], None, ALU.mult))
        sq.dma(xe[:, 0:1].rearrange("(c p) one -> p (c one)", p=128), hb[:],
               allow_slow_non_contiguous=True)


def host_topk_aux(W, core):
    b, h = core // 2, core % 2
    t0g = h * 2048
    tg = np.arange(t0g, t0g + T_CORE, dtype=np.float64)
    t0tile = (np.arange(T_CORE) // 128 * 128) + t0g
    ang_q = tg[:, None] * THETA16[None, :]
    ang_q2 = (2 * tg - t0tile)[:, None] * THETA16[None, :]
    jj = np.arange(192, dtype=np.float64)
    ang_j = jj[None, :] * THETA16[:, None]
    ss = np.arange(96, dtype=np.float64)
    ang_s = ss[None, :] * THETA16[:, None]
    rr = np.arange(64, 96, dtype=np.float64)
    ang_r = rr[:, None] * THETA16[None, :]
    crk = np.tile(np.cos(ang_r).astype(np.float32).reshape(1, 512), (128, 1))
    srk = np.tile(np.sin(ang_r).astype(np.float32).reshape(1, 512), (128, 1))
    thr = np.zeros((128, 16), np.float32)
    for qt in range(16):
        thr[:, qt] = t0g + qt * 128 + np.arange(128) - 64.0
    ramp0 = np.full((128, 96), -3e38, np.float32)
    if h == 0:
        for p in range(64, 95):
            for s_ in range(p + 1, 96):
                ramp0[p, s_] = -1e30 - s_ * 1e26
    mask0 = np.zeros((128, 512), np.uint8)
    if h == 0:
        mask0[0:64, :] = 1
    return {
        "w_out": np.ascontiguousarray(W["W_out"]),
        "idxw_row": np.asarray(W["idx_w"], np.float32)[None, :],
        "rdelta": np.asarray(W["raw_delta"], np.float32)[None, :],
        "thr_i": thr, "ramp0": ramp0, "mask0": mask0,
        "cth_q": np.cos(ang_q).astype(np.float32),
        "sth_q": np.sin(ang_q).astype(np.float32),
        "cth_q2": np.cos(ang_q2).astype(np.float32),
        "sth_q2": np.sin(ang_q2).astype(np.float32),
        "cjb_i": np.cos(ang_j).astype(np.float32),
        "sjb_i": np.sin(ang_j).astype(np.float32),
        "cabs96": np.cos(ang_s).astype(np.float32),
        "sabs96": np.sin(ang_s).astype(np.float32),
        "crk_i": crk, "srk_i": srk,
    }


def host_fused_inputs(inp, core):
    x = np.asarray(inp["L0_attn_a_x"], np.float32)
    b, h = core // 2, core % 2
    P = np.arange(128, dtype=np.uint32)
    mcol = lambda v: np.full((128, 1), v, np.float32)
    m = {}

    Wm = {k: inp[f"L0_attn_a_{k}"] for k in
          ["in_proj_w", "conv_w", "conv_b", "x_proj_w", "dt_proj_w",
           "dt_proj_b", "A_log", "D", "out_proj_w"]}
    pre = host_mamba_pre_inputs(x, Wm, core)
    m["x_ext_m"] = pre.pop("x_ext")
    for k, v in pre.items():
        m["mp_" + k] = v
    Alog = np.asarray(Wm["A_log"], np.float32)[h * 1024:(h + 1) * 1024]
    m["ms_Alog_i"] = np.ascontiguousarray(
        Alog.reshape(8, 128, 16).transpose(1, 0, 2).reshape(128, 128))
    m["ms_D_i"] = pack_vec(np.asarray(Wm["D"], np.float32)[h * 1024:(h + 1) * 1024])
    m["ms_mA_i"] = mcol(1.0 if h == 0 else 0.0)
    m["ms_mB_i"] = mcol(0.0 if h == 0 else 1.0)
    m["mo_opT"] = fmT(Wm["out_proj_w"])
    m["mo_mA_i"] = mcol(1.0 if h == 0 else 0.0)
    m["mo_mB_i"] = mcol(0.0 if h == 0 else 1.0)

    Wf = {k: inp[f"L0_ffn_{k}"] for k in
          ["router_w", "w_gate", "w_up", "w_down", "sw_gate", "sw_up", "sw_down"]}
    ff = host_l0_ffn_inputs(x, Wf, core)
    ff.pop("x_fm")
    for k, v in ff.items():
        m["ff_" + k] = v

    Wb = {k: inp[f"L0_attn_b_{k}"] for k in
          ["Wd_q_w", "Wd_q_b", "Wu_q_w", "Wu_q_b", "Wq_r_w", "Wq_r_b",
           "Wk_r_w", "Wk_r_b", "Wd_kv_w", "Wd_kv_b", "Wu_k_w", "Wu_k_b",
           "Wu_v_w", "Wu_v_b", "Wo_w", "Wo_b"]}
    qkv = host_mla_qkv_inputs(x, Wb, core)
    qkv.pop("x_fm")
    for k, v in qkv.items():
        m["mq_" + k] = v
    m["ma_wo"] = np.ascontiguousarray(np.asarray(Wb["Wo_w"], np.float32))
    m["ma_b_o"] = pack_vec(Wb["Wo_b"])

    Wa = {k: np.asarray(inp[f"L1_attn_a_{k}"], np.float32) for k in
          ["W_dkv", "W_uk", "W_uv", "W_dq", "W_uq", "W_qr", "W_kr", "W_out",
           "idx_wq", "idx_wk", "idx_w", "raw_delta"]}
    tp = host_topk_proj_inputs(x, Wa, core)
    tp.pop("x_fm")
    for k, v in tp.items():
        m["tp_" + k] = v
    for k, v in host_topk_aux(Wa, core).items():
        m["ta_" + k] = v
    m["tg_mA"] = mcol(1.0 if h == 0 else 0.0)
    m["tg_mB"] = mcol(0.0 if h == 0 else 1.0)
    m["hm"] = mcol(1.0 if h == 1 else 0.0)

    Wl = {k: inp[f"L1_ffn_{k}"] for k in ["W_r", "W_k", "W_v", "mu_r", "mu_k"]}
    m["lf_wrT"] = fmT(Wl["W_r"])
    m["lf_wkT"] = fmT(Wl["W_k"])
    m["lf_wvT"] = fmT(Wl["W_v"])
    m["lf_mu_r"] = pack_vec(Wl["mu_r"])
    m["lf_mu_k"] = pack_vec(Wl["mu_k"])

    Ww = {k: inp[f"L1_attn_b_{k}"] for k in
          ["mu_k", "mu_v", "mu_r", "Wk", "Wv", "Wr", "W_out", "w", "u"]}
    m["wk_wkT"] = fmT(Ww["Wk"])
    m["wk_wvT"] = fmT(Ww["Wv"])
    m["wk_wrT"] = fmT(Ww["Wr"])
    m["wk_mu_k"] = pack_vec(Ww["mu_k"])
    m["wk_mu_v"] = pack_vec(Ww["mu_v"])
    m["wk_mu_r"] = pack_vec(Ww["mu_r"])
    base_p = (2 * (b - 1) + h) * 1024 if b > 0 else 0
    m["wk_idxp_i"] = (base_p + np.arange(8, dtype=np.uint32)[None, :] * 128
                      + P[:, None]).astype(np.uint32)
    m["wk_mz_i"] = mcol(0.0 if b == 0 else 1.0)
    m["wo_woT"] = fmT(Ww["W_out"])
    m["wo_wv_"] = pack_vec(Ww["w"])
    m["wo_uv_"] = pack_vec(Ww["u"])
    steps8 = np.zeros((128, 8), np.float32)
    mask8 = np.zeros((128, 8), np.float32)
    for r in range(8):
        bp, hp = r // 2, r % 2
        if hp == h and bp <= b:
            steps8[:, r] = float(b - bp)
            mask8[:, r] = 1.0
    m["wo_steps"] = steps8
    m["wo_mask"] = mask8
    own = (2 * b + h) * 2048
    m["wo_idxo1_i"] = (own + np.arange(8, dtype=np.uint32)[None, :] * 128
                       + P[:, None]).astype(np.uint32)
    m["wo_idxo2_i"] = (own + 1024 + np.arange(8, dtype=np.uint32)[None, :] * 128
                       + P[:, None]).astype(np.uint32)
    return m


def kernel(**inp):
    nc = _prog("fused", build_fused)
    in_maps = [host_fused_inputs(inp, c) for c in range(N_CORES)]
    res = run_bass_kernel_spmd(nc, in_maps, core_ids=list(range(N_CORES)))
    out = np.empty((4, 4096, 1024), np.float32)
    for core in range(N_CORES):
        b, h = core // 2, core % 2
        out[b, h * 2048:(h + 1) * 2048] = res.results[core]["x_outF"].T
    return out
